# version 02
# baseline (speedup 1.0000x reference)
import numpy as np
import ml_dtypes
from contextlib import ExitStack
import concourse.bass as bass
import concourse.mybir as mybir

F32 = mybir.dt.float32
BF16 = mybir.dt.bfloat16
I8 = mybir.dt.int8
F8 = mybir.dt.float8e4
F16 = mybir.dt.float16
AF = mybir.ActivationFunctionType
ALU = mybir.AluOpType
AX = mybir.AxisListType
BF = ml_dtypes.bfloat16
F16N = np.float16

D = 1024
S = 4096
T = 2048
NB = 8
BS = 256
H = 4096
W = 2050
WZ = 4098
EPS = 1e-5


def qkv_perm():
    perm = np.empty(3072, np.int64)
    for sec in range(3):
        for g in range(16):
            perm[1024 * sec + 64 * g: 1024 * sec + 64 * g + 64] = \
                np.arange(192 * g + 64 * sec, 192 * g + 64 * sec + 64)
    return perm

def zrow_to_cglob():
    out = np.empty(1024, np.int64)
    for r in range(1024):
        rank, rr = divmod(r, 512)
        chunk, part = divmod(rr, 128)
        m = (4 if part >= 64 else 0) + chunk
        out[r] = 64 * (8 * rank + m) + part % 64
    return out

def stage_inputs(inputs, n_cores=8):
    x = np.asarray(inputs["L0_attn_x"], np.float32)
    Wqkv = np.asarray(inputs["L0_attn_W_qkv"], np.float32)
    bqkv = np.asarray(inputs["L0_attn_b_qkv"], np.float32)
    Wo = np.asarray(inputs["L0_attn_W_o"], np.float32)
    bo = np.asarray(inputs["L0_attn_b_o"], np.float32)
    w1 = np.asarray(inputs["L0_ffn_w1"], np.float32)
    cw = np.asarray(inputs["L0_ffn_conv_w"], np.float32)
    cb = np.asarray(inputs["L0_ffn_conv_b"], np.float32)
    w2 = np.asarray(inputs["L0_ffn_w2"], np.float32)

    perm = qkv_perm()
    com = dict(
        WqT=np.ascontiguousarray(Wqkv.T[:, perm]),
        bq_pp=np.ascontiguousarray(bqkv[perm].reshape(24, 128).T),
        WoT=np.ascontiguousarray(Wo.T[zrow_to_cglob(), :]),
        bo_pp=np.ascontiguousarray(bo.reshape(8, 128).T),
        w1T=np.ascontiguousarray(w1.T),
        w2T=np.ascontiguousarray(w2.T),
        cw_pp=np.ascontiguousarray(cw[:, 0, :].reshape(32, 128, 3).transpose(1, 2, 0).reshape(128, 96)),
        cb_pp=np.ascontiguousarray(cb.reshape(32, 128).T),
        ident=np.eye(128, dtype=F16N),
        ident32=np.eye(128, dtype=np.float32),
        ones1=np.ones((128, 1), dtype=F16N),
        ones32=np.ones((128, 128), dtype=np.float32),
    )
    maps = []
    for c in range(n_cores):
        b, off = divmod(c, 2)
        xr = np.zeros((D, W), np.float32)
        if off == 0:
            xr[:, 1:2050] = x[b, 0:2049, :].T
        else:
            xr[:, 0:2049] = x[b, 2047:4096, :].T
        mask_l = np.ones((1, 258), np.float32)
        mask_r = np.ones((1, 258), np.float32)
        if off == 0:
            mask_l[0, 0] = 0.0
        else:
            mask_r[0, 257] = 0.0
        maps.append(dict(com, x_res=xr, mask_l=mask_l, mask_r=mask_r))
    return maps

def gather_out(results):
    out = np.empty((4, S, D), np.float32)
    for c, r in enumerate(results):
        b, off = divmod(c, 2)
        out[b, T * off:T * off + T, :] = np.asarray(r["out"]).T
    return out


class P:
    ENGS = ("sync", "tensor", "vector", "scalar", "gpsimd")

    def __init__(self):
        self.q = {e: [] for e in self.ENGS}
        self.cnt = {}
        self.water = {e: {} for e in self.ENGS}
        self.bkg = [None] * 8

    def tok(self, sem):
        return (sem, self.cnt.get(sem, 0))

    def emit(self, eng, fn, waits=(), inc=None, amt=1):
        best = {}
        for smv in waits:
            if smv is None:
                continue
            s, v = smv
            if v > best.get(s, 0):
                best[s] = v
        wl = []
        for s, v in best.items():
            if v > self.water[eng].get(s, 0):
                self.water[eng][s] = v
                wl.append((s, v))
        tk = None
        if inc is not None:
            self.cnt[inc] = self.cnt.get(inc, 0) + amt
            tk = (inc, self.cnt[inc])
        self.q[eng].append((fn, wl, inc, amt))
        return tk

    def run(self, eng_name, handle, sems, state):
        for fn, wl, inc, amt in self.q[eng_name]:
            for s, v in wl:
                handle.wait_ge(sems[s], v)
            inst = fn(handle, state)
            if inc is not None:
                inst.then_inc(sems[inc], amt)


def build_nc(n_cores=8, n_heads=NB, stop_after="full", taps=(), groups=None):
    nc = bass.Bass("TRN2", target_bir_lowering=False)
    p = P()
    if groups is None:
        groups = [[i, i + 1] for i in range(0, n_cores, 2)]

    ap = {}
    for nm, sh, dt in [("x_res", [D, W], F32), ("WqT", [D, 3072], F32),
                       ("bq_pp", [128, 24], F32), ("WoT", [D, D], F32),
                       ("bo_pp", [128, 8], F32), ("w1T", [D, H], F32),
                       ("w2T", [H, D], F32), ("cw_pp", [128, 96], F32),
                       ("cb_pp", [128, 32], F32), ("ident", [128, 128], F16),
                       ("ident32", [128, 128], F32), ("ones1", [128, 1], F16),
                       ("ones32", [128, 128], F32),
                       ("mask_l", [1, 258], F32), ("mask_r", [1, 258], F32)]:
        ap[nm] = nc.dram_tensor(nm, sh, dt, kind="ExternalInput").ap()
    ap["out"] = nc.dram_tensor("out", [D, T], F32, kind="ExternalOutput").ap()
    zin = nc.dram_tensor("zin", [512, WZ], F16)
    zgath = nc.dram_tensor("zgath", [1024, WZ], F16)
    wodram = nc.dram_tensor("wodram", [1024, 1024], F16)

    tap_shapes = {"stats": ([1, 2 * W], F32), "Y": ([128, 24 * BS], F16),
                  "ln1": ([128, 8 * BS], F16), "zin": ([512, WZ], F16),
                  "x1w": ([128, 8 * 258], F32), "xq": ([128, 8 * 258], BF16),
                  "h": ([128, 32 * 258], F32), "xq2": ([128, 2 * 260], BF16),
                  "hpre": ([128, 32 * 258], F32), "w1q": ([128, 32768], F8)}
    tapap = {}
    for tname in taps:
        sh, dt = tap_shapes[tname]
        tapap[tname] = nc.dram_tensor("tap_" + tname, sh, dt, kind="ExternalOutput").ap()

    ctx = ExitStack()
    sbt = lambda n, s, d: ctx.enter_context(nc.sbuf_tensor(n, s, d))

    s_ident = sbt("s_ident", [128, 128], F16)
    s_id32 = sbt("s_id32", [128, 128], F32)
    s_ones = sbt("s_ones", [128, 1], F16)
    s_on32 = sbt("s_on32", [128, 128], F32)
    s_bq = sbt("s_bq", [128, 24], F32)
    s_bo = sbt("s_bo", [128, 8], F32)
    s_cw = sbt("s_cw", [128, 96], F32)
    s_cb = sbt("s_cb", [128, 32], F32)
    s_maskl = sbt("s_maskl", [1, 258], F32)
    s_maskr = sbt("s_maskr", [1, 258], F32)
    s_mr = sbt("s_mr", [1, 2 * W], F32)
    s_scl = sbt("s_scl", [1, 1040], F32)
    s_row = s_scl[0:1, 0:512]
    s_ws = sbt("s_ws", [128, 8], F32)

    AR1 = sbt("AR1", [128, 12288], F32)
    s_wq = AR1[:, 0:12288].bitcast(F16)
    s_w1q = AR1[:, 0:8192].bitcast(F8)
    s_xqb = AR1[:, 8192:9224].bitcast(BF16)
    s_y8r = AR1[:, 9224:9354].bitcast(I8)
    s_ybr = AR1[:, 9354:9614].bitcast(BF16)
    s_bc = AR1[:, 9614:10388]
    s_cast1 = AR1[:, 10388:10644].bitcast(BF16)
    s_cast2 = AR1[:, 10644:10900].bitcast(BF16)

    AR2 = sbt("AR2", [128, 16704], F32)
    s_xblk = AR2[:, 0:4096]
    s_ln1 = AR2[:, 4096:6144].bitcast(F16)
    s_xsq = AR2[:, 6144:7168].bitcast(F16)
    s_Y = AR2[:, 7168:10240].bitcast(F16)
    s_kdup = AR2[:, 10240:11264].bitcast(F16)
    s_vt = AR2[:, 11264:12304].bitcast(BF16)
    s_E = AR2[:, 12304:13328].bitcast(BF16)
    s_rb = AR2[:, 13328:14352]
    s_zn = AR2[:, 14352:14864].bitcast(F16)
    s_zr = AR2[:, 14864:15888]
    s_atmp = AR2[:, 15888:16144]
    s_abc = AR2[:, 16144:16656]
    s_x1w = AR2[:, 0:2064]
    s_lt = AR2[:, 2064:4128]
    s_xq8 = AR2[:, 4128:4644].bitcast(I8)
    s_w8t = AR2[:, 4128:4640].bitcast(I8)
    s_tc0 = AR2[:, 4644:5160]
    s_red = AR2[:, 5160:5684]
    s_tcb = AR2[:, 5684:5942]
    s_out = AR2[:, 5942:6454]
    s_xr = AR2[:, 6454:7478]
    s_wos = AR2[:, 7478:8502].bitcast(F16)
    s_w2q = AR2[:, 8502:16694].bitcast(F8)

    AR3 = sbt("AR3", [128, 8256], F32)
    s_wst = AR3[:, 0:4096]
    s_hy = AR3[:, 0:8256]

    ARZ = sbt("ARZ", [128, 8200], F32)
    s_z = ARZ[:, :].bitcast(F16)

    ps = ctx.enter_context(nc.psum_tensor("ps", [128, 4096], F32))

    def bank(i, cols=512, n0=0):
        return ps[:, 512 * i + n0: 512 * i + n0 + cols]

    def dma(eng, out, in_, waits=(), sem=None):
        assert sem is not None
        return p.emit(eng, lambda e, st, o=out, i=in_: e.dma_start(out=o, in_=i),
                      waits=waits, inc=sem, amt=16)

    def mm(out, lhsT, rhs, start, stop, waits=(), banks=()):
        w = list(waits)
        if start:
            w += [p.bkg[b] for b in banks]
        return p.emit("tensor",
                      lambda e, st, o=out, l=lhsT, r=rhs, a=start, b=stop:
                      e.matmul(o, l, r, start=a, stop=b), waits=w, inc="tT")

    def tr(out, in_, idt, waits=(), banks=()):
        w = list(waits) + [p.bkg[b] for b in banks]
        return p.emit("tensor", lambda e, st, o=out, i=in_, d=idt: e.transpose(o, i, d),
                      waits=w, inc="tT")

    def rel(banks, tok):
        for b in banks:
            p.bkg[b] = tok

    def vop(fn, waits=()):
        return p.emit("vector", fn, waits=waits, inc="tV")

    def sop(fn, waits=()):
        return p.emit("scalar", fn, waits=waits, inc="tS")

    def fence_v(tok):
        return vop(lambda e, st: e.memset(s_ws[:, 6:7], 0.0), waits=[tok])

    def fm(dram_ap):
        return dram_ap.rearrange("(c p) w -> p c w", p=128)

    for o_, i_ in [(s_ident[:, :], ap["ident"][:, :]), (s_id32[:, :], ap["ident32"][:, :]),
                   (s_ones[:, :], ap["ones1"][:, :]), (s_on32[:, :], ap["ones32"][:, :]),
                   (s_bq[:, :], ap["bq_pp"][:, :]), (s_bo[:, :], ap["bo_pp"][:, :]),
                   (s_cw[:, :], ap["cw_pp"][:, :]), (s_cb[:, :], ap["cb_pp"][:, :]),
                   (s_maskl[:, :], ap["mask_l"][:, :]), (s_maskr[:, :], ap["mask_r"][:, :])]:
        dma("sync", o_, i_, sem="dC")
    tk_consts = p.tok("dC")
    tk_id = tk_id32 = tk_ones = tk_on32 = tk_consts
    tk_vtones = vop(lambda e, st: e.memset(s_vt[:, 64:32 * 65:65], 1.0))
    tk_zpad = vop(lambda e, st: e.memset(s_z[:, :], 0.0))
    tk_y8ms = vop(lambda e, st: e.memset(s_y8r[:, :], 0))
    tk_mrms = vop(lambda e, st: e.memset(s_mr[0:1, :], 0.0))
    vop(lambda e, st: e.memset(s_ws[:, 4:5], -80.0))
    tk_expc = vop(lambda e, st: e.memset(s_ws[:, 5:6], 0.125))

    def bcast_sb(row_ap, n, bnk, dst, waits):
        tm = mm(bank(bnk, n), s_on32[0:1, 0:128], row_ap, True, True,
                waits=list(waits) + [tk_on32], banks=(bnk,))
        tv = vop(lambda e, st, d=dst, b=bnk, k=n: e.tensor_copy(out=d, in_=bank(b, k)),
                 waits=[tm])
        rel((bnk,), tv)
        return tv

    def bcast_ps(row_ap, n, bnk, waits):
        return mm(bank(bnk, n), s_on32[0:1, 0:128], row_ap, True, True,
                  waits=list(waits) + [tk_on32], banks=(bnk,))

    xr3 = fm(ap["x_res"])
    WqT3 = fm(ap["WqT"])
    WoT3 = fm(ap["WoT"])
    w1T3 = fm(ap["w1T"])
    w2T3 = fm(ap["w2T"])
    wod3 = fm(wodram.ap())

    xslot_free = [None, None]
    tk_stats = []
    for j in range(NB):
        sl = j % 2
        xs = s_xblk[:, 2048 * sl:2048 * sl + 2048]
        lns = s_ln1[:, 2048 * sl:2048 * sl + 2048]
        tkx = dma("sync", xs.rearrange("p (c w) -> p c w", c=8),
                  xr3[:, :, 1 + BS * j:1 + BS * j + BS], waits=[xslot_free[sl]],
                  sem="dX%d" % sl)
        tkb = vop(lambda e, st, o=lns, i=xs: e.tensor_copy(out=o, in_=i), waits=[tkx])
        tkq = vop(lambda e, st, i=lns: e.tensor_mul(s_xsq[:, :], i, i),
                  waits=[tkb, tk_stats[-1] if tk_stats else None])
        tm1 = tm2 = None
        for dc in range(8):
            tm1 = mm(ps[0:1, 1024:1024 + 256], s_ones[:, 0:1],
                     lns[:, 256 * dc:256 * dc + 256], start=(dc == 0), stop=(dc == 7),
                     waits=[tkb, tk_ones], banks=(2,))
            tm2 = mm(ps[0:1, 1536:1536 + 256], s_ones[:, 0:1],
                     s_xsq[:, 256 * dc:256 * dc + 256], start=(dc == 0), stop=(dc == 7),
                     waits=[tkq], banks=(3,))
        t1 = vop(lambda e, st, j=j: e.tensor_scalar(
            out=s_mr[0:1, BS * j:BS * j + BS], in0=ps[0:1, 1024:1024 + 256],
            scalar1=1.0 / D, scalar2=None, op0=ALU.mult), waits=[tm1, tk_mrms])
        t2 = vop(lambda e, st, j=j: e.tensor_scalar(
            out=s_mr[0:1, W + BS * j:W + BS * j + BS], in0=ps[0:1, 1536:1536 + 256],
            scalar1=1.0 / D, scalar2=None, op0=ALU.mult), waits=[tm2])
        rel((2,), t1)
        rel((3,), t2)
        t3 = vop(lambda e, st, j=j: e.tensor_mul(
            s_row[0:1, 0:BS], s_mr[0:1, BS * j:BS * j + BS],
            s_mr[0:1, BS * j:BS * j + BS]), waits=[t1])
        t4 = vop(lambda e, st, j=j: e.tensor_sub(
            s_mr[0:1, W + BS * j:W + BS * j + BS],
            s_mr[0:1, W + BS * j:W + BS * j + BS], s_row[0:1, 0:BS]), waits=[t2, t3])
        t5 = vop(lambda e, st, j=j: e.tensor_scalar(
            out=s_mr[0:1, W + BS * j:W + BS * j + BS],
            in0=s_mr[0:1, W + BS * j:W + BS * j + BS],
            scalar1=EPS, scalar2=None, op0=ALU.add), waits=[t4])
        t6 = vop(lambda e, st, j=j: e.reciprocal(
            out=s_mr[0:1, W + BS * j:W + BS * j + BS],
            in_=s_mr[0:1, W + BS * j:W + BS * j + BS]), waits=[t5])
        tk_stats.append(t6)
        xslot_free[sl] = tkb
    tk_rstd = sop(lambda e, st: e.activation(
        out=s_mr[0:1, W:W + T], in_=s_mr[0:1, W:W + T], func=AF.Sqrt),
        waits=[tk_stats[-1]])

    if "stats" in tapap:
        dma("sync", tapap["stats"][0:1, :], s_mr[0:1, :], waits=[tk_rstd], sem="dTP")

    wst_free = [None, None]
    slot_ctr = [0]
    qstate = {}

    def sub_chunks(width):
        return [(a, min(2048, width - a)) for a in range(0, width, 2048)]

    def j_cast(src_sl, dst_sl, toks, key):
        def go():
            sl = slot_ctr[0] & 1
            slot_ctr[0] += 1
            n = src_sl.shape[-1]
            tkd = dma("sync", s_wst[:, 2048 * sl:2048 * sl + n], src_sl,
                      waits=[wst_free[sl]], sem="dW%d" % sl)
            tkv = vop(lambda e, st, s=sl, d=dst_sl, nn=n: e.tensor_copy(
                out=d, in_=s_wst[:, 2048 * s:2048 * s + nn]), waits=[tkd])
            wst_free[sl] = tkv
            toks[key] = tkv
        return go

    def j_q1(src_sl, key, last, nel):
        def go():
            sl = slot_ctr[0] & 1
            slot_ctr[0] += 1
            n = src_sl.shape[-1]
            tkd = dma("sync", s_wst[:, 2048 * sl:2048 * sl + n], src_sl,
                      waits=[wst_free[sl]], sem="dW%d" % sl)
            tkr = vop(lambda e, st, s=sl, nn=n: e.reduce_sum(
                out=s_ws[:, 3:4], in_=s_wst[:, 2048 * s:2048 * s + nn],
                axis=AX.X, apply_absolute_value=True),
                waits=[tkd, qstate.get((key, "redfree"))])
            if (key, "acc") not in qstate:
                ta = vop(lambda e, st: e.tensor_copy(out=s_ws[:, 2:3], in_=s_ws[:, 3:4]),
                         waits=[tkr])
            else:
                ta = vop(lambda e, st: e.tensor_add(
                    s_ws[:, 2:3], s_ws[:, 2:3], s_ws[:, 3:4]),
                    waits=[tkr, qstate.get((key, "acc"))])
            qstate[(key, "acc")] = ta
            qstate[(key, "redfree")] = ta
            wst_free[sl] = tkr
            if last:
                tt = tr(ps[0:1, 1024:1152], s_ws[:, 2:3], s_id32[:, :],
                        waits=[ta, tk_id32], banks=(2,))
                tsum = vop(lambda e, st: e.reduce_sum(
                    out=s_scl[0:1, 1036:1037], in_=ps[0:1, 1024:1152], axis=AX.X),
                    waits=[tt, qstate.get("rowfree")])
                rel((2,), tsum)
                mcol = 1032 if key == "w1" else 1033
                tmean = vop(lambda e, st, mc=mcol, ne=nel: e.tensor_scalar(
                    out=s_scl[0:1, mc:mc + 1], in0=s_scl[0:1, 1036:1037],
                    scalar1=1.0 / ne, scalar2=None, op0=ALU.mult), waits=[tsum])
                trec = vop(lambda e, st, mc=mcol: e.reciprocal(
                    out=s_scl[0:1, 1037:1038], in_=s_scl[0:1, mc:mc + 1]),
                    waits=[tmean])
                wcol = 0 if key == "w1" else 1
                tbm = mm(ps[:, 1024:1025], s_on32[0:1, 0:128], s_scl[0:1, 1037:1038],
                         True, True, waits=[trec, tk_on32], banks=(2,))
                tbv = vop(lambda e, st, wc=wcol: e.tensor_copy(
                    out=s_ws[:, wc:wc + 1], in_=ps[:, 1024:1025]), waits=[tbm])
                rel((2,), tbv)
                qstate[(key, "winv")] = tbv
                qstate[(key, "mean")] = tmean
                qstate["rowfree"] = trec
        return go

    def j_q2(src_sl, dst_sl, key, toks, tkey):
        def go():
            sl = slot_ctr[0] & 1
            slot_ctr[0] += 1
            n = src_sl.shape[-1]
            wcol = 0 if key == "w1" else 1
            tkd = dma("sync", s_wst[:, 2048 * sl:2048 * sl + n], src_sl,
                      waits=[wst_free[sl]], sem="dW%d" % sl)
            ta = vop(lambda e, st, s=sl, nn=n, wc=wcol: e.tensor_scalar(
                out=s_wst[:, 2048 * s:2048 * s + nn],
                in0=s_wst[:, 2048 * s:2048 * s + nn],
                scalar1=s_ws[:, wc:wc + 1], scalar2=None, op0=ALU.mult),
                waits=[tkd, qstate[(key, "winv")]])
            tc8 = vop(lambda e, st, s=sl, nn=n: e.tensor_scalar(
                out=s_w8t[:, 0:nn], in0=s_wst[:, 2048 * s:2048 * s + nn],
                scalar1=-1.0, scalar2=1.0, op0=ALU.max, op1=ALU.min),
                waits=[ta, qstate.get("w8free")])
            td = vop(lambda e, st, d=dst_sl, nn=n: e.tensor_copy(
                out=d, in_=s_w8t[:, 0:nn]), waits=[tc8])
            qstate["w8free"] = td
            wst_free[sl] = tc8
            toks[tkey] = td
        return go

    jobsA = []
    tk_wq = {}
    for kc in range(8):
        for (a, n) in sub_chunks(3072):
            jobsA.append(j_cast(WqT3[:, kc, a:a + n],
                                s_wq[:, 3072 * kc + a:3072 * kc + a + n], tk_wq,
                                (kc, a)))
    w1subs = [(kc, a, n) for kc in range(8) for (a, n) in sub_chunks(4096)]
    for i, (kc, a, n) in enumerate(w1subs):
        jobsA.append(j_q1(w1T3[:, kc, a:a + n], "w1", i == len(w1subs) - 1, D * H))
    for i, kc in enumerate(range(32)):
        jobsA.append(j_q1(w2T3[:, kc, :], "w2", i == 31, D * H))

    jobsB0 = []
    tk_wocast = {}
    for kc in range(8):
        def mk(kc=kc):
            def go():
                sl = slot_ctr[0] & 1
                slot_ctr[0] += 1
                tkd = dma("sync", s_wst[:, 2048 * sl:2048 * sl + 1024],
                          WoT3[:, kc, :], waits=[wst_free[sl]], sem="dW%d" % sl)
                wsl = kc % 2
                tkv = vop(lambda e, st, s=sl, ws=wsl: e.tensor_copy(
                    out=s_wos[:, 1024 * ws:1024 * ws + 1024],
                    in_=s_wst[:, 2048 * s:2048 * s + 1024]),
                    waits=[tkd, qstate.get(("wos", wsl))])
                tks = dma("sync", wod3[:, kc, :],
                          s_wos[:, 1024 * wsl:1024 * wsl + 1024], waits=[tkv],
                          sem="dWD%d" % wsl)
                qstate[("wos", wsl)] = tks
                wst_free[sl] = tkv
                tk_wocast[kc] = tks
            return go
        jobsB0.append(mk())
    tk_w1q = {}
    for kc in range(8):
        for (a, n) in sub_chunks(4096):
            jobsB0.append(j_q2(w1T3[:, kc, a:a + n],
                               s_w1q[:, 4096 * kc + a:4096 * kc + a + n], "w1",
                               tk_w1q, kc))
    jobsB1 = []
    tk_w2q = {}
    for kc in range(32):
        jobsB1.append(j_q2(w2T3[:, kc, :], s_w2q[:, 1024 * kc:1024 * kc + 1024], "w2",
                           tk_w2q, kc))

    for jb in jobsA[:16]:
        jb()
    jobsA = jobsA[16:]
    nh = min(n_heads, NB)
    per_block = (len(jobsA) + nh - 1) // max(nh, 1)

    lnslot_free = [None, None]
    tk_Y_free = None
    zloc_last = None
    prev_av = [None, None]
    tk_zn_free = None
    tk_rb_free = None
    tk_zr_free = None

    for j in range(nh):
        sl = j % 2
        xs = s_xblk[:, 2048 * sl:2048 * sl + 2048]
        lns = s_ln1[:, 2048 * sl:2048 * sl + 2048]
        tkx = dma("sync", xs.rearrange("p (c w) -> p c w", c=8),
                  xr3[:, :, 1 + BS * j:1 + BS * j + BS], waits=[xslot_free[sl]],
                  sem="dX%d" % sl)
        tva = bcast_sb(s_mr[0:1, BS * j:BS * j + BS], BS, 2, s_abc[:, 0:256],
                       [tk_rstd, tk_Y_free])
        tvb = bcast_sb(s_mr[0:1, W + BS * j:W + BS * j + BS], BS, 3,
                       s_abc[:, 256:512], [tk_rstd])
        tk_norm = []
        tprev2 = None
        for dc in range(8):
            tn1 = vop(lambda e, st, d=dc, x=xs: e.tensor_sub(
                s_atmp[:, 0:256], x[:, 256 * d:256 * d + 256], s_abc[:, 0:256]),
                waits=[tva, tkx, lnslot_free[sl], tprev2])
            tn2 = vop(lambda e, st, d=dc, l=lns: e.tensor_mul(
                l[:, 256 * d:256 * d + 256], s_atmp[:, 0:256], s_abc[:, 256:512]),
                waits=[tvb, tn1])
            tprev2 = tn2
            tk_norm.append(tn2)
        xslot_free[sl] = tk_norm[-1]
        if "ln1" in tapap and j == 0:
            tk_norm[-1] = fence_v(dma("sync", tapap["ln1"][:, :], lns[:, :],
                                      waits=[tk_norm[-1]], sem="dTP"))

        tk_evY = [None, None]
        for cc in range(24):
            bk = cc % 2
            tmm = None
            for dc in range(8):
                tmm = mm(bank(bk, 256),
                         s_wq[:, 3072 * dc + 128 * cc:3072 * dc + 128 * cc + 128],
                         lns[:, 256 * dc:256 * dc + 256], start=(dc == 0), stop=(dc == 7),
                         waits=[tk_norm[dc], tk_wq.get((dc, 0)), tk_wq.get((dc, 2048)),
                                tk_Y_free], banks=(bk,))
            tev = vop(lambda e, st, c=cc, b=bk: e.tensor_scalar(
                out=s_Y[:, 256 * c:256 * c + 256], in0=bank(b, 256),
                scalar1=s_bq[:, c:c + 1], scalar2=None, op0=ALU.add),
                waits=[tmm, tk_consts])
            rel((bk,), tev)
            tk_evY[bk] = tev
        tk_Yall = [tk_evY[0], tk_evY[1]]
        lnslot_free[sl] = tk_Yall[1]
        if "Y" in tapap and j == 0:
            tk_Yall = [fence_v(dma("sync", tapap["Y"][:, :], s_Y[:, :],
                                   waits=tk_Yall, sem="dTP"))]

        dma("gpsimd", s_kdup[0:64, :], s_Y[64:128, 2048:4096],
            waits=tk_Yall + [tk_Y_free], sem="dKD")
        tk_kd1 = tk_kd2 = dma("gpsimd", s_kdup[64:128, :], s_Y[0:64, 2048:4096],
                              waits=tk_Yall, sem="dKD")

        tk_vt = []
        tk_vtev = None
        for grp in range(8):
            tts = []
            for t4 in range(4):
                kc = 4 * grp + t4
                g, hf = kc // 2, kc % 2
                pb = 64 * (g % 2)
                src = s_Y[pb:pb + 64, 4096 + 256 * (g // 2) + 128 * hf:
                          4096 + 256 * (g // 2) + 128 * hf + 128]
                idt = s_ident[pb:pb + 64, pb:pb + 64]
                dst = ps[:, 512 * (4 + t4):512 * (4 + t4) + 32].bitcast(F16)
                tts.append(tr(dst, src, idt,
                              waits=tk_Yall + [tk_id, tk_Y_free], banks=(4 + t4,)))
            tk_vtev = vop(lambda e, st, gg=grp: e.tensor_copy(
                out=s_vt[:, 65 * 4 * gg:65 * 4 * gg + 65 * 4]
                    .rearrange("p (k c) -> p k c", k=4)[:, :, 0:64],
                in_=ps[:, 2048:4096].bitcast(F16)
                    .rearrange("p (k c) -> p k c", k=4)[:, :, 0:64]),
                waits=tts + [tk_vtones, tk_Y_free])
            rel((4, 5, 6, 7), tk_vtev)
            tk_vt.append(tk_vtev)

        pm = 64 * (j // 4)
        zch = j % 4
        for qc in range(2):
            tk_O = [None, None]
            for kc in range(32):
                g_k, hf = kc // 2, kc % 2
                for par in range(2):
                    pb = 64 * par
                    if (g_k % 2) == par:
                        klhs = s_Y[pb:pb + 64, 2048 + 256 * (g_k // 2) + 128 * hf:
                                   2048 + 256 * (g_k // 2) + 128 * hf + 128]
                        kw = tk_Yall
                    else:
                        klhs = s_kdup[pb:pb + 64, 256 * (g_k // 2) + 128 * hf:
                                      256 * (g_k // 2) + 128 * hf + 128]
                        kw = [tk_kd1 if par == 0 else tk_kd2]
                    qrhs = s_Y[pb:pb + 64, 1024 * qc:1024 * qc + 1024]
                    sbk = 2 * par
                    t1 = mm(bank(sbk), klhs, qrhs[:, 0:512], True, True,
                            waits=list(kw), banks=(sbk,))
                    t2 = mm(bank(sbk + 1), klhs, qrhs[:, 512:1024], True, True,
                            waits=[], banks=(sbk + 1,))
                    texp = sop(lambda e, st, b=sbk, pp=par: e.activation(
                        out=s_E[:, 1024 * pp:1024 * pp + 1024],
                        in_=ps[:, 512 * b:512 * b + 1024], func=AF.Exp,
                        scale=s_ws[:, 5:6], bias=s_ws[:, 4:5]),
                        waits=[t1, t2, prev_av[par], tk_expc])
                    rel((sbk, sbk + 1), texp)
                    obank = 4 + 2 * par
                    tav = None
                    for half in range(2):
                        tav = mm(ps[0:65, 512 * obank + 512 * half:
                                     512 * obank + 512 * half + 512],
                                 s_vt[:, 65 * kc:65 * kc + 65],
                                 s_E[:, 1024 * par + 512 * half:
                                     1024 * par + 512 * half + 512],
                                 start=(kc == 0), stop=(kc == 31),
                                 waits=[texp, tk_vt[kc // 4]],
                                 banks=(obank, obank + 1))
                    prev_av[par] = tav
                    tk_O[par] = tav
            for par in range(2):
                obank = 4 + 2 * par
                trc = vop(lambda e, st, b=obank: e.reciprocal(
                    out=s_zr[64:65, 0:1024], in_=ps[64:65, 512 * b:512 * b + 1024]),
                    waits=[tk_O[par], tk_zr_free])
                tb1 = mm(ps[0:64, 512 * 2 * par:512 * 2 * par + 512],
                         s_on32[64:65, 0:64], s_zr[64:65, 0:512],
                         True, True, waits=[trc, tk_on32], banks=(2 * par,))
                tb2 = mm(ps[0:64, 512 * (2 * par + 1):512 * (2 * par + 1) + 512],
                         s_on32[64:65, 0:64], s_zr[64:65, 512:1024], True, True,
                         waits=[], banks=(2 * par + 1,))
                tbv = vop(lambda e, st, pp=par: e.tensor_copy(
                    out=s_rb[0:64, 0:1024],
                    in_=ps[0:64, 512 * 2 * pp:512 * 2 * pp + 1024]),
                    waits=[tb1, tb2, tk_rb_free])
                rel((2 * par, 2 * par + 1), tbv)
                tk_zr_free = tb2
                tev = vop(lambda e, st, b=obank: e.tensor_mul(
                    s_zn[0:64, 0:1024], ps[0:64, 512 * b:512 * b + 1024],
                    s_rb[0:64, 0:1024]), waits=[tbv, tk_zn_free])
                rel((obank, obank + 1), tev)
                tk_rb_free = tev
                if j < 4:
                    srcz = s_zn[0:64, :]
                    tk_shift = tev
                else:
                    tk_shift = dma("gpsimd", s_zn[64:128, 0:1024], s_zn[0:64, 0:1024],
                                   waits=[tev, tk_zn_free], sem="dZN")
                    srcz = s_zn[64:128, :]
                evs = []
                for i in range(4):
                    g = par + 2 * (4 * qc + i)
                    st0 = zch * WZ + 1 + g
                    evs.append(vop(lambda e, st, ii=i, s0=st0, sz=srcz, pb=pm:
                        e.tensor_copy(
                            out=s_z[pb:pb + 64, s0:s0 + 16 * 255 + 1:16],
                            in_=sz[:, 256 * ii:256 * ii + 256]),
                        waits=[tk_shift, tk_zpad]))
                tk_zn_free = evs[-1]
                zloc_last = evs[-1]
        tk_Y_free = zloc_last

        for _ in range(per_block):
            if jobsA:
                jobsA.pop(0)()
    while jobsA:
        jobsA.pop(0)()

    tk_zst = []
    for ch in range(4):
        tk_zst.append(dma("gpsimd", zin[128 * ch:128 * ch + 128, :],
                          s_z[:, ch * WZ:ch * WZ + WZ], waits=[zloc_last], sem="dZI"))
    if "zin" in tapap:
        for ch in range(4):
            tkzt = dma("sync", tapap["zin"][128 * ch:128 * ch + 128, :],
                       s_z[:, ch * WZ:ch * WZ + WZ], waits=[zloc_last], sem="dTP")
        zloc_last = fence_v(tkzt)

    if stop_after == "A":
        ms = vop(lambda e, st: e.memset(s_out[:, 0:512], 0.0), waits=[zloc_last])
        tko = dma("sync", ap["out"][0:128, 0:512], s_out[:, 0:512], waits=[ms], sem="dO0")
        p.emit("sync", lambda e, st: e.nop(),
               waits=[tko] + [p.tok(s) for s in ["dC","dX0","dX1","dTP","dW0","dW1","dWD0","dWD1","dKD","dZN","dZI","dZB","dWO0","dWO1","dXR0","dXR1","dO0","dO1","tT","tV","tS","tG","CC"]])
        return _finish(nc, ctx, p)

    tk_cc = p.emit("gpsimd", lambda e, st: e.collective_compute(
        "AllGather", ALU.bypass, replica_groups=groups,
        ins=[zin.ap().opt()], outs=[zgath.ap().opt()]),
        waits=tk_zst, inc="CC", amt=1)

    for jb in jobsB0:
        jb()

    tk_pid = p.emit("gpsimd", lambda e, st: (
        st.__setitem__("T0", e.compute_val((e.partition_id() % 2) * T)), e.nop())[-1],
        waits=[], inc="tG")
    tk_zb = []
    for zc in range(8):
        tk_zb.append(p.emit("gpsimd", lambda e, st, z=zc: e.dma_start(
            out=s_z[:, 2050 * z:2050 * z + 2050],
            in_=zgath[128 * z:128 * z + 128, bass.ds(st["T0"], 2050)]),
            waits=[tk_cc, tk_pid, zloc_last], inc="dZB", amt=16))

    wos_free = [qstate.get(("wos", 0)), qstate.get(("wos", 1))]
    tk_woL = {}

    def wo_load(oc):
        wsl = oc % 2
        tkd = dma("gpsimd", s_wos[:, 1024 * wsl:1024 * wsl + 1024], wod3[:, oc, :],
                  waits=[wos_free[wsl], tk_wocast.get(oc), zloc_last],
                  sem="dWO%d" % wsl)
        tk_woL[oc] = (tkd, wsl)
        return tkd, wsl

    xr_free = [None, None]
    for jb in jobsB1:
        jb()

    if stop_after == "B0":
        p.emit("sync", lambda e, st: e.nop(),
               waits=[p.tok(s) for s in ["dC","dX0","dX1","dTP","dW0","dW1","dWD0","dWD1","dKD","dZN","dZI","dZB","dWO0","dWO1","dXR0","dXR1","dO0","dO1","tT","tV","tS","tG","CC"]])
        return _finish(nc, ctx, p)

    def crosspart_max(waits):
        tt1 = tr(ps[:, 512 * 6:512 * 6 + 128], s_red[:, 0:128], s_id32[:, :],
                 waits=list(waits) + [tk_id32], banks=(6,))
        tt2 = tr(ps[:, 512 * 7:512 * 7 + 128], s_red[:, 128:256], s_id32[:, :],
                 waits=list(waits), banks=(7,))
        tt3 = tr(ps[0:2, 512 * 5:512 * 5 + 128], s_red[:, 256:258], s_id32[:, :],
                 waits=list(waits), banks=(5,))
        r1 = vop(lambda e, st: e.reduce_max(out=s_red[:, 520:521],
                 in_=ps[:, 512 * 6:512 * 6 + 128], axis=AX.X), waits=[tt1])
        r2 = vop(lambda e, st: e.reduce_max(out=s_red[:, 521:522],
                 in_=ps[:, 512 * 7:512 * 7 + 128], axis=AX.X), waits=[tt2])
        r3 = vop(lambda e, st: e.reduce_max(out=s_red[0:2, 522:523],
                 in_=ps[0:2, 512 * 5:512 * 5 + 128], axis=AX.X), waits=[tt3])
        rel((6,), r1)
        rel((7,), r2)
        rel((5,), r3)
        m1 = tr(ps[0:1, 512 * 6:512 * 6 + 128], s_red[:, 520:521], s_id32[:, :],
                waits=[r1], banks=(6,))
        m2 = tr(ps[0:1, 512 * 7:512 * 7 + 128], s_red[:, 521:522], s_id32[:, :],
                waits=[r2], banks=(7,))
        m3 = tr(ps[0:1, 512 * 5:512 * 5 + 2], s_red[0:2, 522:523], s_id32[0:2, 0:2],
                waits=[r3], banks=(5,))
        c1 = vop(lambda e, st: e.tensor_copy(
            out=s_scl[0:1, 0:128], in_=ps[0:1, 512 * 6:512 * 6 + 128]), waits=[m1])
        c2 = vop(lambda e, st: e.tensor_copy(
            out=s_scl[0:1, 128:256], in_=ps[0:1, 512 * 7:512 * 7 + 128]), waits=[m2])
        c3 = vop(lambda e, st: e.tensor_copy(
            out=s_scl[0:1, 256:258], in_=ps[0:1, 512 * 5:512 * 5 + 2]), waits=[m3])
        rel((6,), c1)
        rel((7,), c2)
        rel((5,), c3)
        return [c1, c2, c3], [m1, m2, m3]

    tk_lt_rd = None
    st_guard = [None, None]
    row_guard = qstate.get("rowfree")
    tk_red_guard = []
    tk_scl_guard = [qstate.get("rowfree")]
    tk_xq8_rd = [tk_w2q.get(31)]
    tk_xqb_rd = None
    tk_hy_wr_ok = tk_w2q.get(31)
    tk_x1w_rd = None
    tk_out_dma = [None, None]
    tk_out_done = []
    prev_te1 = None
    for tc in range(8):
        j0 = 256 * tc
        tk_x1w = []
        tk_wm = {}
        for zc in range(8):
            tkwd, wsl = wo_load(zc)
            for oc in range(8):
                tk_wm[oc] = mm(ps[:, 512 * oc:512 * oc + 258],
                               s_wos[:, 1024 * wsl + 128 * oc:1024 * wsl + 128 * oc + 128],
                               s_z[:, 2050 * zc + j0:2050 * zc + j0 + 258],
                               start=(zc == 0), stop=(zc == 7),
                               waits=[tkwd, tk_zb[7]], banks=(oc,))
            wos_free[wsl] = tk_wm[7]
        for oc in range(8):
            sl = oc % 2
            tkxr = dma("sync", s_xr[:, 512 * sl:512 * sl + 258],
                       xr3[:, oc, j0:j0 + 258], waits=[xr_free[sl]], sem="dXR%d" % sl)
            tstt = vop(lambda e, st, o=oc: e.scalar_tensor_tensor(
                out=s_x1w[:, 258 * o:258 * o + 258], in0=ps[:, 512 * o:512 * o + 258],
                scalar=s_bo[:, o:o + 1], in1=s_xr[:, 512 * (o % 2):512 * (o % 2) + 258],
                op0=ALU.add, op1=ALU.add),
                waits=[tk_wm[oc], tkxr, tk_consts, tk_x1w_rd])
            rel((oc,), tstt)
            xr_free[sl] = tstt
            tk_x1w.append(tstt)
        if "x1w" in tapap and tc == 0:
            tk_x1w = [fence_v(dma("sync", tapap["x1w"][:, :], s_x1w[:, :],
                                  waits=tk_x1w, sem="dTP"))] * 8
        tm1 = tm2 = None
        tk_c1rd = [st_guard[0]]
        tk_c2rd = [st_guard[1]]
        for dc in range(8):
            tcst = vop(lambda e, st, d=dc: e.tensor_copy(
                out=s_cast1[:, 0:258], in_=s_x1w[:, 258 * d:258 * d + 258]),
                waits=[tk_x1w[dc]] + tk_c1rd)
            tsq = vop(lambda e, st: e.tensor_mul(
                s_cast2[:, 0:258], s_cast1[:, 0:258], s_cast1[:, 0:258]),
                waits=[tcst] + tk_c2rd)
            tm1 = mm(ps[0:1, 1024:1024 + 258], s_ones[:, 0:1], s_cast1[:, 0:258],
                     start=(dc == 0), stop=(dc == 7), waits=[tcst, tk_ones],
                     banks=(2,))
            tm2 = mm(ps[0:1, 1536:1536 + 258], s_ones[:, 0:1], s_cast2[:, 0:258],
                     start=(dc == 0), stop=(dc == 7), waits=[tsq], banks=(3,))
            tk_c1rd, tk_c2rd = [tsq, tm1], [tm2]
        st_guard = [tm1, tm2]
        u1 = vop(lambda e, st, a=j0: e.tensor_scalar(
            out=s_mr[0:1, a:a + 258], in0=ps[0:1, 1024:1024 + 258],
            scalar1=1.0 / D, scalar2=None, op0=ALU.mult), waits=[tm1, tk_mrms])
        u2 = vop(lambda e, st, a=j0: e.tensor_scalar(
            out=s_mr[0:1, W + a:W + a + 258], in0=ps[0:1, 1536:1536 + 258],
            scalar1=1.0 / D, scalar2=None, op0=ALU.mult), waits=[tm2])
        rel((2,), u1)
        rel((3,), u2)
        u3 = vop(lambda e, st, a=j0: e.tensor_mul(
            s_row[0:1, 0:258], s_mr[0:1, a:a + 258], s_mr[0:1, a:a + 258]),
            waits=[u1, row_guard])
        u4 = vop(lambda e, st, a=j0: e.tensor_sub(
            s_mr[0:1, W + a:W + a + 258], s_mr[0:1, W + a:W + a + 258],
            s_row[0:1, 0:258]), waits=[u2, u3])
        u5 = vop(lambda e, st, a=j0: e.tensor_scalar(
            out=s_mr[0:1, W + a:W + a + 258], in0=s_mr[0:1, W + a:W + a + 258],
            scalar1=EPS, scalar2=None, op0=ALU.add), waits=[u4])
        u6 = vop(lambda e, st, a=j0: e.reciprocal(
            out=s_mr[0:1, W + a:W + a + 258], in_=s_mr[0:1, W + a:W + a + 258]),
            waits=[u5])
        row_guard = u3
        tk_rstd2 = sop(lambda e, st, a=j0: e.activation(
            out=s_mr[0:1, W + a:W + a + 258], in_=s_mr[0:1, W + a:W + a + 258],
            func=AF.Sqrt), waits=[u6, tk_rstd])
        tmb = bcast_ps(s_mr[0:1, j0:j0 + 258], 258, 4, [tk_rstd2, u1])
        tk_sub = []
        for dc in range(8):
            ts_ = vop(lambda e, st, d=dc: e.tensor_sub(
                s_lt[:, 258 * d:258 * d + 258], s_x1w[:, 258 * d:258 * d + 258],
                bank(4, 258)), waits=[tmb, tk_lt_rd, tk_x1w[dc]])
            tk_sub.append(ts_)
        rel((4,), tk_sub[-1])
        tb_ = vop(lambda e, st: e.scalar_tensor_tensor(
            out=s_red[:, 0:258], in0=s_lt[:, 0:258], scalar=-1.0,
            in1=s_lt[:, 0:258], op0=ALU.mult, op1=ALU.max),
            waits=[tk_sub[0]] + tk_red_guard)
        for dci in range(1, 8):
            t_a = vop(lambda e, st, d=dci: e.scalar_tensor_tensor(
                out=s_red[:, 258:516], in0=s_lt[:, 258 * d:258 * d + 258],
                scalar=-1.0, in1=s_lt[:, 258 * d:258 * d + 258],
                op0=ALU.mult, op1=ALU.max),
                waits=[tk_sub[dci], tb_])
            tb_ = vop(lambda e, st: e.tensor_tensor(out=s_red[:, 0:258],
                      in0=s_red[:, 0:258], in1=s_red[:, 258:516], op=ALU.max),
                      waits=[t_a])
        cps, mns = crosspart_max([tb_] + tk_scl_guard)
        q1 = vop(lambda e, st, jj=j0: e.tensor_mul(s_scl[0:1, 258:516],
                 s_scl[0:1, 0:258], s_mr[0:1, W + jj:W + jj + 258]),
                 waits=cps + [tk_rstd2])
        q2 = vop(lambda e, st: e.tensor_scalar(out=s_scl[0:1, 774:1032],
                 in0=s_scl[0:1, 258:516], scalar1=1e-5, scalar2=None, op0=ALU.max),
                 waits=[q1])
        q3 = vop(lambda e, st: e.reciprocal(out=s_scl[0:1, 258:516],
                 in_=s_scl[0:1, 774:1032]), waits=[q2])
        q4 = vop(lambda e, st, jj=j0: e.tensor_mul(s_scl[0:1, 258:516],
                 s_scl[0:1, 258:516], s_mr[0:1, W + jj:W + jj + 258]), waits=[q3])
        q5 = vop(lambda e, st: e.tensor_scalar(out=s_scl[0:1, 258:516],
                 in0=s_scl[0:1, 258:516], scalar1=127.0, scalar2=None, op0=ALU.mult),
                 waits=[q4])
        q6 = vop(lambda e, st: e.tensor_scalar(out=s_scl[0:1, 516:774],
                 in0=s_scl[0:1, 774:1032], scalar1=s_scl[0:1, 1032:1033],
                 scalar2=1.0 / 127.0, op0=ALU.mult, op1=ALU.mult),
                 waits=[q2, qstate[("w1", "mean")]])
        if tc == 0:
            q6 = vop(lambda e, st: e.tensor_mul(s_scl[0:1, 516:774], s_scl[0:1, 516:774],
                     s_maskl[0:1, :]), waits=[q6, tk_consts])
        if tc == 7:
            q6 = vop(lambda e, st: e.tensor_mul(s_scl[0:1, 516:774], s_scl[0:1, 516:774],
                     s_maskr[0:1, :]), waits=[q6, tk_consts])
        tcb = bcast_ps(s_scl[0:1, 258:516], 258, 4, [q5])
        tsbv = bcast_sb(s_scl[0:1, 516:774], 258, 5, s_bc[:, 0:258], [q6])
        tq8 = []
        for dc in range(8):
            tq8.append(vop(lambda e, st, d=dc: e.tensor_mul(
                s_xq8[:, 258 * d:258 * d + 258], s_lt[:, 258 * d:258 * d + 258],
                bank(4, 258)), waits=[tcb, tk_sub[dc]] + tk_xq8_rd))
        tk_xq8_rd = []
        rel((4,), tq8[-1])
        tqb = vop(lambda e, st: e.tensor_copy(out=s_xqb[:, :], in_=s_xq8[:, :]),
                  waits=tq8 + [tk_xqb_rd])
        if "w1q" in tapap and tc == 0:
            tqb = fence_v(dma("sync", tapap["w1q"][:, :], s_w1q[:, :],
                              waits=[tqb, tk_w1q.get(7)], sem="dTP"))
        tk_lt_rd = tq8[-1]
        if "xq" in tapap and tc == 0:
            tqb = fence_v(dma("sync", tapap["xq"][:, :], s_xqb[:, :], waits=[tqb],
                              sem="dTP"))
        tk_z2 = []
        prev_tt1 = None
        prev_tt2 = None
        prev_silu = None
        for hc in range(32):
            bk = hc % 2
            tmm = None
            for dc in range(8):
                tmm = mm(bank(bk, 258),
                         s_w1q[:, 4096 * dc + 128 * hc:4096 * dc + 128 * hc + 128],
                         s_xqb[:, 258 * dc:258 * dc + 258], start=(dc == 0),
                         stop=(dc == 7), waits=[tqb, tk_w1q.get(dc)], banks=(bk,))
            tev = vop(lambda e, st, h=hc, b=bk: e.tensor_mul(
                s_hy[:, 258 * h:258 * h + 258], bank(b, 258),
                s_bc[:, 0:258]), waits=[tmm, tsbv, tk_hy_wr_ok])
            rel((bk,), tev)
            if "hpre" in tapap and tc == 0:
                tev = fence_v(dma("sync", tapap["hpre"][:, 258 * hc:258 * hc + 258],
                                  s_hy[:, 258 * hc:258 * hc + 258], waits=[tev],
                                  sem="dTP"))
            tt0 = sop(lambda e, st, h=hc: e.activation(
                out=s_tc0[:, 0:256], in_=s_hy[:, 258 * h:258 * h + 256],
                func=AF.Identity, bias=s_cb[:, h:h + 1], scale=s_cw[:, h:h + 1]),
                waits=[tev, tk_consts, prev_tt1])
            tt1 = vop(lambda e, st, h=hc: e.scalar_tensor_tensor(
                out=s_tc0[:, 258:258 + 256],
                in0=s_hy[:, 258 * h + 1:258 * h + 257],
                scalar=s_cw[:, 32 + h:32 + h + 1], in1=s_tc0[:, 0:256],
                op0=ALU.mult, op1=ALU.add), waits=[tt0, tev, prev_tt2])
            tt2 = vop(lambda e, st, h=hc: e.scalar_tensor_tensor(
                out=s_tcb[:, 0:256],
                in0=s_hy[:, 258 * h + 2:258 * h + 258],
                scalar=s_cw[:, 64 + h:64 + h + 1], in1=s_tc0[:, 258:258 + 256],
                op0=ALU.mult, op1=ALU.add), waits=[tt1, prev_silu])
            tt3 = sop(lambda e, st, h=hc: e.activation(
                out=s_hy[:, 258 * h:258 * h + 256],
                in_=s_tcb[:, 0:256], func=AF.Silu), waits=[tt2])
            prev_tt1, prev_tt2, prev_silu = tt1, tt2, tt3
            tk_z2.append(tt3)
        if "h" in tapap and tc == 0:
            tk_z2 = [fence_v(dma("sync", tapap["h"][:, :], s_hy[:, :], waits=tk_z2,
                                 sem="dTP"))] * 32
        ab = vop(lambda e, st: e.scalar_tensor_tensor(
            out=s_red[:, 0:256], in0=s_hy[:, 0:256], scalar=-1.0,
            in1=s_hy[:, 0:256], op0=ALU.mult, op1=ALU.max),
            waits=[tk_z2[0]] + mns)
        for hc in range(1, 32):
            t_a = vop(lambda e, st, h=hc: e.scalar_tensor_tensor(
                out=s_red[:, 258:258 + 256], in0=s_hy[:, 258 * h:258 * h + 256],
                scalar=-1.0, in1=s_hy[:, 258 * h:258 * h + 256],
                op0=ALU.mult, op1=ALU.max),
                waits=[tk_z2[hc], ab])
            ab = vop(lambda e, st: e.tensor_tensor(out=s_red[:, 0:256],
                     in0=s_red[:, 0:256], in1=s_red[:, 258:258 + 256], op=ALU.max),
                     waits=[t_a])
        ab2 = vop(lambda e, st: e.memset(s_red[:, 256:258], 0.0), waits=[ab, cps[2]])
        cps2, mns2 = crosspart_max([ab, ab2] + cps)
        r1 = vop(lambda e, st: e.tensor_scalar(out=s_scl[0:1, 774:1032],
                 in0=s_scl[0:1, 0:258], scalar1=1e-5, scalar2=None, op0=ALU.max),
                 waits=cps2 + [q6])
        r2 = vop(lambda e, st: e.reciprocal(out=s_scl[0:1, 0:258],
                 in_=s_scl[0:1, 774:1032]), waits=[r1])
        r3 = vop(lambda e, st: e.tensor_scalar(out=s_scl[0:1, 0:258],
                 in0=s_scl[0:1, 0:258], scalar1=127.0, scalar2=None, op0=ALU.mult),
                 waits=[r2])
        r4 = vop(lambda e, st: e.tensor_scalar(out=s_scl[0:1, 516:774],
                 in0=s_scl[0:1, 774:1032], scalar1=s_scl[0:1, 1033:1034],
                 scalar2=1.0 / 127.0, op0=ALU.mult, op1=ALU.mult),
                 waits=[r1, qstate[("w2", "mean")], tsbv])
        ts2v = bcast_sb(s_scl[0:1, 0:256], 256, 4, s_bc[:, 516:516 + 256],
                        [r3, tcb, tq8[-1]])
        tg2v = bcast_sb(s_scl[0:1, 516:516 + 256], 256, 5, s_bc[:, 258:258 + 256],
                        [r4, tsbv])
        tk_scl_guard = [ts2v, tg2v]
        tcv_prev = [None, None]
        tmm2 = {oc: None for oc in range(8)}
        last_mm = None
        ty8_last = None
        for hc in range(32):
            sl2 = hc % 2
            ty8 = vop(lambda e, st, h=hc, s=sl2: e.tensor_mul(
                s_y8r[:, 260 * s:260 * s + 256],
                s_hy[:, 258 * h:258 * h + 256], s_bc[:, 516:516 + 256]),
                waits=[ts2v, tk_z2[hc], tcv_prev[sl2], tk_y8ms])
            tcv = vop(lambda e, st, s=sl2: e.tensor_copy(
                out=s_ybr[:, 260 * s:260 * s + 256],
                in_=s_y8r[:, 260 * s:260 * s + 256]),
                waits=[ty8, (last_mm if hc >= 2 else None)])
            for oc in range(8):
                tmm2[oc] = mm(bank(oc, 256),
                              s_w2q[:, 1024 * hc + 128 * oc:1024 * hc + 128 * oc + 128],
                              s_ybr[:, 260 * sl2:260 * sl2 + 256], start=(hc == 0),
                              stop=(hc == 31), waits=[tcv, tk_w2q.get(hc)],
                              banks=(oc,))
            tcv_prev[sl2] = tmm2[7]
            last_mm = tmm2[7]
            ty8_last = ty8
        tk_hy_wr_ok = ty8_last
        for oc in range(8):
            so_sl = oc % 2
            te1 = vop(lambda e, st, b=oc: e.tensor_mul(
                s_tc0[:, 0:256], bank(b, 256), s_bc[:, 258:258 + 256]),
                waits=[tmm2[oc], tg2v, prev_te1])
            rel((oc,), te1)
            te2 = vop(lambda e, st, o=oc, t=tc, s=so_sl: e.tensor_add(
                s_out[:, 256 * s:256 * s + 256], s_tc0[:, 0:256],
                s_x1w[:, 258 * o + 1:258 * o + 257]),
                waits=[te1, tk_out_dma[so_sl], tk_x1w[oc]])
            prev_te1 = te2
            tko = dma("sync", fm(ap["out"])[:, oc, 256 * tc:256 * tc + 256],
                      s_out[:, 256 * so_sl:256 * so_sl + 256], waits=[te2],
                      sem="dO%d" % so_sl)
            tk_out_dma[so_sl] = tko
            tk_out_done.append(tko)
        tk_x1w_rd = tk_out_done[-1]
        tk_red_guard = mns2 + cps2
        tk_xqb_rd = tk_z2[-1]
        if "xq2" in tapap and tc == 0:
            dma("sync", tapap["xq2"][:, :], s_ybr[:, :],
                waits=[tcv_prev[0], tcv_prev[1]], sem="dTP")

    p.emit("sync", lambda e, st: e.nop(),
           waits=[p.tok(s) for s in ["dC","dX0","dX1","dTP","dW0","dW1","dWD0","dWD1","dKD","dZN","dZI","dZB","dWO0","dWO1","dXR0","dXR1","dO0","dO1","tT","tV","tS","tG","CC"]])
    return _finish(nc, ctx, p)


def _finish(nc, ctx, p):
    sems = {}
    for s in ["dC", "dX0", "dX1", "dTP", "dW0", "dW1", "dWD0", "dWD1", "dKD", "dZN", "dZI",
              "dZB", "dWO0", "dWO1", "dXR0", "dXR1", "dO0", "dO1",
              "tT", "tV", "tS", "tG", "CC"]:
        sems[s] = ctx.enter_context(nc.semaphore(s))
    blk = ctx.enter_context(nc.Block())
    state = {}

    @blk.sync
    def _(e):
        p.run("sync", e, sems, state)

    @blk.tensor
    def _(e):
        p.run("tensor", e, sems, state)

    @blk.vector
    def _(e):
        p.run("vector", e, sems, state)

    @blk.scalar
    def _(e):
        p.run("scalar", e, sems, state)

    @blk.gpsimd
    def _(e):
        p.run("gpsimd", e, sems, state)

    ctx.close()
    return nc



def kernel(**inputs):
    from concourse.bass_utils import run_bass_kernel_spmd
    maps = stage_inputs(inputs, n_cores=8)
    nc = build_nc(n_cores=8, n_heads=NB, stop_after="full", taps=())
    res = run_bass_kernel_spmd(nc, maps, core_ids=list(range(8)))
    return gather_out(res.results)


# version 03
# speedup vs baseline: 1.2699x; 1.2699x over previous
import numpy as np
import ml_dtypes
from contextlib import ExitStack
import concourse.bass as bass
import concourse.mybir as mybir

F32 = mybir.dt.float32
BF16 = mybir.dt.bfloat16
I8 = mybir.dt.int8
F8 = mybir.dt.float8e4
F16 = mybir.dt.float16
AF = mybir.ActivationFunctionType
ALU = mybir.AluOpType
AX = mybir.AxisListType
BF = ml_dtypes.bfloat16
F16N = np.float16

D = 1024
S = 4096
T = 2048
NB = 8
BS = 256
H = 4096
W = 2050
WZ = 4098
EPS = 1e-5


def qkv_perm():
    perm = np.empty(3072, np.int64)
    for sec in range(3):
        for g in range(16):
            perm[1024 * sec + 64 * g: 1024 * sec + 64 * g + 64] = \
                np.arange(192 * g + 64 * sec, 192 * g + 64 * sec + 64)
    return perm

def zrow_to_cglob():
    out = np.empty(1024, np.int64)
    for r in range(1024):
        rank, rr = divmod(r, 512)
        chunk, part = divmod(rr, 128)
        m = (4 if part >= 64 else 0) + chunk
        out[r] = 64 * (8 * rank + m) + part % 64
    return out

def stage_inputs(inputs, n_cores=8):
    x = np.asarray(inputs["L0_attn_x"], np.float32)
    Wqkv = np.asarray(inputs["L0_attn_W_qkv"], np.float32)
    bqkv = np.asarray(inputs["L0_attn_b_qkv"], np.float32)
    Wo = np.asarray(inputs["L0_attn_W_o"], np.float32)
    bo = np.asarray(inputs["L0_attn_b_o"], np.float32)
    w1 = np.asarray(inputs["L0_ffn_w1"], np.float32)
    cw = np.asarray(inputs["L0_ffn_conv_w"], np.float32)
    cb = np.asarray(inputs["L0_ffn_conv_b"], np.float32)
    w2 = np.asarray(inputs["L0_ffn_w2"], np.float32)

    perm = qkv_perm()
    com = dict(
        WqT=np.ascontiguousarray(Wqkv.T[:, perm]),
        bq_pp=np.ascontiguousarray(bqkv[perm].reshape(24, 128).T),
        WoT=np.ascontiguousarray(Wo.T[zrow_to_cglob(), :]),
        bo_pp=np.ascontiguousarray(bo.reshape(8, 128).T),
        w1T=np.ascontiguousarray(w1.T),
        w2T=np.ascontiguousarray(w2.T),
        cw_pp=np.ascontiguousarray(cw[:, 0, :].reshape(32, 128, 3).transpose(1, 2, 0).reshape(128, 96)),
        cb_pp=np.ascontiguousarray(cb.reshape(32, 128).T),
        ident=np.eye(128, dtype=F16N),
        ident32=np.eye(128, dtype=np.float32),
        ones1=np.ones((128, 1), dtype=F16N),
        ones32=np.ones((128, 128), dtype=np.float32),
    )
    maps = []
    for c in range(n_cores):
        b, off = divmod(c, 2)
        xr = np.zeros((D, W), np.float32)
        if off == 0:
            xr[:, 1:2050] = x[b, 0:2049, :].T
        else:
            xr[:, 0:2049] = x[b, 2047:4096, :].T
        mask_l = np.ones((1, 258), np.float32)
        mask_r = np.ones((1, 258), np.float32)
        if off == 0:
            mask_l[0, 0] = 0.0
        else:
            mask_r[0, 257] = 0.0
        maps.append(dict(com, x_res=xr, mask_l=mask_l, mask_r=mask_r))
    return maps

def gather_out(results):
    out = np.empty((4, S, D), np.float32)
    for c, r in enumerate(results):
        b, off = divmod(c, 2)
        out[b, T * off:T * off + T, :] = np.asarray(r["out"]).T
    return out


class P:
    ENGS = ("sync", "tensor", "vector", "scalar", "gpsimd")

    def __init__(self):
        self.q = {e: [] for e in self.ENGS}
        self.cnt = {}
        self.water = {e: {} for e in self.ENGS}
        self.bkg = [None] * 8

    def tok(self, sem):
        return (sem, self.cnt.get(sem, 0))

    def emit(self, eng, fn, waits=(), inc=None, amt=1):
        best = {}
        for smv in waits:
            if smv is None:
                continue
            s, v = smv
            if v > best.get(s, 0):
                best[s] = v
        wl = []
        for s, v in best.items():
            if v > self.water[eng].get(s, 0):
                self.water[eng][s] = v
                wl.append((s, v))
        tk = None
        if inc is not None:
            self.cnt[inc] = self.cnt.get(inc, 0) + amt
            tk = (inc, self.cnt[inc])
        self.q[eng].append((fn, wl, inc, amt))
        return tk

    def run(self, eng_name, handle, sems, state):
        for fn, wl, inc, amt in self.q[eng_name]:
            for s, v in wl:
                handle.wait_ge(sems[s], v)
            inst = fn(handle, state)
            if inc is not None:
                inst.then_inc(sems[inc], amt)


def build_nc(n_cores=8, n_heads=NB, stop_after="full", taps=(), groups=None):
    nc = bass.Bass("TRN2", target_bir_lowering=False)
    p = P()
    if groups is None:
        groups = [[i, i + 1] for i in range(0, n_cores, 2)]

    ap = {}
    for nm, sh, dt in [("x_res", [D, W], F32), ("WqT", [D, 3072], F32),
                       ("bq_pp", [128, 24], F32), ("WoT", [D, D], F32),
                       ("bo_pp", [128, 8], F32), ("w1T", [D, H], F32),
                       ("w2T", [H, D], F32), ("cw_pp", [128, 96], F32),
                       ("cb_pp", [128, 32], F32), ("ident", [128, 128], F16),
                       ("ident32", [128, 128], F32), ("ones1", [128, 1], F16),
                       ("ones32", [128, 128], F32),
                       ("mask_l", [1, 258], F32), ("mask_r", [1, 258], F32)]:
        ap[nm] = nc.dram_tensor(nm, sh, dt, kind="ExternalInput").ap()
    ap["out"] = nc.dram_tensor("out", [D, T], F32, kind="ExternalOutput").ap()
    zin = nc.dram_tensor("zin", [512, WZ], F16)
    zgath = nc.dram_tensor("zgath", [1024, WZ], F16)
    wodram = nc.dram_tensor("wodram", [1024, 1024], F16)

    tap_shapes = {"stats": ([1, 2 * W], F32), "Y": ([128, 24 * BS], F16),
                  "ln1": ([128, 8 * BS], F16), "zin": ([512, WZ], F16),
                  "x1w": ([128, 8 * 258], F32), "xq": ([128, 8 * 258], BF16),
                  "h": ([128, 32 * 258], F32), "xq2": ([128, 2 * 260], BF16),
                  "hpre": ([128, 32 * 258], F32), "w1q": ([128, 32768], F8)}
    tapap = {}
    for tname in taps:
        sh, dt = tap_shapes[tname]
        tapap[tname] = nc.dram_tensor("tap_" + tname, sh, dt, kind="ExternalOutput").ap()

    ctx = ExitStack()
    sbt = lambda n, s, d: ctx.enter_context(nc.sbuf_tensor(n, s, d))

    s_ident = sbt("s_ident", [128, 128], F16)
    s_id32 = sbt("s_id32", [128, 128], F32)
    s_ones = sbt("s_ones", [128, 1], F16)
    s_on32 = sbt("s_on32", [128, 128], F32)
    s_bq = sbt("s_bq", [128, 24], F32)
    s_bo = sbt("s_bo", [128, 8], F32)
    s_cw = sbt("s_cw", [128, 96], F32)
    s_cb = sbt("s_cb", [128, 32], F32)
    s_maskl = sbt("s_maskl", [1, 258], F32)
    s_maskr = sbt("s_maskr", [1, 258], F32)
    s_mr = sbt("s_mr", [1, 2 * W], F32)
    s_scl = sbt("s_scl", [1, 1040], F32)
    s_row = s_scl[0:1, 0:512]
    s_ws = sbt("s_ws", [128, 8], F32)

    AR1 = sbt("AR1", [128, 12288], F32)
    s_wq = AR1[:, 0:12288].bitcast(F16)
    s_w1q = AR1[:, 0:8192].bitcast(F8)
    s_xqb = AR1[:, 8192:9224].bitcast(BF16)
    s_y8r = AR1[:, 9224:9354].bitcast(I8)
    s_ybr = AR1[:, 9354:9614].bitcast(BF16)
    s_bc = AR1[:, 9614:10388]
    s_cast1 = AR1[:, 10388:10644].bitcast(BF16)
    s_cast2 = AR1[:, 10644:10900].bitcast(BF16)

    AR2 = sbt("AR2", [128, 16704], F32)
    s_xblk = AR2[:, 0:4096]
    s_ln1 = AR2[:, 4096:6144].bitcast(F16)
    s_xsq = AR2[:, 6144:7168].bitcast(F16)
    s_Y = AR2[:, 7168:10240].bitcast(F16)
    s_kdup = AR2[:, 10240:11264].bitcast(F16)
    s_vt = AR2[:, 11264:12304].bitcast(BF16)
    s_E = AR2[:, 12304:14352].bitcast(BF16)
    s_zn = AR2[:, 14352:14864].bitcast(F16)
    s_zr = AR2[:, 14864:15888]
    s_qd = AR2[:, 6144:7168].bitcast(F16)
    s_atmp = AR2[:, 15888:16144]
    s_abc = AR2[:, 16144:16656]
    s_x1w = AR2[:, 0:2064]
    s_lt = AR2[:, 2064:4128]
    s_xq8 = AR2[:, 4128:4644].bitcast(I8)
    s_w8t = AR2[:, 4128:4640].bitcast(I8)
    s_tc0 = AR2[:, 4644:5160]
    s_red = AR2[:, 5160:5684]
    s_tcb = AR2[:, 5684:5942]
    s_out = AR2[:, 5942:6454]
    s_xr = AR2[:, 6454:7478]
    s_wos = AR2[:, 7478:8502].bitcast(F16)
    s_w2q = AR2[:, 8502:16694].bitcast(F8)

    AR3 = sbt("AR3", [128, 8256], F32)
    s_wst = AR3[:, 0:4096]
    s_hy = AR3[:, 0:8256]
    OT = AR3[:, 4096:6144]

    ARZ = sbt("ARZ", [128, 8200], F32)
    s_z = ARZ[:, :].bitcast(F16)

    ps = ctx.enter_context(nc.psum_tensor("ps", [128, 4096], F32))

    def bank(i, cols=512, n0=0):
        return ps[:, 512 * i + n0: 512 * i + n0 + cols]

    def dma(eng, out, in_, waits=(), sem=None):
        assert sem is not None
        return p.emit(eng, lambda e, st, o=out, i=in_: e.dma_start(out=o, in_=i),
                      waits=waits, inc=sem, amt=16)

    def mm(out, lhsT, rhs, start, stop, waits=(), banks=()):
        w = list(waits)
        if start:
            w += [p.bkg[b] for b in banks]
        return p.emit("tensor",
                      lambda e, st, o=out, l=lhsT, r=rhs, a=start, b=stop:
                      e.matmul(o, l, r, start=a, stop=b), waits=w, inc="tT")

    def tr(out, in_, idt, waits=(), banks=()):
        w = list(waits) + [p.bkg[b] for b in banks]
        return p.emit("tensor", lambda e, st, o=out, i=in_, d=idt: e.transpose(o, i, d),
                      waits=w, inc="tT")

    def rel(banks, tok):
        for b in banks:
            p.bkg[b] = tok

    def vop(fn, waits=()):
        return p.emit("vector", fn, waits=waits, inc="tV")

    def sop(fn, waits=()):
        return p.emit("scalar", fn, waits=waits, inc="tS")

    def fence_v(tok):
        return vop(lambda e, st: e.memset(s_ws[:, 6:7], 0.0), waits=[tok])

    def fm(dram_ap):
        return dram_ap.rearrange("(c p) w -> p c w", p=128)

    for o_, i_ in [(s_ident[:, :], ap["ident"][:, :]), (s_id32[:, :], ap["ident32"][:, :]),
                   (s_ones[:, :], ap["ones1"][:, :]), (s_on32[:, :], ap["ones32"][:, :]),
                   (s_bq[:, :], ap["bq_pp"][:, :]), (s_bo[:, :], ap["bo_pp"][:, :]),
                   (s_cw[:, :], ap["cw_pp"][:, :]), (s_cb[:, :], ap["cb_pp"][:, :]),
                   (s_maskl[:, :], ap["mask_l"][:, :]), (s_maskr[:, :], ap["mask_r"][:, :])]:
        dma("sync", o_, i_, sem="dC")
    tk_consts = p.tok("dC")
    tk_id = tk_id32 = tk_ones = tk_on32 = tk_consts
    tk_vtones = vop(lambda e, st: e.memset(s_vt[:, 64:32 * 65:65], 1.0))
    tk_zpad = vop(lambda e, st: e.memset(s_z[:, :], 0.0))
    tk_y8ms = vop(lambda e, st: e.memset(s_y8r[:, :], 0))
    tk_mrms = vop(lambda e, st: e.memset(s_mr[0:1, :], 0.0))
    vop(lambda e, st: e.memset(s_ws[:, 4:5], -80.0))
    tk_expc = vop(lambda e, st: e.memset(s_ws[:, 5:6], 0.125))

    def bcast_sb(row_ap, n, bnk, dst, waits):
        tm = mm(bank(bnk, n), s_on32[0:1, 0:128], row_ap, True, True,
                waits=list(waits) + [tk_on32], banks=(bnk,))
        tv = vop(lambda e, st, d=dst, b=bnk, k=n: e.tensor_copy(out=d, in_=bank(b, k)),
                 waits=[tm])
        rel((bnk,), tv)
        return tv

    def bcast_ps(row_ap, n, bnk, waits):
        return mm(bank(bnk, n), s_on32[0:1, 0:128], row_ap, True, True,
                  waits=list(waits) + [tk_on32], banks=(bnk,))

    xr3 = fm(ap["x_res"])
    WqT3 = fm(ap["WqT"])
    WoT3 = fm(ap["WoT"])
    w1T3 = fm(ap["w1T"])
    w2T3 = fm(ap["w2T"])
    wod3 = fm(wodram.ap())

    xslot_free = [None, None]
    tk_stats = []
    for j in range(NB):
        sl = j % 2
        xs = s_xblk[:, 2048 * sl:2048 * sl + 2048]
        lns = s_ln1[:, 2048 * sl:2048 * sl + 2048]
        tkx = dma("sync", xs.rearrange("p (c w) -> p c w", c=8),
                  xr3[:, :, 1 + BS * j:1 + BS * j + BS], waits=[xslot_free[sl]],
                  sem="dX%d" % sl)
        tkb = vop(lambda e, st, o=lns, i=xs: e.tensor_copy(out=o, in_=i), waits=[tkx])
        tkq = vop(lambda e, st, i=lns: e.tensor_mul(s_xsq[:, :], i, i),
                  waits=[tkb, tk_stats[-1] if tk_stats else None])
        tm1 = tm2 = None
        for dc in range(8):
            tm1 = mm(ps[0:1, 1024:1024 + 256], s_ones[:, 0:1],
                     lns[:, 256 * dc:256 * dc + 256], start=(dc == 0), stop=(dc == 7),
                     waits=[tkb, tk_ones], banks=(2,))
            tm2 = mm(ps[0:1, 1536:1536 + 256], s_ones[:, 0:1],
                     s_xsq[:, 256 * dc:256 * dc + 256], start=(dc == 0), stop=(dc == 7),
                     waits=[tkq], banks=(3,))
        t1 = vop(lambda e, st, j=j: e.tensor_scalar(
            out=s_mr[0:1, BS * j:BS * j + BS], in0=ps[0:1, 1024:1024 + 256],
            scalar1=1.0 / D, scalar2=None, op0=ALU.mult), waits=[tm1, tk_mrms])
        t2 = vop(lambda e, st, j=j: e.tensor_scalar(
            out=s_mr[0:1, W + BS * j:W + BS * j + BS], in0=ps[0:1, 1536:1536 + 256],
            scalar1=1.0 / D, scalar2=None, op0=ALU.mult), waits=[tm2])
        rel((2,), t1)
        rel((3,), t2)
        t3 = vop(lambda e, st, j=j: e.tensor_mul(
            s_row[0:1, 0:BS], s_mr[0:1, BS * j:BS * j + BS],
            s_mr[0:1, BS * j:BS * j + BS]), waits=[t1])
        t4 = vop(lambda e, st, j=j: e.tensor_sub(
            s_mr[0:1, W + BS * j:W + BS * j + BS],
            s_mr[0:1, W + BS * j:W + BS * j + BS], s_row[0:1, 0:BS]), waits=[t2, t3])
        t5 = vop(lambda e, st, j=j: e.tensor_scalar(
            out=s_mr[0:1, W + BS * j:W + BS * j + BS],
            in0=s_mr[0:1, W + BS * j:W + BS * j + BS],
            scalar1=EPS, scalar2=None, op0=ALU.add), waits=[t4])
        t6 = vop(lambda e, st, j=j: e.reciprocal(
            out=s_mr[0:1, W + BS * j:W + BS * j + BS],
            in_=s_mr[0:1, W + BS * j:W + BS * j + BS]), waits=[t5])
        tk_stats.append(t6)
        xslot_free[sl] = tkb
    tk_rstd = sop(lambda e, st: e.activation(
        out=s_mr[0:1, W:W + T], in_=s_mr[0:1, W:W + T], func=AF.Sqrt),
        waits=[tk_stats[-1]])

    if "stats" in tapap:
        dma("sync", tapap["stats"][0:1, :], s_mr[0:1, :], waits=[tk_rstd], sem="dTP")

    wst_free = [None, None]
    slot_ctr = [0]
    qstate = {}

    def sub_chunks(width):
        return [(a, min(2048, width - a)) for a in range(0, width, 2048)]

    def j_cast(src_sl, dst_sl, toks, key):
        def go():
            sl = slot_ctr[0] & 1
            slot_ctr[0] += 1
            n = src_sl.shape[-1]
            tkd = dma("sync", s_wst[:, 2048 * sl:2048 * sl + n], src_sl,
                      waits=[wst_free[sl]], sem="dW%d" % sl)
            tkv = vop(lambda e, st, s=sl, d=dst_sl, nn=n: e.tensor_copy(
                out=d, in_=s_wst[:, 2048 * s:2048 * s + nn]), waits=[tkd])
            wst_free[sl] = tkv
            toks[key] = tkv
        return go

    def j_q1(src_sl, key, last, nel):
        def go():
            sl = slot_ctr[0] & 1
            slot_ctr[0] += 1
            n = src_sl.shape[-1]
            tkd = dma("sync", s_wst[:, 2048 * sl:2048 * sl + n], src_sl,
                      waits=[wst_free[sl]], sem="dW%d" % sl)
            tkr = vop(lambda e, st, s=sl, nn=n: e.reduce_sum(
                out=s_ws[:, 3:4], in_=s_wst[:, 2048 * s:2048 * s + nn],
                axis=AX.X, apply_absolute_value=True),
                waits=[tkd, qstate.get((key, "redfree"))])
            if (key, "acc") not in qstate:
                ta = vop(lambda e, st: e.tensor_copy(out=s_ws[:, 2:3], in_=s_ws[:, 3:4]),
                         waits=[tkr])
            else:
                ta = vop(lambda e, st: e.tensor_add(
                    s_ws[:, 2:3], s_ws[:, 2:3], s_ws[:, 3:4]),
                    waits=[tkr, qstate.get((key, "acc"))])
            qstate[(key, "acc")] = ta
            qstate[(key, "redfree")] = ta
            wst_free[sl] = tkr
            if last:
                tt = tr(ps[0:1, 1024:1152], s_ws[:, 2:3], s_id32[:, :],
                        waits=[ta, tk_id32], banks=(2,))
                tsum = vop(lambda e, st: e.reduce_sum(
                    out=s_scl[0:1, 1036:1037], in_=ps[0:1, 1024:1152], axis=AX.X),
                    waits=[tt, qstate.get("rowfree")])
                rel((2,), tsum)
                mcol = 1032 if key == "w1" else 1033
                tmean = vop(lambda e, st, mc=mcol, ne=nel: e.tensor_scalar(
                    out=s_scl[0:1, mc:mc + 1], in0=s_scl[0:1, 1036:1037],
                    scalar1=1.0 / ne, scalar2=None, op0=ALU.mult), waits=[tsum])
                trec = vop(lambda e, st, mc=mcol: e.reciprocal(
                    out=s_scl[0:1, 1037:1038], in_=s_scl[0:1, mc:mc + 1]),
                    waits=[tmean])
                wcol = 0 if key == "w1" else 1
                tbm = mm(ps[:, 1024:1025], s_on32[0:1, 0:128], s_scl[0:1, 1037:1038],
                         True, True, waits=[trec, tk_on32], banks=(2,))
                tbv = vop(lambda e, st, wc=wcol: e.tensor_copy(
                    out=s_ws[:, wc:wc + 1], in_=ps[:, 1024:1025]), waits=[tbm])
                rel((2,), tbv)
                qstate[(key, "winv")] = tbv
                qstate[(key, "mean")] = tmean
                qstate["rowfree"] = trec
        return go

    def j_q2(src_sl, dst_sl, key, toks, tkey):
        def go():
            sl = slot_ctr[0] & 1
            slot_ctr[0] += 1
            n = src_sl.shape[-1]
            wcol = 0 if key == "w1" else 1
            tkd = dma("sync", s_wst[:, 2048 * sl:2048 * sl + n], src_sl,
                      waits=[wst_free[sl]], sem="dW%d" % sl)
            ta = vop(lambda e, st, s=sl, nn=n, wc=wcol: e.tensor_scalar(
                out=s_wst[:, 2048 * s:2048 * s + nn],
                in0=s_wst[:, 2048 * s:2048 * s + nn],
                scalar1=s_ws[:, wc:wc + 1], scalar2=None, op0=ALU.mult),
                waits=[tkd, qstate[(key, "winv")]])
            tc8 = vop(lambda e, st, s=sl, nn=n: e.tensor_scalar(
                out=s_w8t[:, 0:nn], in0=s_wst[:, 2048 * s:2048 * s + nn],
                scalar1=-1.0, scalar2=1.0, op0=ALU.max, op1=ALU.min),
                waits=[ta, qstate.get("w8free")])
            td = vop(lambda e, st, d=dst_sl, nn=n: e.tensor_copy(
                out=d, in_=s_w8t[:, 0:nn]), waits=[tc8])
            qstate["w8free"] = td
            wst_free[sl] = tc8
            toks[tkey] = td
        return go

    jobsA = []
    tk_wq = {}
    for kc in range(8):
        for (a, n) in sub_chunks(3072):
            jobsA.append(j_cast(WqT3[:, kc, a:a + n],
                                s_wq[:, 3072 * kc + a:3072 * kc + a + n], tk_wq,
                                (kc, a)))
    w1subs = [(kc, a, n) for kc in range(8) for (a, n) in sub_chunks(4096)]
    for i, (kc, a, n) in enumerate(w1subs):
        jobsA.append(j_q1(w1T3[:, kc, a:a + n], "w1", i == len(w1subs) - 1, D * H))
    for i, kc in enumerate(range(32)):
        jobsA.append(j_q1(w2T3[:, kc, :], "w2", i == 31, D * H))

    jobsB0 = []
    tk_wocast = {}
    for kc in range(8):
        def mk(kc=kc):
            def go():
                sl = slot_ctr[0] & 1
                slot_ctr[0] += 1
                tkd = dma("sync", s_wst[:, 2048 * sl:2048 * sl + 1024],
                          WoT3[:, kc, :], waits=[wst_free[sl]], sem="dW%d" % sl)
                wsl = kc % 2
                tkv = vop(lambda e, st, s=sl, ws=wsl: e.tensor_copy(
                    out=s_wos[:, 1024 * ws:1024 * ws + 1024],
                    in_=s_wst[:, 2048 * s:2048 * s + 1024]),
                    waits=[tkd, qstate.get(("wos", wsl))])
                tks = dma("sync", wod3[:, kc, :],
                          s_wos[:, 1024 * wsl:1024 * wsl + 1024], waits=[tkv],
                          sem="dWD%d" % wsl)
                qstate[("wos", wsl)] = tks
                wst_free[sl] = tkv
                tk_wocast[kc] = tks
            return go
        jobsB0.append(mk())
    tk_w1q = {}
    for kc in range(8):
        for (a, n) in sub_chunks(4096):
            jobsB0.append(j_q2(w1T3[:, kc, a:a + n],
                               s_w1q[:, 4096 * kc + a:4096 * kc + a + n], "w1",
                               tk_w1q, kc))
    jobsB1 = []
    tk_w2q = {}
    for kc in range(32):
        jobsB1.append(j_q2(w2T3[:, kc, :], s_w2q[:, 1024 * kc:1024 * kc + 1024], "w2",
                           tk_w2q, kc))

    for jb in jobsA[:16]:
        jb()
    jobsA = jobsA[16:]
    nh = min(n_heads, NB)
    per_block = (len(jobsA) + nh - 1) // max(nh, 1)

    lnslot_free = [None, None]
    tk_Y_free = None
    zloc_last = None
    NS = {"prev_av": [None, None], "av_last": [None, None], "zn_free": None,
          "rb_free": None, "zr_free": [None, None], "ot_free": [None, None],
          "pend_norm": None, "zloc_last": None}

    def flush_norm():
        qcn, jn, pend = NS["pend_norm"]
        pmn = 64 * (jn // 4)
        zchn = jn % 4
        for (par, tevO, tevZ) in pend:
            obank = 4 + 2 * par
            zo = 1024 * par
            trb1 = mm(ps[0:64, 512 * obank:512 * obank + 512], s_on32[64:65, 0:64],
                      OT[64:65, zo:zo + 512], True, True, waits=[tevZ, tk_on32],
                      banks=(obank,))
            trb2 = mm(ps[0:64, 512 * (obank + 1):512 * (obank + 1) + 512],
                      s_on32[64:65, 0:64], OT[64:65, zo + 512:zo + 1024], True, True,
                      waits=[], banks=(obank + 1,))
            trbe = vop(lambda e, st, b=obank: e.tensor_copy(
                out=s_zr[0:64, 0:1024], in_=ps[0:64, 512 * b:512 * b + 1024]),
                waits=[trb1, trb2, NS["rb_free"]])
            rel((obank, obank + 1), trbe)
            NS["zr_free"][par] = trbe
            trec = vop(lambda e, st: e.reciprocal(
                out=s_zr[0:64, 0:1024], in_=s_zr[0:64, 0:1024]), waits=[trbe])
            tdiv = vop(lambda e, st, pp=par: e.tensor_mul(
                s_zn[0:64, 0:1024], OT[0:64, 1024 * pp:1024 * pp + 1024],
                s_zr[0:64, 0:1024]), waits=[trec, tevO, NS["zn_free"]])
            NS["rb_free"] = tdiv
            NS["ot_free"][par] = tdiv
            if jn < 4:
                srcz = s_zn[0:64, :]
                tk_shift = tdiv
            else:
                tk_shift = dma("gpsimd", s_zn[64:128, 0:1024], s_zn[0:64, 0:1024],
                               waits=[tdiv, NS["zn_free"]], sem="dZN")
                srcz = s_zn[64:128, :]
            evs = []
            for i in range(4):
                g = par + 2 * (4 * qcn + i)
                st0 = zchn * WZ + 1 + g
                evs.append(vop(lambda e, st, ii=i, s0=st0, sz=srcz, pb=pmn:
                    e.tensor_copy(
                        out=s_z[pb:pb + 64, s0:s0 + 16 * 255 + 1:16],
                        in_=sz[:, 256 * ii:256 * ii + 256]),
                    waits=[tk_shift, tk_zpad]))
            NS["zn_free"] = evs[-1]
            NS["zloc_last"] = evs[-1]
        NS["pend_norm"] = None

    for j in range(nh):
        sl = j % 2
        xs = s_xblk[:, 2048 * sl:2048 * sl + 2048]
        lns = s_ln1[:, 2048 * sl:2048 * sl + 2048]
        tkx = dma("sync", xs.rearrange("p (c w) -> p c w", c=8),
                  xr3[:, :, 1 + BS * j:1 + BS * j + BS], waits=[xslot_free[sl]],
                  sem="dX%d" % sl)
        tva = bcast_sb(s_mr[0:1, BS * j:BS * j + BS], BS, 2, s_abc[:, 0:256],
                       [tk_rstd, tk_Y_free])
        tvb = bcast_sb(s_mr[0:1, W + BS * j:W + BS * j + BS], BS, 3,
                       s_abc[:, 256:512], [tk_rstd])
        tk_norm = []
        tprev2 = None
        for dc in range(8):
            tn1 = vop(lambda e, st, d=dc, x=xs: e.tensor_sub(
                s_atmp[:, 0:256], x[:, 256 * d:256 * d + 256], s_abc[:, 0:256]),
                waits=[tva, tkx, lnslot_free[sl], tprev2])
            tn2 = vop(lambda e, st, d=dc, l=lns: e.tensor_mul(
                l[:, 256 * d:256 * d + 256], s_atmp[:, 0:256], s_abc[:, 256:512]),
                waits=[tvb, tn1])
            tprev2 = tn2
            tk_norm.append(tn2)
        xslot_free[sl] = tk_norm[-1]
        if "ln1" in tapap and j == 0:
            tk_norm[-1] = fence_v(dma("sync", tapap["ln1"][:, :], lns[:, :],
                                      waits=[tk_norm[-1]], sem="dTP"))

        tk_evY = [None, None]
        for cc in range(24):
            bk = cc % 2
            tmm = None
            for dc in range(8):
                tmm = mm(bank(bk, 256),
                         s_wq[:, 3072 * dc + 128 * cc:3072 * dc + 128 * cc + 128],
                         lns[:, 256 * dc:256 * dc + 256], start=(dc == 0), stop=(dc == 7),
                         waits=[tk_norm[dc], tk_wq.get((dc, 0)), tk_wq.get((dc, 2048)),
                                tk_Y_free], banks=(bk,))
            tev = vop(lambda e, st, c=cc, b=bk: e.tensor_scalar(
                out=s_Y[:, 256 * c:256 * c + 256], in0=bank(b, 256),
                scalar1=s_bq[:, c:c + 1], scalar2=None, op0=ALU.add),
                waits=[tmm, tk_consts])
            rel((bk,), tev)
            tk_evY[bk] = tev
        tk_Yall = [tk_evY[0], tk_evY[1]]
        lnslot_free[sl] = tk_Yall[1]
        if "Y" in tapap and j == 0:
            tk_Yall = [fence_v(dma("sync", tapap["Y"][:, :], s_Y[:, :],
                                   waits=tk_Yall, sem="dTP"))]

        dma("gpsimd", s_kdup[0:64, :], s_Y[64:128, 2048:4096],
            waits=tk_Yall + [tk_Y_free], sem="dKD")
        tk_kd1 = tk_kd2 = dma("gpsimd", s_qd[0:64, :], s_Y[64:128, 0:2048],
                              waits=tk_Yall, sem="dKD")

        tk_vt = []
        tk_vtev = None
        for grp in range(8):
            tts = []
            for t4 in range(4):
                kc = 4 * grp + t4
                g, hf = kc // 2, kc % 2
                pb = 64 * (g % 2)
                src = s_Y[pb:pb + 64, 4096 + 256 * (g // 2) + 128 * hf:
                          4096 + 256 * (g // 2) + 128 * hf + 128]
                idt = s_ident[pb:pb + 64, pb:pb + 64]
                dst = ps[:, 512 * (4 + t4):512 * (4 + t4) + 32].bitcast(F16)
                tts.append(tr(dst, src, idt,
                              waits=tk_Yall + [tk_id, tk_Y_free], banks=(4 + t4,)))
            tk_vtev = vop(lambda e, st, gg=grp: e.tensor_copy(
                out=s_vt[:, 65 * 4 * gg:65 * 4 * gg + 65 * 4]
                    .rearrange("p (k c) -> p k c", k=4)[:, :, 0:64],
                in_=ps[:, 2048:4096].bitcast(F16)
                    .rearrange("p (k c) -> p k c", k=4)[:, :, 0:64]),
                waits=tts + [tk_vtones, tk_Y_free])
            rel((4, 5, 6, 7), tk_vtev)
            tk_vt.append(tk_vtev)

        pm = 64 * (j // 4)
        zch = j % 4
        texp_of = {}

        def emit_av(kc, texp_of=texp_of, tk_vt=tk_vt):
            slot = kc % 2
            for par in range(2):
                obank = 4 + 2 * par
                tav = None
                for half in range(2):
                    tav = mm(ps[0:65, 512 * obank + 512 * half:
                                 512 * obank + 512 * half + 512],
                             s_vt[:, 65 * kc:65 * kc + 65],
                             s_E[:, 2048 * slot + 1024 * par + 512 * half:
                                 2048 * slot + 1024 * par + 512 * half + 512],
                             start=(kc == 0), stop=(kc == 31),
                             waits=[texp_of[kc], tk_vt[kc // 4]],
                             banks=(obank, obank + 1))
                NS["av_last"][par] = tav
            NS["prev_av"][slot] = tav

        for qc in range(2):
            tk_O = [None, None]
            for kc in range(32):
                g_k, hf = kc // 2, kc % 2
                slot = kc % 2
                if (g_k % 2) == 0:
                    klhs = s_Y[0:64, 2048 + 256 * (g_k // 2) + 128 * hf:
                               2048 + 256 * (g_k // 2) + 128 * hf + 128]
                    kw = tk_Yall
                else:
                    klhs = s_kdup[0:64, 256 * (g_k // 2) + 128 * hf:
                                  256 * (g_k // 2) + 128 * hf + 128]
                    kw = [tk_kd1]
                ts4 = []
                for par in range(2):
                    if par == 0:
                        qrhs = s_Y[0:64, 1024 * qc:1024 * qc + 1024]
                        qw = tk_Yall
                    else:
                        qrhs = s_qd[0:64, 1024 * qc:1024 * qc + 1024]
                        qw = [tk_kd2]
                    sbk = 2 * par
                    ts4.append(mm(bank(sbk), klhs, qrhs[:, 0:512], True, True,
                                  waits=list(kw) + list(qw), banks=(sbk,)))
                    ts4.append(mm(bank(sbk + 1), klhs, qrhs[:, 512:1024], True, True,
                                  waits=[], banks=(sbk + 1,)))
                texp = sop(lambda e, st, sl2=slot: e.activation(
                    out=s_E[:, 2048 * sl2:2048 * sl2 + 2048],
                    in_=ps[:, 0:2048], func=AF.Exp,
                    scale=s_ws[:, 5:6], bias=s_ws[:, 4:5]),
                    waits=ts4 + [NS["prev_av"][slot], tk_expc])
                rel((0, 1, 2, 3), texp)
                texp_of[kc] = texp
                if kc == 1 and NS["pend_norm"]:
                    flush_norm()
                if kc > 0:
                    emit_av(kc - 1)
            emit_av(31)
            for par in range(2):
                tk_O[par] = NS["av_last"][par]
            pend = []
            for par in range(2):
                obank = 4 + 2 * par
                tevO = vop(lambda e, st, b=obank, pp=par: e.tensor_copy(
                    out=OT[0:64, 1024 * pp:1024 * pp + 1024],
                    in_=ps[0:64, 512 * b:512 * b + 1024]),
                    waits=[tk_O[par], NS["ot_free"][par]])
                tevZ = vop(lambda e, st, b=obank, pp=par: e.tensor_copy(
                    out=OT[64:65, 1024 * pp:1024 * pp + 1024],
                    in_=ps[64:65, 512 * b:512 * b + 1024]),
                    waits=[tk_O[par], NS["zr_free"][par]])
                rel((obank, obank + 1), tevZ)
                pend.append((par, tevO, tevZ))
            NS["pend_norm"] = (qc, j, pend)
            if qc == 1 and j == nh - 1:
                flush_norm()
        zloc_last = NS["zloc_last"]
        tk_Y_free = NS["av_last"][1]

        for _ in range(per_block):
            if jobsA:
                jobsA.pop(0)()
    while jobsA:
        jobsA.pop(0)()

    tk_zst = []
    for ch in range(4):
        tk_zst.append(dma("gpsimd", zin[128 * ch:128 * ch + 128, :],
                          s_z[:, ch * WZ:ch * WZ + WZ], waits=[zloc_last], sem="dZI"))
    if "zin" in tapap:
        for ch in range(4):
            tkzt = dma("sync", tapap["zin"][128 * ch:128 * ch + 128, :],
                       s_z[:, ch * WZ:ch * WZ + WZ], waits=[zloc_last], sem="dTP")
        zloc_last = fence_v(tkzt)

    if stop_after == "A":
        ms = vop(lambda e, st: e.memset(s_out[:, 0:512], 0.0), waits=[zloc_last])
        tko = dma("sync", ap["out"][0:128, 0:512], s_out[:, 0:512], waits=[ms], sem="dO0")
        p.emit("sync", lambda e, st: e.nop(),
               waits=[tko] + [p.tok(s) for s in ["dC","dX0","dX1","dTP","dW0","dW1","dWD0","dWD1","dKD","dZN","dZI","dZB","dWO0","dWO1","dXR0","dXR1","dO0","dO1","tT","tV","tS","tG","CC"]])
        return _finish(nc, ctx, p)

    tk_cc = p.emit("gpsimd", lambda e, st: e.collective_compute(
        "AllGather", ALU.bypass, replica_groups=groups,
        ins=[zin.ap().opt()], outs=[zgath.ap().opt()]),
        waits=tk_zst, inc="CC", amt=1)

    for jb in jobsB0:
        jb()

    tk_pid = p.emit("gpsimd", lambda e, st: (
        st.__setitem__("T0", e.compute_val((e.partition_id() % 2) * T)), e.nop())[-1],
        waits=[], inc="tG")
    tk_zb = []
    for zc in range(8):
        tk_zb.append(p.emit("gpsimd", lambda e, st, z=zc: e.dma_start(
            out=s_z[:, 2050 * z:2050 * z + 2050],
            in_=zgath[128 * z:128 * z + 128, bass.ds(st["T0"], 2050)]),
            waits=[tk_cc, tk_pid, zloc_last], inc="dZB", amt=16))

    wos_free = [qstate.get(("wos", 0)), qstate.get(("wos", 1))]
    tk_woL = {}

    def wo_load(oc):
        wsl = oc % 2
        tkd = dma("gpsimd", s_wos[:, 1024 * wsl:1024 * wsl + 1024], wod3[:, oc, :],
                  waits=[wos_free[wsl], tk_wocast.get(oc), zloc_last],
                  sem="dWO%d" % wsl)
        tk_woL[oc] = (tkd, wsl)
        return tkd, wsl

    xr_free = [None, None]
    for jb in jobsB1:
        jb()

    if stop_after == "B0":
        p.emit("sync", lambda e, st: e.nop(),
               waits=[p.tok(s) for s in ["dC","dX0","dX1","dTP","dW0","dW1","dWD0","dWD1","dKD","dZN","dZI","dZB","dWO0","dWO1","dXR0","dXR1","dO0","dO1","tT","tV","tS","tG","CC"]])
        return _finish(nc, ctx, p)

    def crosspart_max(waits):
        tt1 = tr(ps[:, 512 * 6:512 * 6 + 128], s_red[:, 0:128], s_id32[:, :],
                 waits=list(waits) + [tk_id32], banks=(6,))
        tt2 = tr(ps[:, 512 * 7:512 * 7 + 128], s_red[:, 128:256], s_id32[:, :],
                 waits=list(waits), banks=(7,))
        tt3 = tr(ps[0:2, 512 * 5:512 * 5 + 128], s_red[:, 256:258], s_id32[:, :],
                 waits=list(waits), banks=(5,))
        r1 = vop(lambda e, st: e.reduce_max(out=s_red[:, 520:521],
                 in_=ps[:, 512 * 6:512 * 6 + 128], axis=AX.X), waits=[tt1])
        r2 = vop(lambda e, st: e.reduce_max(out=s_red[:, 521:522],
                 in_=ps[:, 512 * 7:512 * 7 + 128], axis=AX.X), waits=[tt2])
        r3 = vop(lambda e, st: e.reduce_max(out=s_red[0:2, 522:523],
                 in_=ps[0:2, 512 * 5:512 * 5 + 128], axis=AX.X), waits=[tt3])
        rel((6,), r1)
        rel((7,), r2)
        rel((5,), r3)
        m1 = tr(ps[0:1, 512 * 6:512 * 6 + 128], s_red[:, 520:521], s_id32[:, :],
                waits=[r1], banks=(6,))
        m2 = tr(ps[0:1, 512 * 7:512 * 7 + 128], s_red[:, 521:522], s_id32[:, :],
                waits=[r2], banks=(7,))
        m3 = tr(ps[0:1, 512 * 5:512 * 5 + 2], s_red[0:2, 522:523], s_id32[0:2, 0:2],
                waits=[r3], banks=(5,))
        c1 = vop(lambda e, st: e.tensor_copy(
            out=s_scl[0:1, 0:128], in_=ps[0:1, 512 * 6:512 * 6 + 128]), waits=[m1])
        c2 = vop(lambda e, st: e.tensor_copy(
            out=s_scl[0:1, 128:256], in_=ps[0:1, 512 * 7:512 * 7 + 128]), waits=[m2])
        c3 = vop(lambda e, st: e.tensor_copy(
            out=s_scl[0:1, 256:258], in_=ps[0:1, 512 * 5:512 * 5 + 2]), waits=[m3])
        rel((6,), c1)
        rel((7,), c2)
        rel((5,), c3)
        return [c1, c2, c3], [m1, m2, m3]

    tk_lt_rd = None
    st_guard = [None, None]
    row_guard = qstate.get("rowfree")
    tk_red_guard = []
    tk_scl_guard = [qstate.get("rowfree")]
    tk_xq8_rd = [tk_w2q.get(31)]
    tk_xqb_rd = None
    tk_hy_wr_ok = tk_w2q.get(31)
    tk_x1w_rd = None
    tk_out_dma = [None, None]
    tk_out_done = []
    prev_te1 = None
    for tc in range(8):
        j0 = 256 * tc
        tk_x1w = []
        tk_wm = {}
        for zc in range(8):
            tkwd, wsl = wo_load(zc)
            for oc in range(8):
                tk_wm[oc] = mm(ps[:, 512 * oc:512 * oc + 258],
                               s_wos[:, 1024 * wsl + 128 * oc:1024 * wsl + 128 * oc + 128],
                               s_z[:, 2050 * zc + j0:2050 * zc + j0 + 258],
                               start=(zc == 0), stop=(zc == 7),
                               waits=[tkwd, tk_zb[7]], banks=(oc,))
            wos_free[wsl] = tk_wm[7]
        for oc in range(8):
            sl = oc % 2
            tkxr = dma("sync", s_xr[:, 512 * sl:512 * sl + 258],
                       xr3[:, oc, j0:j0 + 258], waits=[xr_free[sl]], sem="dXR%d" % sl)
            tstt = vop(lambda e, st, o=oc: e.scalar_tensor_tensor(
                out=s_x1w[:, 258 * o:258 * o + 258], in0=ps[:, 512 * o:512 * o + 258],
                scalar=s_bo[:, o:o + 1], in1=s_xr[:, 512 * (o % 2):512 * (o % 2) + 258],
                op0=ALU.add, op1=ALU.add),
                waits=[tk_wm[oc], tkxr, tk_consts, tk_x1w_rd])
            rel((oc,), tstt)
            xr_free[sl] = tstt
            tk_x1w.append(tstt)
        if "x1w" in tapap and tc == 0:
            tk_x1w = [fence_v(dma("sync", tapap["x1w"][:, :], s_x1w[:, :],
                                  waits=tk_x1w, sem="dTP"))] * 8
        tm1 = tm2 = None
        tk_c1rd = [st_guard[0]]
        tk_c2rd = [st_guard[1]]
        for dc in range(8):
            tcst = vop(lambda e, st, d=dc: e.tensor_copy(
                out=s_cast1[:, 0:258], in_=s_x1w[:, 258 * d:258 * d + 258]),
                waits=[tk_x1w[dc]] + tk_c1rd)
            tsq = vop(lambda e, st: e.tensor_mul(
                s_cast2[:, 0:258], s_cast1[:, 0:258], s_cast1[:, 0:258]),
                waits=[tcst] + tk_c2rd)
            tm1 = mm(ps[0:1, 1024:1024 + 258], s_ones[:, 0:1], s_cast1[:, 0:258],
                     start=(dc == 0), stop=(dc == 7), waits=[tcst, tk_ones],
                     banks=(2,))
            tm2 = mm(ps[0:1, 1536:1536 + 258], s_ones[:, 0:1], s_cast2[:, 0:258],
                     start=(dc == 0), stop=(dc == 7), waits=[tsq], banks=(3,))
            tk_c1rd, tk_c2rd = [tsq, tm1], [tm2]
        st_guard = [tm1, tm2]
        u1 = vop(lambda e, st, a=j0: e.tensor_scalar(
            out=s_mr[0:1, a:a + 258], in0=ps[0:1, 1024:1024 + 258],
            scalar1=1.0 / D, scalar2=None, op0=ALU.mult), waits=[tm1, tk_mrms])
        u2 = vop(lambda e, st, a=j0: e.tensor_scalar(
            out=s_mr[0:1, W + a:W + a + 258], in0=ps[0:1, 1536:1536 + 258],
            scalar1=1.0 / D, scalar2=None, op0=ALU.mult), waits=[tm2])
        rel((2,), u1)
        rel((3,), u2)
        u3 = vop(lambda e, st, a=j0: e.tensor_mul(
            s_row[0:1, 0:258], s_mr[0:1, a:a + 258], s_mr[0:1, a:a + 258]),
            waits=[u1, row_guard])
        u4 = vop(lambda e, st, a=j0: e.tensor_sub(
            s_mr[0:1, W + a:W + a + 258], s_mr[0:1, W + a:W + a + 258],
            s_row[0:1, 0:258]), waits=[u2, u3])
        u5 = vop(lambda e, st, a=j0: e.tensor_scalar(
            out=s_mr[0:1, W + a:W + a + 258], in0=s_mr[0:1, W + a:W + a + 258],
            scalar1=EPS, scalar2=None, op0=ALU.add), waits=[u4])
        u6 = vop(lambda e, st, a=j0: e.reciprocal(
            out=s_mr[0:1, W + a:W + a + 258], in_=s_mr[0:1, W + a:W + a + 258]),
            waits=[u5])
        row_guard = u3
        tk_rstd2 = sop(lambda e, st, a=j0: e.activation(
            out=s_mr[0:1, W + a:W + a + 258], in_=s_mr[0:1, W + a:W + a + 258],
            func=AF.Sqrt), waits=[u6, tk_rstd])
        tmb = bcast_ps(s_mr[0:1, j0:j0 + 258], 258, 4, [tk_rstd2, u1])
        tk_sub = []
        for dc in range(8):
            ts_ = vop(lambda e, st, d=dc: e.tensor_sub(
                s_lt[:, 258 * d:258 * d + 258], s_x1w[:, 258 * d:258 * d + 258],
                bank(4, 258)), waits=[tmb, tk_lt_rd, tk_x1w[dc]])
            tk_sub.append(ts_)
        rel((4,), tk_sub[-1])
        tb_ = vop(lambda e, st: e.scalar_tensor_tensor(
            out=s_red[:, 0:258], in0=s_lt[:, 0:258], scalar=-1.0,
            in1=s_lt[:, 0:258], op0=ALU.mult, op1=ALU.max),
            waits=[tk_sub[0]] + tk_red_guard)
        for dci in range(1, 8):
            t_a = vop(lambda e, st, d=dci: e.scalar_tensor_tensor(
                out=s_red[:, 258:516], in0=s_lt[:, 258 * d:258 * d + 258],
                scalar=-1.0, in1=s_lt[:, 258 * d:258 * d + 258],
                op0=ALU.mult, op1=ALU.max),
                waits=[tk_sub[dci], tb_])
            tb_ = vop(lambda e, st: e.tensor_tensor(out=s_red[:, 0:258],
                      in0=s_red[:, 0:258], in1=s_red[:, 258:516], op=ALU.max),
                      waits=[t_a])
        cps, mns = crosspart_max([tb_] + tk_scl_guard)
        q1 = vop(lambda e, st, jj=j0: e.tensor_mul(s_scl[0:1, 258:516],
                 s_scl[0:1, 0:258], s_mr[0:1, W + jj:W + jj + 258]),
                 waits=cps + [tk_rstd2])
        q2 = vop(lambda e, st: e.tensor_scalar(out=s_scl[0:1, 774:1032],
                 in0=s_scl[0:1, 258:516], scalar1=1e-5, scalar2=None, op0=ALU.max),
                 waits=[q1])
        q3 = vop(lambda e, st: e.reciprocal(out=s_scl[0:1, 258:516],
                 in_=s_scl[0:1, 774:1032]), waits=[q2])
        q4 = vop(lambda e, st, jj=j0: e.tensor_mul(s_scl[0:1, 258:516],
                 s_scl[0:1, 258:516], s_mr[0:1, W + jj:W + jj + 258]), waits=[q3])
        q5 = vop(lambda e, st: e.tensor_scalar(out=s_scl[0:1, 258:516],
                 in0=s_scl[0:1, 258:516], scalar1=127.0, scalar2=None, op0=ALU.mult),
                 waits=[q4])
        q6 = vop(lambda e, st: e.tensor_scalar(out=s_scl[0:1, 516:774],
                 in0=s_scl[0:1, 774:1032], scalar1=s_scl[0:1, 1032:1033],
                 scalar2=1.0 / 127.0, op0=ALU.mult, op1=ALU.mult),
                 waits=[q2, qstate[("w1", "mean")]])
        if tc == 0:
            q6 = vop(lambda e, st: e.tensor_mul(s_scl[0:1, 516:774], s_scl[0:1, 516:774],
                     s_maskl[0:1, :]), waits=[q6, tk_consts])
        if tc == 7:
            q6 = vop(lambda e, st: e.tensor_mul(s_scl[0:1, 516:774], s_scl[0:1, 516:774],
                     s_maskr[0:1, :]), waits=[q6, tk_consts])
        tcb = bcast_ps(s_scl[0:1, 258:516], 258, 4, [q5])
        tsbv = bcast_sb(s_scl[0:1, 516:774], 258, 5, s_bc[:, 0:258], [q6])
        tq8 = []
        for dc in range(8):
            tq8.append(vop(lambda e, st, d=dc: e.tensor_mul(
                s_xq8[:, 258 * d:258 * d + 258], s_lt[:, 258 * d:258 * d + 258],
                bank(4, 258)), waits=[tcb, tk_sub[dc]] + tk_xq8_rd))
        tk_xq8_rd = []
        rel((4,), tq8[-1])
        tqb = vop(lambda e, st: e.tensor_copy(out=s_xqb[:, :], in_=s_xq8[:, :]),
                  waits=tq8 + [tk_xqb_rd])
        if "w1q" in tapap and tc == 0:
            tqb = fence_v(dma("sync", tapap["w1q"][:, :], s_w1q[:, :],
                              waits=[tqb, tk_w1q.get(7)], sem="dTP"))
        tk_lt_rd = tq8[-1]
        if "xq" in tapap and tc == 0:
            tqb = fence_v(dma("sync", tapap["xq"][:, :], s_xqb[:, :], waits=[tqb],
                              sem="dTP"))
        tk_z2 = []
        prev_tt1 = None
        prev_tt2 = None
        prev_silu = None
        for hc in range(32):
            bk = hc % 2
            tmm = None
            for dc in range(8):
                tmm = mm(bank(bk, 258),
                         s_w1q[:, 4096 * dc + 128 * hc:4096 * dc + 128 * hc + 128],
                         s_xqb[:, 258 * dc:258 * dc + 258], start=(dc == 0),
                         stop=(dc == 7), waits=[tqb, tk_w1q.get(dc)], banks=(bk,))
            tev = vop(lambda e, st, h=hc, b=bk: e.tensor_mul(
                s_hy[:, 258 * h:258 * h + 258], bank(b, 258),
                s_bc[:, 0:258]), waits=[tmm, tsbv, tk_hy_wr_ok, NS["rb_free"]])
            rel((bk,), tev)
            if "hpre" in tapap and tc == 0:
                tev = fence_v(dma("sync", tapap["hpre"][:, 258 * hc:258 * hc + 258],
                                  s_hy[:, 258 * hc:258 * hc + 258], waits=[tev],
                                  sem="dTP"))
            tt0 = sop(lambda e, st, h=hc: e.activation(
                out=s_tc0[:, 0:256], in_=s_hy[:, 258 * h:258 * h + 256],
                func=AF.Identity, bias=s_cb[:, h:h + 1], scale=s_cw[:, h:h + 1]),
                waits=[tev, tk_consts, prev_tt1])
            tt1 = vop(lambda e, st, h=hc: e.scalar_tensor_tensor(
                out=s_tc0[:, 258:258 + 256],
                in0=s_hy[:, 258 * h + 1:258 * h + 257],
                scalar=s_cw[:, 32 + h:32 + h + 1], in1=s_tc0[:, 0:256],
                op0=ALU.mult, op1=ALU.add), waits=[tt0, tev, prev_tt2])
            tt2 = vop(lambda e, st, h=hc: e.scalar_tensor_tensor(
                out=s_tcb[:, 0:256],
                in0=s_hy[:, 258 * h + 2:258 * h + 258],
                scalar=s_cw[:, 64 + h:64 + h + 1], in1=s_tc0[:, 258:258 + 256],
                op0=ALU.mult, op1=ALU.add), waits=[tt1, prev_silu])
            tt3 = sop(lambda e, st, h=hc: e.activation(
                out=s_hy[:, 258 * h:258 * h + 256],
                in_=s_tcb[:, 0:256], func=AF.Silu), waits=[tt2])
            prev_tt1, prev_tt2, prev_silu = tt1, tt2, tt3
            tk_z2.append(tt3)
        if "h" in tapap and tc == 0:
            tk_z2 = [fence_v(dma("sync", tapap["h"][:, :], s_hy[:, :], waits=tk_z2,
                                 sem="dTP"))] * 32
        ab = vop(lambda e, st: e.scalar_tensor_tensor(
            out=s_red[:, 0:256], in0=s_hy[:, 0:256], scalar=-1.0,
            in1=s_hy[:, 0:256], op0=ALU.mult, op1=ALU.max),
            waits=[tk_z2[0]] + mns)
        for hc in range(1, 32):
            t_a = vop(lambda e, st, h=hc: e.scalar_tensor_tensor(
                out=s_red[:, 258:258 + 256], in0=s_hy[:, 258 * h:258 * h + 256],
                scalar=-1.0, in1=s_hy[:, 258 * h:258 * h + 256],
                op0=ALU.mult, op1=ALU.max),
                waits=[tk_z2[hc], ab])
            ab = vop(lambda e, st: e.tensor_tensor(out=s_red[:, 0:256],
                     in0=s_red[:, 0:256], in1=s_red[:, 258:258 + 256], op=ALU.max),
                     waits=[t_a])
        ab2 = vop(lambda e, st: e.memset(s_red[:, 256:258], 0.0), waits=[ab, cps[2]])
        cps2, mns2 = crosspart_max([ab, ab2] + cps)
        r1 = vop(lambda e, st: e.tensor_scalar(out=s_scl[0:1, 774:1032],
                 in0=s_scl[0:1, 0:258], scalar1=1e-5, scalar2=None, op0=ALU.max),
                 waits=cps2 + [q6])
        r2 = vop(lambda e, st: e.reciprocal(out=s_scl[0:1, 0:258],
                 in_=s_scl[0:1, 774:1032]), waits=[r1])
        r3 = vop(lambda e, st: e.tensor_scalar(out=s_scl[0:1, 0:258],
                 in0=s_scl[0:1, 0:258], scalar1=127.0, scalar2=None, op0=ALU.mult),
                 waits=[r2])
        r4 = vop(lambda e, st: e.tensor_scalar(out=s_scl[0:1, 516:774],
                 in0=s_scl[0:1, 774:1032], scalar1=s_scl[0:1, 1033:1034],
                 scalar2=1.0 / 127.0, op0=ALU.mult, op1=ALU.mult),
                 waits=[r1, qstate[("w2", "mean")], tsbv])
        ts2v = bcast_sb(s_scl[0:1, 0:256], 256, 4, s_bc[:, 516:516 + 256],
                        [r3, tcb, tq8[-1]])
        tg2v = bcast_sb(s_scl[0:1, 516:516 + 256], 256, 5, s_bc[:, 258:258 + 256],
                        [r4, tsbv])
        tk_scl_guard = [ts2v, tg2v]
        tcv_prev = [None, None]
        tmm2 = {oc: None for oc in range(8)}
        last_mm = None
        ty8_last = None
        for hc in range(32):
            sl2 = hc % 2
            ty8 = vop(lambda e, st, h=hc, s=sl2: e.tensor_mul(
                s_y8r[:, 260 * s:260 * s + 256],
                s_hy[:, 258 * h:258 * h + 256], s_bc[:, 516:516 + 256]),
                waits=[ts2v, tk_z2[hc], tcv_prev[sl2], tk_y8ms])
            tcv = vop(lambda e, st, s=sl2: e.tensor_copy(
                out=s_ybr[:, 260 * s:260 * s + 256],
                in_=s_y8r[:, 260 * s:260 * s + 256]),
                waits=[ty8, (last_mm if hc >= 2 else None)])
            for oc in range(8):
                tmm2[oc] = mm(bank(oc, 256),
                              s_w2q[:, 1024 * hc + 128 * oc:1024 * hc + 128 * oc + 128],
                              s_ybr[:, 260 * sl2:260 * sl2 + 256], start=(hc == 0),
                              stop=(hc == 31), waits=[tcv, tk_w2q.get(hc)],
                              banks=(oc,))
            tcv_prev[sl2] = tmm2[7]
            last_mm = tmm2[7]
            ty8_last = ty8
        tk_hy_wr_ok = ty8_last
        for oc in range(8):
            so_sl = oc % 2
            te1 = vop(lambda e, st, b=oc: e.tensor_mul(
                s_tc0[:, 0:256], bank(b, 256), s_bc[:, 258:258 + 256]),
                waits=[tmm2[oc], tg2v, prev_te1])
            rel((oc,), te1)
            te2 = vop(lambda e, st, o=oc, t=tc, s=so_sl: e.tensor_add(
                s_out[:, 256 * s:256 * s + 256], s_tc0[:, 0:256],
                s_x1w[:, 258 * o + 1:258 * o + 257]),
                waits=[te1, tk_out_dma[so_sl], tk_x1w[oc]])
            prev_te1 = te2
            tko = dma("sync", fm(ap["out"])[:, oc, 256 * tc:256 * tc + 256],
                      s_out[:, 256 * so_sl:256 * so_sl + 256], waits=[te2],
                      sem="dO%d" % so_sl)
            tk_out_dma[so_sl] = tko
            tk_out_done.append(tko)
        tk_x1w_rd = tk_out_done[-1]
        tk_red_guard = mns2 + cps2
        tk_xqb_rd = tk_z2[-1]
        if "xq2" in tapap and tc == 0:
            dma("sync", tapap["xq2"][:, :], s_ybr[:, :],
                waits=[tcv_prev[0], tcv_prev[1]], sem="dTP")

    p.emit("sync", lambda e, st: e.nop(),
           waits=[p.tok(s) for s in ["dC","dX0","dX1","dTP","dW0","dW1","dWD0","dWD1","dKD","dZN","dZI","dZB","dWO0","dWO1","dXR0","dXR1","dO0","dO1","tT","tV","tS","tG","CC"]])
    return _finish(nc, ctx, p)


def _finish(nc, ctx, p):
    sems = {}
    for s in ["dC", "dX0", "dX1", "dTP", "dW0", "dW1", "dWD0", "dWD1", "dKD", "dZN", "dZI",
              "dZB", "dWO0", "dWO1", "dXR0", "dXR1", "dO0", "dO1",
              "tT", "tV", "tS", "tG", "CC"]:
        sems[s] = ctx.enter_context(nc.semaphore(s))
    blk = ctx.enter_context(nc.Block())
    state = {}

    @blk.sync
    def _(e):
        p.run("sync", e, sems, state)

    @blk.tensor
    def _(e):
        p.run("tensor", e, sems, state)

    @blk.vector
    def _(e):
        p.run("vector", e, sems, state)

    @blk.scalar
    def _(e):
        p.run("scalar", e, sems, state)

    @blk.gpsimd
    def _(e):
        p.run("gpsimd", e, sems, state)

    ctx.close()
    return nc



def kernel(**inputs):
    from concourse.bass_utils import run_bass_kernel_spmd
    maps = stage_inputs(inputs, n_cores=8)
    nc = build_nc(n_cores=8, n_heads=NB, stop_after="full", taps=())
    res = run_bass_kernel_spmd(nc, maps, core_ids=list(range(8)))
    return gather_out(res.results)


# version 05
# speedup vs baseline: 1.2828x; 1.0101x over previous
import numpy as np
import ml_dtypes
from contextlib import ExitStack
import concourse.bass as bass
import concourse.mybir as mybir

F32 = mybir.dt.float32
BF16 = mybir.dt.bfloat16
I8 = mybir.dt.int8
F8 = mybir.dt.float8e4
F16 = mybir.dt.float16
AF = mybir.ActivationFunctionType
ALU = mybir.AluOpType
AX = mybir.AxisListType
BF = ml_dtypes.bfloat16
F16N = np.float16

D = 1024
S = 4096
T = 2048
NB = 8
BS = 256
H = 4096
W = 2050
WZ = 4098
EPS = 1e-5


def qkv_perm():
    perm = np.empty(3072, np.int64)
    for sec in range(3):
        for g in range(16):
            perm[1024 * sec + 64 * g: 1024 * sec + 64 * g + 64] = \
                np.arange(192 * g + 64 * sec, 192 * g + 64 * sec + 64)
    return perm

def zrow_to_cglob():
    out = np.empty(1024, np.int64)
    for r in range(1024):
        rank, rr = divmod(r, 512)
        chunk, part = divmod(rr, 128)
        m = (4 if part >= 64 else 0) + chunk
        out[r] = 64 * (8 * rank + m) + part % 64
    return out

def stage_inputs(inputs, n_cores=8):
    x = np.asarray(inputs["L0_attn_x"], np.float32)
    Wqkv = np.asarray(inputs["L0_attn_W_qkv"], np.float32)
    bqkv = np.asarray(inputs["L0_attn_b_qkv"], np.float32)
    Wo = np.asarray(inputs["L0_attn_W_o"], np.float32)
    bo = np.asarray(inputs["L0_attn_b_o"], np.float32)
    w1 = np.asarray(inputs["L0_ffn_w1"], np.float32)
    cw = np.asarray(inputs["L0_ffn_conv_w"], np.float32)
    cb = np.asarray(inputs["L0_ffn_conv_b"], np.float32)
    w2 = np.asarray(inputs["L0_ffn_w2"], np.float32)

    perm = qkv_perm()
    com = dict(
        WqT=np.ascontiguousarray(Wqkv.T[:, perm]),
        bq_pp=np.ascontiguousarray(bqkv[perm].reshape(24, 128).T),
        WoT=np.ascontiguousarray(Wo.T[zrow_to_cglob(), :]),
        bo_pp=np.ascontiguousarray(bo.reshape(8, 128).T),
        w1T=np.ascontiguousarray(w1.T),
        w2T=np.ascontiguousarray(w2.T),
        cw_pp=np.ascontiguousarray(cw[:, 0, :].reshape(32, 128, 3).transpose(1, 2, 0).reshape(128, 96)),
        cb_pp=np.ascontiguousarray(cb.reshape(32, 128).T),
        ident=np.eye(128, dtype=F16N),
        ident32=np.eye(128, dtype=np.float32),
        ones1=np.ones((128, 1), dtype=F16N),
        ones32=np.ones((128, 128), dtype=np.float32),
    )
    maps = []
    for c in range(n_cores):
        b, off = divmod(c, 2)
        xr = np.zeros((D, W), np.float32)
        if off == 0:
            xr[:, 1:2050] = x[b, 0:2049, :].T
        else:
            xr[:, 0:2049] = x[b, 2047:4096, :].T
        mask_l = np.ones((1, 258), np.float32)
        mask_r = np.ones((1, 258), np.float32)
        if off == 0:
            mask_l[0, 0] = 0.0
        else:
            mask_r[0, 257] = 0.0
        maps.append(dict(com, x_res=xr, mask_l=mask_l, mask_r=mask_r))
    return maps

def gather_out(results):
    out = np.empty((4, S, D), np.float32)
    for c, r in enumerate(results):
        b, off = divmod(c, 2)
        out[b, T * off:T * off + T, :] = np.asarray(r["out"]).T
    return out


class P:
    ENGS = ("sync", "tensor", "vector", "scalar", "gpsimd")

    def __init__(self):
        self.q = {e: [] for e in self.ENGS}
        self.cnt = {}
        self.water = {e: {} for e in self.ENGS}
        self.bkg = [None] * 8

    def tok(self, sem):
        return (sem, self.cnt.get(sem, 0))

    def emit(self, eng, fn, waits=(), inc=None, amt=1):
        best = {}
        for smv in waits:
            if smv is None:
                continue
            s, v = smv
            if v > best.get(s, 0):
                best[s] = v
        wl = []
        for s, v in best.items():
            if v > self.water[eng].get(s, 0):
                self.water[eng][s] = v
                wl.append((s, v))
        tk = None
        if inc is not None:
            self.cnt[inc] = self.cnt.get(inc, 0) + amt
            tk = (inc, self.cnt[inc])
        self.q[eng].append((fn, wl, inc, amt))
        return tk

    def run(self, eng_name, handle, sems, state):
        for fn, wl, inc, amt in self.q[eng_name]:
            for s, v in wl:
                handle.wait_ge(sems[s], v)
            inst = fn(handle, state)
            if inc is not None:
                inst.then_inc(sems[inc], amt)


def build_nc(n_cores=8, n_heads=NB, stop_after="full", taps=(), groups=None):
    nc = bass.Bass("TRN2", target_bir_lowering=False)
    p = P()
    if groups is None:
        groups = [[i, i + 1] for i in range(0, n_cores, 2)]

    ap = {}
    for nm, sh, dt in [("x_res", [D, W], F32), ("WqT", [D, 3072], F32),
                       ("bq_pp", [128, 24], F32), ("WoT", [D, D], F32),
                       ("bo_pp", [128, 8], F32), ("w1T", [D, H], F32),
                       ("w2T", [H, D], F32), ("cw_pp", [128, 96], F32),
                       ("cb_pp", [128, 32], F32), ("ident", [128, 128], F16),
                       ("ident32", [128, 128], F32), ("ones1", [128, 1], F16),
                       ("ones32", [128, 128], F32),
                       ("mask_l", [1, 258], F32), ("mask_r", [1, 258], F32)]:
        ap[nm] = nc.dram_tensor(nm, sh, dt, kind="ExternalInput").ap()
    ap["out"] = nc.dram_tensor("out", [D, T], F32, kind="ExternalOutput").ap()
    zin = nc.dram_tensor("zin", [512, WZ], F16)
    zgath = nc.dram_tensor("zgath", [1024, WZ], F16)
    wodram = nc.dram_tensor("wodram", [1024, 1024], F16)

    tap_shapes = {"stats": ([1, 2 * W], F32), "Y": ([128, 24 * BS], F16),
                  "ln1": ([128, 8 * BS], F16), "zin": ([512, WZ], F16),
                  "x1w": ([128, 8 * 258], F32), "xq": ([128, 8 * 258], BF16),
                  "h": ([128, 32 * 258], F32), "xq2": ([128, 2 * 260], BF16),
                  "hpre": ([128, 32 * 258], F32), "w1q": ([128, 32768], F8)}
    tapap = {}
    for tname in taps:
        sh, dt = tap_shapes[tname]
        tapap[tname] = nc.dram_tensor("tap_" + tname, sh, dt, kind="ExternalOutput").ap()

    ctx = ExitStack()
    sbt = lambda n, s, d: ctx.enter_context(nc.sbuf_tensor(n, s, d))

    s_ident = sbt("s_ident", [128, 128], F16)
    s_id32 = sbt("s_id32", [128, 128], F32)
    s_ones = sbt("s_ones", [128, 1], F16)
    s_on32 = sbt("s_on32", [128, 128], F32)
    s_bq = sbt("s_bq", [128, 24], F32)
    s_bo = sbt("s_bo", [128, 8], F32)
    s_cw = sbt("s_cw", [128, 96], F32)
    s_cb = sbt("s_cb", [128, 32], F32)
    s_maskl = sbt("s_maskl", [1, 258], F32)
    s_maskr = sbt("s_maskr", [1, 258], F32)
    s_mr = sbt("s_mr", [1, 2 * W], F32)
    s_scl = sbt("s_scl", [1, 1040], F32)
    s_row = s_scl[0:1, 0:512]
    s_ws = sbt("s_ws", [128, 8], F32)

    AR1 = sbt("AR1", [128, 12288], F32)
    s_wq = AR1[:, 0:12288].bitcast(F16)
    s_w1q = AR1[:, 0:8192].bitcast(F8)
    s_xqb = AR1[:, 8192:9224].bitcast(BF16)
    s_y8r = AR1[:, 9224:9354].bitcast(I8)
    s_ybr = AR1[:, 9354:9614].bitcast(BF16)
    s_bc = AR1[:, 9614:10388]
    s_cast1 = AR1[:, 10388:10644].bitcast(BF16)
    s_cast2 = AR1[:, 10644:10900].bitcast(BF16)

    AR2 = sbt("AR2", [128, 16704], F32)
    s_xblk = AR2[:, 0:4096]
    s_ln1 = AR2[:, 4096:6144].bitcast(F16)
    s_xsq = AR2[:, 6144:7168].bitcast(F16)
    s_Y = AR2[:, 7168:10240].bitcast(F16)
    s_kdup = AR2[:, 10240:11264].bitcast(F16)
    s_vt = AR2[:, 11264:12304].bitcast(BF16)
    s_E = AR2[:, 12304:14352].bitcast(BF16)
    s_zn = AR2[:, 14352:14864].bitcast(F16)
    s_zr = AR2[:, 14864:15888]
    s_qd = AR2[:, 6144:7168].bitcast(F16)
    s_atmp = AR2[:, 15888:16144]
    s_abc = AR2[:, 16144:16656]
    s_x1w = AR2[:, 0:2064]
    s_lt = AR2[:, 2064:4128]
    s_xq8 = AR2[:, 4128:4644].bitcast(I8)
    s_w8t = AR2[:, 4128:4640].bitcast(I8)
    s_tc0 = AR2[:, 4644:5160]
    s_red = AR2[:, 5160:5684]
    s_tcb = AR2[:, 5684:5942]
    s_out = AR2[:, 5942:6454]
    s_xr = AR2[:, 6454:7478]
    s_wos = AR2[:, 7478:8502].bitcast(F16)
    s_w2q = AR2[:, 8502:16694].bitcast(F8)

    AR3 = sbt("AR3", [128, 8256], F32)
    s_wst = AR3[:, 0:4096]
    s_hy = AR3[:, 0:8256]
    OT = AR3[:, 4096:6144]

    ARZ = sbt("ARZ", [128, 8200], F32)
    s_z = ARZ[:, :].bitcast(F16)

    ps = ctx.enter_context(nc.psum_tensor("ps", [128, 4096], F32))

    def bank(i, cols=512, n0=0):
        return ps[:, 512 * i + n0: 512 * i + n0 + cols]

    def dma(eng, out, in_, waits=(), sem=None):
        assert sem is not None
        return p.emit(eng, lambda e, st, o=out, i=in_: e.dma_start(out=o, in_=i),
                      waits=waits, inc=sem, amt=16)

    def mm(out, lhsT, rhs, start, stop, waits=(), banks=()):
        w = list(waits)
        if start:
            w += [p.bkg[b] for b in banks]
        return p.emit("tensor",
                      lambda e, st, o=out, l=lhsT, r=rhs, a=start, b=stop:
                      e.matmul(o, l, r, start=a, stop=b), waits=w, inc="tT")

    def tr(out, in_, idt, waits=(), banks=()):
        w = list(waits) + [p.bkg[b] for b in banks]
        return p.emit("tensor", lambda e, st, o=out, i=in_, d=idt: e.transpose(o, i, d),
                      waits=w, inc="tT")

    def rel(banks, tok):
        for b in banks:
            p.bkg[b] = tok

    def vop(fn, waits=()):
        return p.emit("vector", fn, waits=waits, inc="tV")

    def sop(fn, waits=()):
        return p.emit("scalar", fn, waits=waits, inc="tS")

    def pop(fn, waits=()):
        return p.emit("gpsimd", fn, waits=waits, inc="tG")

    def fence_v(tok):
        return vop(lambda e, st: e.memset(s_ws[:, 6:7], 0.0), waits=[tok])

    def fm(dram_ap):
        return dram_ap.rearrange("(c p) w -> p c w", p=128)

    for o_, i_ in [(s_ident[:, :], ap["ident"][:, :]), (s_id32[:, :], ap["ident32"][:, :]),
                   (s_ones[:, :], ap["ones1"][:, :]), (s_on32[:, :], ap["ones32"][:, :]),
                   (s_bq[:, :], ap["bq_pp"][:, :]), (s_bo[:, :], ap["bo_pp"][:, :]),
                   (s_cw[:, :], ap["cw_pp"][:, :]), (s_cb[:, :], ap["cb_pp"][:, :]),
                   (s_maskl[:, :], ap["mask_l"][:, :]), (s_maskr[:, :], ap["mask_r"][:, :])]:
        dma("sync", o_, i_, sem="dC")
    tk_consts = p.tok("dC")
    tk_id = tk_id32 = tk_ones = tk_on32 = tk_consts
    tk_vtones = vop(lambda e, st: e.memset(s_vt[:, 64:32 * 65:65], 1.0))
    tk_zpad = vop(lambda e, st: e.memset(s_z[:, :], 0.0))
    tk_y8ms = vop(lambda e, st: e.memset(s_y8r[:, :], 0))
    tk_mrms = vop(lambda e, st: e.memset(s_mr[0:1, :], 0.0))
    vop(lambda e, st: e.memset(s_ws[:, 4:5], -80.0))
    tk_expc = vop(lambda e, st: e.memset(s_ws[:, 5:6], 0.125))

    def bcast_sb(row_ap, n, bnk, dst, waits):
        tm = mm(bank(bnk, n), s_on32[0:1, 0:128], row_ap, True, True,
                waits=list(waits) + [tk_on32], banks=(bnk,))
        tv = vop(lambda e, st, d=dst, b=bnk, k=n: e.tensor_copy(out=d, in_=bank(b, k)),
                 waits=[tm])
        rel((bnk,), tv)
        return tv

    def bcast_ps(row_ap, n, bnk, waits):
        return mm(bank(bnk, n), s_on32[0:1, 0:128], row_ap, True, True,
                  waits=list(waits) + [tk_on32], banks=(bnk,))

    xr3 = fm(ap["x_res"])
    WqT3 = fm(ap["WqT"])
    WoT3 = fm(ap["WoT"])
    w1T3 = fm(ap["w1T"])
    w2T3 = fm(ap["w2T"])
    wod3 = fm(wodram.ap())

    xslot_free = [None, None]
    tk_stats = []
    for j in range(NB):
        sl = j % 2
        xs = s_xblk[:, 2048 * sl:2048 * sl + 2048]
        lns = s_ln1[:, 2048 * sl:2048 * sl + 2048]
        tkx = dma("sync", xs.rearrange("p (c w) -> p c w", c=8),
                  xr3[:, :, 1 + BS * j:1 + BS * j + BS], waits=[xslot_free[sl]],
                  sem="dX%d" % sl)
        tkb = vop(lambda e, st, o=lns, i=xs: e.tensor_copy(out=o, in_=i), waits=[tkx])
        tkq = vop(lambda e, st, i=lns: e.tensor_mul(s_xsq[:, :], i, i),
                  waits=[tkb, tk_stats[-1] if tk_stats else None])
        tm1 = tm2 = None
        for dc in range(8):
            tm1 = mm(ps[0:1, 1024:1024 + 256], s_ones[:, 0:1],
                     lns[:, 256 * dc:256 * dc + 256], start=(dc == 0), stop=(dc == 7),
                     waits=[tkb, tk_ones], banks=(2,))
            tm2 = mm(ps[0:1, 1536:1536 + 256], s_ones[:, 0:1],
                     s_xsq[:, 256 * dc:256 * dc + 256], start=(dc == 0), stop=(dc == 7),
                     waits=[tkq], banks=(3,))
        t1 = vop(lambda e, st, j=j: e.tensor_scalar(
            out=s_mr[0:1, BS * j:BS * j + BS], in0=ps[0:1, 1024:1024 + 256],
            scalar1=1.0 / D, scalar2=None, op0=ALU.mult), waits=[tm1, tk_mrms])
        t2 = vop(lambda e, st, j=j: e.tensor_scalar(
            out=s_mr[0:1, W + BS * j:W + BS * j + BS], in0=ps[0:1, 1536:1536 + 256],
            scalar1=1.0 / D, scalar2=None, op0=ALU.mult), waits=[tm2])
        rel((2,), t1)
        rel((3,), t2)
        t3 = vop(lambda e, st, j=j: e.tensor_mul(
            s_row[0:1, 0:BS], s_mr[0:1, BS * j:BS * j + BS],
            s_mr[0:1, BS * j:BS * j + BS]), waits=[t1])
        t4 = vop(lambda e, st, j=j: e.tensor_sub(
            s_mr[0:1, W + BS * j:W + BS * j + BS],
            s_mr[0:1, W + BS * j:W + BS * j + BS], s_row[0:1, 0:BS]), waits=[t2, t3])
        t5 = vop(lambda e, st, j=j: e.tensor_scalar(
            out=s_mr[0:1, W + BS * j:W + BS * j + BS],
            in0=s_mr[0:1, W + BS * j:W + BS * j + BS],
            scalar1=EPS, scalar2=None, op0=ALU.add), waits=[t4])
        t6 = vop(lambda e, st, j=j: e.reciprocal(
            out=s_mr[0:1, W + BS * j:W + BS * j + BS],
            in_=s_mr[0:1, W + BS * j:W + BS * j + BS]), waits=[t5])
        tk_stats.append(t6)
        xslot_free[sl] = tkb
    tk_rstd = sop(lambda e, st: e.activation(
        out=s_mr[0:1, W:W + T], in_=s_mr[0:1, W:W + T], func=AF.Sqrt),
        waits=[tk_stats[-1]])

    if "stats" in tapap:
        dma("sync", tapap["stats"][0:1, :], s_mr[0:1, :], waits=[tk_rstd], sem="dTP")

    wst_free = [None, None]
    slot_ctr = [0]
    qstate = {}

    def sub_chunks(width):
        return [(a, min(2048, width - a)) for a in range(0, width, 2048)]

    def j_cast(src_sl, dst_sl, toks, key):
        def go():
            sl = slot_ctr[0] & 1
            slot_ctr[0] += 1
            n = src_sl.shape[-1]
            tkd = dma("sync", s_wst[:, 2048 * sl:2048 * sl + n], src_sl,
                      waits=[wst_free[sl]], sem="dW%d" % sl)
            tkv = vop(lambda e, st, s=sl, d=dst_sl, nn=n: e.tensor_copy(
                out=d, in_=s_wst[:, 2048 * s:2048 * s + nn]), waits=[tkd])
            wst_free[sl] = tkv
            toks[key] = tkv
        return go

    def j_q1(src_sl, key, last, nel):
        def go():
            sl = slot_ctr[0] & 1
            slot_ctr[0] += 1
            n = src_sl.shape[-1]
            tkd = dma("sync", s_wst[:, 2048 * sl:2048 * sl + n], src_sl,
                      waits=[wst_free[sl]], sem="dW%d" % sl)
            tkr = vop(lambda e, st, s=sl, nn=n: e.reduce_sum(
                out=s_ws[:, 3:4], in_=s_wst[:, 2048 * s:2048 * s + nn],
                axis=AX.X, apply_absolute_value=True),
                waits=[tkd, qstate.get((key, "redfree"))])
            if (key, "acc") not in qstate:
                ta = vop(lambda e, st: e.tensor_copy(out=s_ws[:, 2:3], in_=s_ws[:, 3:4]),
                         waits=[tkr])
            else:
                ta = vop(lambda e, st: e.tensor_add(
                    s_ws[:, 2:3], s_ws[:, 2:3], s_ws[:, 3:4]),
                    waits=[tkr, qstate.get((key, "acc"))])
            qstate[(key, "acc")] = ta
            qstate[(key, "redfree")] = ta
            wst_free[sl] = tkr
            if last:
                tt = tr(ps[0:1, 1024:1152], s_ws[:, 2:3], s_id32[:, :],
                        waits=[ta, tk_id32], banks=(2,))
                tsum = vop(lambda e, st: e.reduce_sum(
                    out=s_scl[0:1, 1036:1037], in_=ps[0:1, 1024:1152], axis=AX.X),
                    waits=[tt, qstate.get("rowfree")])
                rel((2,), tsum)
                mcol = 1032 if key == "w1" else 1033
                tmean = vop(lambda e, st, mc=mcol, ne=nel: e.tensor_scalar(
                    out=s_scl[0:1, mc:mc + 1], in0=s_scl[0:1, 1036:1037],
                    scalar1=1.0 / ne, scalar2=None, op0=ALU.mult), waits=[tsum])
                trec = vop(lambda e, st, mc=mcol: e.reciprocal(
                    out=s_scl[0:1, 1037:1038], in_=s_scl[0:1, mc:mc + 1]),
                    waits=[tmean])
                wcol = 0 if key == "w1" else 1
                tbm = mm(ps[:, 1024:1025], s_on32[0:1, 0:128], s_scl[0:1, 1037:1038],
                         True, True, waits=[trec, tk_on32], banks=(2,))
                tbv = vop(lambda e, st, wc=wcol: e.tensor_copy(
                    out=s_ws[:, wc:wc + 1], in_=ps[:, 1024:1025]), waits=[tbm])
                rel((2,), tbv)
                qstate[(key, "winv")] = tbv
                qstate[(key, "mean")] = tmean
                qstate["rowfree"] = trec
        return go

    def j_q2(src_sl, dst_sl, key, toks, tkey):
        def go():
            sl = slot_ctr[0] & 1
            slot_ctr[0] += 1
            n = src_sl.shape[-1]
            wcol = 0 if key == "w1" else 1
            tkd = dma("sync", s_wst[:, 2048 * sl:2048 * sl + n], src_sl,
                      waits=[wst_free[sl]], sem="dW%d" % sl)
            ta = vop(lambda e, st, s=sl, nn=n, wc=wcol: e.tensor_scalar(
                out=s_wst[:, 2048 * s:2048 * s + nn],
                in0=s_wst[:, 2048 * s:2048 * s + nn],
                scalar1=s_ws[:, wc:wc + 1], scalar2=None, op0=ALU.mult),
                waits=[tkd, qstate[(key, "winv")]])
            tc8 = vop(lambda e, st, s=sl, nn=n: e.tensor_scalar(
                out=s_w8t[:, 0:nn], in0=s_wst[:, 2048 * s:2048 * s + nn],
                scalar1=-1.0, scalar2=1.0, op0=ALU.max, op1=ALU.min),
                waits=[ta, qstate.get("w8free")])
            td = vop(lambda e, st, d=dst_sl, nn=n: e.tensor_copy(
                out=d, in_=s_w8t[:, 0:nn]), waits=[tc8])
            qstate["w8free"] = td
            wst_free[sl] = tc8
            toks[tkey] = td
        return go

    jobsA = []
    tk_wq = {}
    for kc in range(8):
        for (a, n) in sub_chunks(3072):
            jobsA.append(j_cast(WqT3[:, kc, a:a + n],
                                s_wq[:, 3072 * kc + a:3072 * kc + a + n], tk_wq,
                                (kc, a)))
    w1subs = [(kc, a, n) for kc in range(8) for (a, n) in sub_chunks(4096)]
    for i, (kc, a, n) in enumerate(w1subs):
        jobsA.append(j_q1(w1T3[:, kc, a:a + n], "w1", i == len(w1subs) - 1, D * H))
    for i, kc in enumerate(range(32)):
        jobsA.append(j_q1(w2T3[:, kc, :], "w2", i == 31, D * H))

    jobsB0 = []
    tk_wocast = {}
    for kc in range(8):
        def mk(kc=kc):
            def go():
                sl = slot_ctr[0] & 1
                slot_ctr[0] += 1
                tkd = dma("sync", s_wst[:, 2048 * sl:2048 * sl + 1024],
                          WoT3[:, kc, :], waits=[wst_free[sl]], sem="dW%d" % sl)
                wsl = kc % 2
                tkv = vop(lambda e, st, s=sl, ws=wsl: e.tensor_copy(
                    out=s_wos[:, 1024 * ws:1024 * ws + 1024],
                    in_=s_wst[:, 2048 * s:2048 * s + 1024]),
                    waits=[tkd, qstate.get(("wos", wsl))])
                tks = dma("sync", wod3[:, kc, :],
                          s_wos[:, 1024 * wsl:1024 * wsl + 1024], waits=[tkv],
                          sem="dWD%d" % wsl)
                qstate[("wos", wsl)] = tks
                wst_free[sl] = tkv
                tk_wocast[kc] = tks
            return go
        jobsB0.append(mk())
    tk_w1q = {}
    for kc in range(8):
        for (a, n) in sub_chunks(4096):
            jobsB0.append(j_q2(w1T3[:, kc, a:a + n],
                               s_w1q[:, 4096 * kc + a:4096 * kc + a + n], "w1",
                               tk_w1q, kc))
    jobsB1 = []
    tk_w2q = {}
    for kc in range(32):
        jobsB1.append(j_q2(w2T3[:, kc, :], s_w2q[:, 1024 * kc:1024 * kc + 1024], "w2",
                           tk_w2q, kc))

    for jb in jobsA[:16]:
        jb()
    jobsA = jobsA[16:]
    nh = min(n_heads, NB)
    per_block = (len(jobsA) + nh - 1) // max(nh, 1)

    lnslot_free = [None, None]
    tk_Y_free = None
    zloc_last = None
    NS = {"prev_av": [None, None], "av_last": [None, None], "zn_free": None,
          "rb_free": None, "zr_free": [None, None], "ot_free": [None, None],
          "pend_norm": None, "zloc_last": None}

    def flush_norm():
        qcn, jn, pend = NS["pend_norm"]
        pmn = 64 * (jn // 4)
        zchn = jn % 4
        for (par, tevO, tevZ) in pend:
            obank = 4 + 2 * par
            zo = 1024 * par
            trb1 = mm(ps[0:64, 512 * obank:512 * obank + 512], s_on32[64:65, 0:64],
                      OT[64:65, zo:zo + 512], True, True, waits=[tevZ, tk_on32],
                      banks=(obank,))
            trb2 = mm(ps[0:64, 512 * (obank + 1):512 * (obank + 1) + 512],
                      s_on32[64:65, 0:64], OT[64:65, zo + 512:zo + 1024], True, True,
                      waits=[], banks=(obank + 1,))
            trbe = vop(lambda e, st, b=obank: e.tensor_copy(
                out=s_zr[0:64, 0:1024], in_=ps[0:64, 512 * b:512 * b + 1024]),
                waits=[trb1, trb2, NS["rb_free"]])
            rel((obank, obank + 1), trbe)
            NS["zr_free"][par] = trbe
            trec = vop(lambda e, st: e.reciprocal(
                out=s_zr[0:64, 0:1024], in_=s_zr[0:64, 0:1024]), waits=[trbe])
            tdiv = vop(lambda e, st, pp=par: e.tensor_mul(
                s_zn[0:64, 0:1024], OT[0:64, 1024 * pp:1024 * pp + 1024],
                s_zr[0:64, 0:1024]), waits=[trec, tevO, NS["zn_free"]])
            NS["rb_free"] = tdiv
            NS["ot_free"][par] = tdiv
            if jn < 4:
                srcz = s_zn[0:64, :]
                tk_shift = tdiv
            else:
                tk_shift = dma("gpsimd", s_zn[64:128, 0:1024], s_zn[0:64, 0:1024],
                               waits=[tdiv, NS["zn_free"]], sem="dZN")
                srcz = s_zn[64:128, :]
            evs = []
            for i in range(4):
                g = par + 2 * (4 * qcn + i)
                st0 = zchn * WZ + 1 + g
                evs.append(vop(lambda e, st, ii=i, s0=st0, sz=srcz, pb=pmn:
                    e.tensor_copy(
                        out=s_z[pb:pb + 64, s0:s0 + 16 * 255 + 1:16],
                        in_=sz[:, 256 * ii:256 * ii + 256]),
                    waits=[tk_shift, tk_zpad]))
            NS["zn_free"] = evs[-1]
            NS["zloc_last"] = evs[-1]
        NS["pend_norm"] = None

    for j in range(nh):
        sl = j % 2
        xs = s_xblk[:, 2048 * sl:2048 * sl + 2048]
        lns = s_ln1[:, 2048 * sl:2048 * sl + 2048]
        tkx = dma("sync", xs.rearrange("p (c w) -> p c w", c=8),
                  xr3[:, :, 1 + BS * j:1 + BS * j + BS], waits=[xslot_free[sl]],
                  sem="dX%d" % sl)
        tva = bcast_sb(s_mr[0:1, BS * j:BS * j + BS], BS, 2, s_abc[:, 0:256],
                       [tk_rstd, tk_Y_free])
        tvb = bcast_sb(s_mr[0:1, W + BS * j:W + BS * j + BS], BS, 3,
                       s_abc[:, 256:512], [tk_rstd])
        tk_norm = []
        tprev2 = None
        for dc in range(8):
            tn1 = vop(lambda e, st, d=dc, x=xs: e.tensor_sub(
                s_atmp[:, 0:256], x[:, 256 * d:256 * d + 256], s_abc[:, 0:256]),
                waits=[tva, tkx, lnslot_free[sl], tprev2])
            tn2 = vop(lambda e, st, d=dc, l=lns: e.tensor_mul(
                l[:, 256 * d:256 * d + 256], s_atmp[:, 0:256], s_abc[:, 256:512]),
                waits=[tvb, tn1])
            tprev2 = tn2
            tk_norm.append(tn2)
        xslot_free[sl] = tk_norm[-1]
        if "ln1" in tapap and j == 0:
            tk_norm[-1] = fence_v(dma("sync", tapap["ln1"][:, :], lns[:, :],
                                      waits=[tk_norm[-1]], sem="dTP"))

        tk_evY = [None, None]
        tk_kd1 = tk_kd2 = None
        for cc in range(24):
            bk = cc % 2
            tmm = None
            for dc in range(8):
                tmm = mm(bank(bk, 256),
                         s_wq[:, 3072 * dc + 128 * cc:3072 * dc + 128 * cc + 128],
                         lns[:, 256 * dc:256 * dc + 256], start=(dc == 0), stop=(dc == 7),
                         waits=[tk_norm[dc], tk_wq.get((dc, 0)), tk_wq.get((dc, 2048)),
                                tk_Y_free], banks=(bk,))
            tev = vop(lambda e, st, c=cc, b=bk: e.tensor_scalar(
                out=s_Y[:, 256 * c:256 * c + 256], in0=bank(b, 256),
                scalar1=s_bq[:, c:c + 1], scalar2=None, op0=ALU.add),
                waits=[tmm, tk_consts])
            rel((bk,), tev)
            tk_evY[bk] = tev
            if cc == 7:
                tk_kd2 = dma("gpsimd", s_qd[0:64, :], s_Y[64:128, 0:2048],
                             waits=[tev, tk_Y_free], sem="dKD")
            if cc == 15:
                tk_kd1 = dma("gpsimd", s_kdup[0:64, :], s_Y[64:128, 2048:4096],
                             waits=[tev, tk_Y_free], sem="dKD")
        tk_kd2 = tk_kd1
        tk_Yall = [tk_evY[0], tk_evY[1]]
        lnslot_free[sl] = tk_Yall[1]
        if "Y" in tapap and j == 0:
            tk_Yall = [fence_v(dma("sync", tapap["Y"][:, :], s_Y[:, :],
                                   waits=tk_Yall, sem="dTP"))]

        tk_vt = []
        tk_vtev = None
        for grp in range(8):
            tts = []
            for t4 in range(4):
                kc = 4 * grp + t4
                g, hf = kc // 2, kc % 2
                pb = 64 * (g % 2)
                src = s_Y[pb:pb + 64, 4096 + 256 * (g // 2) + 128 * hf:
                          4096 + 256 * (g // 2) + 128 * hf + 128]
                idt = s_ident[pb:pb + 64, pb:pb + 64]
                dst = ps[:, 512 * (4 + t4):512 * (4 + t4) + 32].bitcast(F16)
                tts.append(tr(dst, src, idt,
                              waits=tk_Yall + [tk_id, tk_Y_free], banks=(4 + t4,)))
            tk_vtev = vop(lambda e, st, gg=grp: e.tensor_copy(
                out=s_vt[:, 65 * 4 * gg:65 * 4 * gg + 65 * 4]
                    .rearrange("p (k c) -> p k c", k=4)[:, :, 0:64],
                in_=ps[:, 2048:4096].bitcast(F16)
                    .rearrange("p (k c) -> p k c", k=4)[:, :, 0:64]),
                waits=tts + [tk_vtones, tk_Y_free])
            rel((4, 5, 6, 7), tk_vtev)
            tk_vt.append(tk_vtev)

        pm = 64 * (j // 4)
        zch = j % 4
        texp_of = {}

        def emit_av(kc, texp_of=texp_of, tk_vt=tk_vt):
            slot = kc % 2
            for par in range(2):
                obank = 4 + 2 * par
                tav = None
                for half in range(2):
                    tav = mm(ps[0:65, 512 * obank + 512 * half:
                                 512 * obank + 512 * half + 512],
                             s_vt[:, 65 * kc:65 * kc + 65],
                             s_E[:, 2048 * slot + 1024 * par + 512 * half:
                                 2048 * slot + 1024 * par + 512 * half + 512],
                             start=(kc == 0), stop=(kc == 31),
                             waits=[texp_of[kc], tk_vt[kc // 4]],
                             banks=(obank, obank + 1))
                NS["av_last"][par] = tav
            NS["prev_av"][slot] = tav

        for qc in range(2):
            tk_O = [None, None]
            for kc in range(32):
                g_k, hf = kc // 2, kc % 2
                slot = kc % 2
                if (g_k % 2) == 0:
                    klhs = s_Y[0:64, 2048 + 256 * (g_k // 2) + 128 * hf:
                               2048 + 256 * (g_k // 2) + 128 * hf + 128]
                    kw = tk_Yall
                else:
                    klhs = s_kdup[0:64, 256 * (g_k // 2) + 128 * hf:
                                  256 * (g_k // 2) + 128 * hf + 128]
                    kw = [tk_kd1]
                ts4 = []
                for par in range(2):
                    if par == 0:
                        qrhs = s_Y[0:64, 1024 * qc:1024 * qc + 1024]
                        qw = tk_Yall
                    else:
                        qrhs = s_qd[0:64, 1024 * qc:1024 * qc + 1024]
                        qw = [tk_kd2]
                    sbk = 2 * par
                    ts4.append(mm(bank(sbk), klhs, qrhs[:, 0:512], True, True,
                                  waits=list(kw) + list(qw), banks=(sbk,)))
                    ts4.append(mm(bank(sbk + 1), klhs, qrhs[:, 512:1024], True, True,
                                  waits=[], banks=(sbk + 1,)))
                texp = sop(lambda e, st, sl2=slot: e.activation(
                    out=s_E[:, 2048 * sl2:2048 * sl2 + 2048],
                    in_=ps[:, 0:2048], func=AF.Exp,
                    scale=s_ws[:, 5:6], bias=s_ws[:, 4:5]),
                    waits=ts4 + [NS["prev_av"][slot], tk_expc])
                rel((0, 1, 2, 3), texp)
                texp_of[kc] = texp
                if kc == 1 and NS["pend_norm"]:
                    flush_norm()
                if kc > 0:
                    emit_av(kc - 1)
            emit_av(31)
            for par in range(2):
                tk_O[par] = NS["av_last"][par]
            pend = []
            for par in range(2):
                obank = 4 + 2 * par
                tevO = vop(lambda e, st, b=obank, pp=par: e.tensor_copy(
                    out=OT[0:64, 1024 * pp:1024 * pp + 1024],
                    in_=ps[0:64, 512 * b:512 * b + 1024]),
                    waits=[tk_O[par], NS["ot_free"][par]])
                tevZ = vop(lambda e, st, b=obank, pp=par: e.tensor_copy(
                    out=OT[64:65, 1024 * pp:1024 * pp + 1024],
                    in_=ps[64:65, 512 * b:512 * b + 1024]),
                    waits=[tk_O[par], NS["zr_free"][par]])
                rel((obank, obank + 1), tevZ)
                pend.append((par, tevO, tevZ))
            NS["pend_norm"] = (qc, j, pend)
            if qc == 1 and j == nh - 1:
                flush_norm()
        zloc_last = NS["zloc_last"]
        tk_Y_free = NS["av_last"][1]

        for _ in range(per_block):
            if jobsA:
                jobsA.pop(0)()
    while jobsA:
        jobsA.pop(0)()

    tk_zst = []
    for ch in range(4):
        tk_zst.append(dma("gpsimd", zin[128 * ch:128 * ch + 128, :],
                          s_z[:, ch * WZ:ch * WZ + WZ], waits=[zloc_last], sem="dZI"))
    if "zin" in tapap:
        for ch in range(4):
            tkzt = dma("sync", tapap["zin"][128 * ch:128 * ch + 128, :],
                       s_z[:, ch * WZ:ch * WZ + WZ], waits=[zloc_last], sem="dTP")
        zloc_last = fence_v(tkzt)

    if stop_after == "A":
        ms = vop(lambda e, st: e.memset(s_out[:, 0:512], 0.0), waits=[zloc_last])
        tko = dma("sync", ap["out"][0:128, 0:512], s_out[:, 0:512], waits=[ms], sem="dO0")
        p.emit("sync", lambda e, st: e.nop(),
               waits=[tko] + [p.tok(s) for s in ["dC","dX0","dX1","dTP","dW0","dW1","dWD0","dWD1","dKD","dZN","dZI","dZB","dWO0","dWO1","dXR0","dXR1","dO0","dO1","tT","tV","tS","tG","CC"]])
        return _finish(nc, ctx, p)

    tk_cc = p.emit("gpsimd", lambda e, st: e.collective_compute(
        "AllGather", ALU.bypass, replica_groups=groups,
        ins=[zin.ap().opt()], outs=[zgath.ap().opt()]),
        waits=tk_zst, inc="CC", amt=1)

    for jb in jobsB0:
        jb()

    tk_pid = p.emit("gpsimd", lambda e, st: (
        st.__setitem__("T0", e.compute_val((e.partition_id() % 2) * T)), e.nop())[-1],
        waits=[], inc="tG")
    tk_zb = []
    for zc in range(8):
        tk_zb.append(p.emit("gpsimd", lambda e, st, z=zc: e.dma_start(
            out=s_z[:, 2050 * z:2050 * z + 2050],
            in_=zgath[128 * z:128 * z + 128, bass.ds(st["T0"], 2050)]),
            waits=[tk_cc, tk_pid, zloc_last], inc="dZB", amt=16))

    wos_free = [qstate.get(("wos", 0)), qstate.get(("wos", 1))]
    tk_woL = {}

    def wo_load(oc):
        wsl = oc % 2
        tkd = dma("gpsimd", s_wos[:, 1024 * wsl:1024 * wsl + 1024], wod3[:, oc, :],
                  waits=[wos_free[wsl], tk_wocast.get(oc), zloc_last],
                  sem="dWO%d" % wsl)
        tk_woL[oc] = (tkd, wsl)
        return tkd, wsl

    xr_free = [None, None]
    for jb in jobsB1:
        jb()

    if stop_after == "B0":
        p.emit("sync", lambda e, st: e.nop(),
               waits=[p.tok(s) for s in ["dC","dX0","dX1","dTP","dW0","dW1","dWD0","dWD1","dKD","dZN","dZI","dZB","dWO0","dWO1","dXR0","dXR1","dO0","dO1","tT","tV","tS","tG","CC"]])
        return _finish(nc, ctx, p)

    def crosspart_max(waits):
        tt1 = tr(ps[:, 512 * 6:512 * 6 + 128], s_red[:, 0:128], s_id32[:, :],
                 waits=list(waits) + [tk_id32], banks=(6,))
        tt2 = tr(ps[:, 512 * 7:512 * 7 + 128], s_red[:, 128:256], s_id32[:, :],
                 waits=list(waits), banks=(7,))
        tt3 = tr(ps[0:2, 512 * 5:512 * 5 + 128], s_red[:, 256:258], s_id32[:, :],
                 waits=list(waits), banks=(5,))
        r1 = vop(lambda e, st: e.reduce_max(out=s_red[:, 520:521],
                 in_=ps[:, 512 * 6:512 * 6 + 128], axis=AX.X), waits=[tt1])
        r2 = vop(lambda e, st: e.reduce_max(out=s_red[:, 521:522],
                 in_=ps[:, 512 * 7:512 * 7 + 128], axis=AX.X), waits=[tt2])
        r3 = vop(lambda e, st: e.reduce_max(out=s_red[0:2, 522:523],
                 in_=ps[0:2, 512 * 5:512 * 5 + 128], axis=AX.X), waits=[tt3])
        rel((6,), r1)
        rel((7,), r2)
        rel((5,), r3)
        m1 = tr(ps[0:1, 512 * 6:512 * 6 + 128], s_red[:, 520:521], s_id32[:, :],
                waits=[r1], banks=(6,))
        m2 = tr(ps[0:1, 512 * 7:512 * 7 + 128], s_red[:, 521:522], s_id32[:, :],
                waits=[r2], banks=(7,))
        m3 = tr(ps[0:1, 512 * 5:512 * 5 + 2], s_red[0:2, 522:523], s_id32[0:2, 0:2],
                waits=[r3], banks=(5,))
        c1 = vop(lambda e, st: e.tensor_copy(
            out=s_scl[0:1, 0:128], in_=ps[0:1, 512 * 6:512 * 6 + 128]), waits=[m1])
        c2 = vop(lambda e, st: e.tensor_copy(
            out=s_scl[0:1, 128:256], in_=ps[0:1, 512 * 7:512 * 7 + 128]), waits=[m2])
        c3 = vop(lambda e, st: e.tensor_copy(
            out=s_scl[0:1, 256:258], in_=ps[0:1, 512 * 5:512 * 5 + 2]), waits=[m3])
        rel((6,), c1)
        rel((7,), c2)
        rel((5,), c3)
        return [c1, c2, c3], [m1, m2, m3]

    tk_lt_rd = None
    st_guard = [None, None]
    row_guard = qstate.get("rowfree")
    tk_red_guard = []
    tk_scl_guard = [qstate.get("rowfree")]
    tk_xq8_rd = [tk_w2q.get(31)]
    tk_xqb_rd = None
    tk_hy_wr_ok = tk_w2q.get(31)
    tk_x1w_rd = None
    tk_out_dma = [None, None]
    tk_out_done = []
    prev_te1 = None
    for tc in range(8):
        j0 = 256 * tc
        tk_x1w = []
        tk_wm = {}
        for zc in range(8):
            tkwd, wsl = wo_load(zc)
            for oc in range(8):
                tk_wm[oc] = mm(ps[:, 512 * oc:512 * oc + 258],
                               s_wos[:, 1024 * wsl + 128 * oc:1024 * wsl + 128 * oc + 128],
                               s_z[:, 2050 * zc + j0:2050 * zc + j0 + 258],
                               start=(zc == 0), stop=(zc == 7),
                               waits=[tkwd, tk_zb[7]], banks=(oc,))
            wos_free[wsl] = tk_wm[7]
        for oc in range(8):
            sl = oc % 2
            tkxr = dma("sync", s_xr[:, 512 * sl:512 * sl + 258],
                       xr3[:, oc, j0:j0 + 258], waits=[xr_free[sl]], sem="dXR%d" % sl)
            tstt = vop(lambda e, st, o=oc: e.scalar_tensor_tensor(
                out=s_x1w[:, 258 * o:258 * o + 258], in0=ps[:, 512 * o:512 * o + 258],
                scalar=s_bo[:, o:o + 1], in1=s_xr[:, 512 * (o % 2):512 * (o % 2) + 258],
                op0=ALU.add, op1=ALU.add),
                waits=[tk_wm[oc], tkxr, tk_consts, tk_x1w_rd])
            rel((oc,), tstt)
            xr_free[sl] = tstt
            tk_x1w.append(tstt)
        if "x1w" in tapap and tc == 0:
            tk_x1w = [fence_v(dma("sync", tapap["x1w"][:, :], s_x1w[:, :],
                                  waits=tk_x1w, sem="dTP"))] * 8
        tm1 = tm2 = None
        tk_c1rd = [st_guard[0]]
        tk_c2rd = [st_guard[1]]
        for dc in range(8):
            tcst = sop(lambda e, st, d=dc: e.activation(
                out=s_cast1[:, 0:258], in_=s_x1w[:, 258 * d:258 * d + 258],
                func=AF.Copy), waits=[tk_x1w[dc]] + tk_c1rd)
            tsq = sop(lambda e, st, d=dc: e.activation(
                out=s_cast2[:, 0:258], in_=s_x1w[:, 258 * d:258 * d + 258],
                func=AF.Square), waits=[tk_x1w[dc], tcst] + tk_c2rd)
            tm1 = mm(ps[0:1, 1024:1024 + 258], s_ones[:, 0:1], s_cast1[:, 0:258],
                     start=(dc == 0), stop=(dc == 7), waits=[tcst, tk_ones],
                     banks=(2,))
            tm2 = mm(ps[0:1, 1536:1536 + 258], s_ones[:, 0:1], s_cast2[:, 0:258],
                     start=(dc == 0), stop=(dc == 7), waits=[tsq], banks=(3,))
            tk_c1rd, tk_c2rd = [tsq, tm1], [tm2]
        st_guard = [tm1, tm2]
        u1 = vop(lambda e, st, a=j0: e.tensor_scalar(
            out=s_mr[0:1, a:a + 258], in0=ps[0:1, 1024:1024 + 258],
            scalar1=1.0 / D, scalar2=None, op0=ALU.mult), waits=[tm1, tk_mrms])
        u2 = vop(lambda e, st, a=j0: e.tensor_scalar(
            out=s_mr[0:1, W + a:W + a + 258], in0=ps[0:1, 1536:1536 + 258],
            scalar1=1.0 / D, scalar2=None, op0=ALU.mult), waits=[tm2])
        rel((2,), u1)
        rel((3,), u2)
        u3 = vop(lambda e, st, a=j0: e.tensor_mul(
            s_row[0:1, 0:258], s_mr[0:1, a:a + 258], s_mr[0:1, a:a + 258]),
            waits=[u1, row_guard])
        u4 = vop(lambda e, st, a=j0: e.tensor_sub(
            s_mr[0:1, W + a:W + a + 258], s_mr[0:1, W + a:W + a + 258],
            s_row[0:1, 0:258]), waits=[u2, u3])
        u5 = vop(lambda e, st, a=j0: e.tensor_scalar(
            out=s_mr[0:1, W + a:W + a + 258], in0=s_mr[0:1, W + a:W + a + 258],
            scalar1=EPS, scalar2=None, op0=ALU.add), waits=[u4])
        u6 = vop(lambda e, st, a=j0: e.reciprocal(
            out=s_mr[0:1, W + a:W + a + 258], in_=s_mr[0:1, W + a:W + a + 258]),
            waits=[u5])
        row_guard = u3
        tk_rstd2 = sop(lambda e, st, a=j0: e.activation(
            out=s_mr[0:1, W + a:W + a + 258], in_=s_mr[0:1, W + a:W + a + 258],
            func=AF.Sqrt), waits=[u6, tk_rstd])
        tmb = bcast_ps(s_mr[0:1, j0:j0 + 258], 258, 4, [tk_rstd2, u1])
        tk_sub = []
        for dc in range(8):
            ts_ = vop(lambda e, st, d=dc: e.tensor_sub(
                s_lt[:, 258 * d:258 * d + 258], s_x1w[:, 258 * d:258 * d + 258],
                bank(4, 258)), waits=[tmb, tk_lt_rd, tk_x1w[dc]])
            tk_sub.append(ts_)
        rel((4,), tk_sub[-1])
        tb_ = vop(lambda e, st: e.scalar_tensor_tensor(
            out=s_red[:, 0:258], in0=s_lt[:, 0:258], scalar=-1.0,
            in1=s_lt[:, 0:258], op0=ALU.mult, op1=ALU.max),
            waits=[tk_sub[0]] + tk_red_guard)
        for dci in range(1, 8):
            t_a = vop(lambda e, st, d=dci: e.scalar_tensor_tensor(
                out=s_red[:, 258:516], in0=s_lt[:, 258 * d:258 * d + 258],
                scalar=-1.0, in1=s_lt[:, 258 * d:258 * d + 258],
                op0=ALU.mult, op1=ALU.max),
                waits=[tk_sub[dci], tb_])
            tb_ = vop(lambda e, st: e.tensor_tensor(out=s_red[:, 0:258],
                      in0=s_red[:, 0:258], in1=s_red[:, 258:516], op=ALU.max),
                      waits=[t_a])
        cps, mns = crosspart_max([tb_] + tk_scl_guard)
        q1 = vop(lambda e, st, jj=j0: e.tensor_mul(s_scl[0:1, 258:516],
                 s_scl[0:1, 0:258], s_mr[0:1, W + jj:W + jj + 258]),
                 waits=cps + [tk_rstd2])
        q2 = vop(lambda e, st: e.tensor_scalar(out=s_scl[0:1, 774:1032],
                 in0=s_scl[0:1, 258:516], scalar1=1e-5, scalar2=None, op0=ALU.max),
                 waits=[q1])
        q3 = vop(lambda e, st: e.reciprocal(out=s_scl[0:1, 258:516],
                 in_=s_scl[0:1, 774:1032]), waits=[q2])
        q4 = vop(lambda e, st, jj=j0: e.tensor_mul(s_scl[0:1, 258:516],
                 s_scl[0:1, 258:516], s_mr[0:1, W + jj:W + jj + 258]), waits=[q3])
        q5 = vop(lambda e, st: e.tensor_scalar(out=s_scl[0:1, 258:516],
                 in0=s_scl[0:1, 258:516], scalar1=127.0, scalar2=None, op0=ALU.mult),
                 waits=[q4])
        q6 = vop(lambda e, st: e.tensor_scalar(out=s_scl[0:1, 516:774],
                 in0=s_scl[0:1, 774:1032], scalar1=s_scl[0:1, 1032:1033],
                 scalar2=1.0 / 127.0, op0=ALU.mult, op1=ALU.mult),
                 waits=[q2, qstate[("w1", "mean")]])
        if tc == 0:
            q6 = vop(lambda e, st: e.tensor_mul(s_scl[0:1, 516:774], s_scl[0:1, 516:774],
                     s_maskl[0:1, :]), waits=[q6, tk_consts])
        if tc == 7:
            q6 = vop(lambda e, st: e.tensor_mul(s_scl[0:1, 516:774], s_scl[0:1, 516:774],
                     s_maskr[0:1, :]), waits=[q6, tk_consts])
        tcb = bcast_ps(s_scl[0:1, 258:516], 258, 4, [q5])
        tsbv = bcast_sb(s_scl[0:1, 516:774], 258, 5, s_bc[:, 0:258], [q6])
        tq8 = []
        for dc in range(8):
            tq8.append(vop(lambda e, st, d=dc: e.tensor_mul(
                s_xq8[:, 258 * d:258 * d + 258], s_lt[:, 258 * d:258 * d + 258],
                bank(4, 258)), waits=[tcb, tk_sub[dc]] + tk_xq8_rd))
        tk_xq8_rd = []
        rel((4,), tq8[-1])
        tqb = vop(lambda e, st: e.tensor_copy(out=s_xqb[:, :], in_=s_xq8[:, :]),
                  waits=tq8 + [tk_xqb_rd])
        if "w1q" in tapap and tc == 0:
            tqb = fence_v(dma("sync", tapap["w1q"][:, :], s_w1q[:, :],
                              waits=[tqb, tk_w1q.get(7)], sem="dTP"))
        tk_lt_rd = tq8[-1]
        if "xq" in tapap and tc == 0:
            tqb = fence_v(dma("sync", tapap["xq"][:, :], s_xqb[:, :], waits=[tqb],
                              sem="dTP"))
        tk_z2 = []
        prev_tt1 = None
        prev_tt2 = None
        prev_silu = None
        for hc in range(32):
            bk = hc % 4
            tmm = None
            for dc in range(8):
                tmm = mm(bank(bk, 258),
                         s_w1q[:, 4096 * dc + 128 * hc:4096 * dc + 128 * hc + 128],
                         s_xqb[:, 258 * dc:258 * dc + 258], start=(dc == 0),
                         stop=(dc == 7), waits=[tqb, tk_w1q.get(dc)], banks=(bk,))
            tev = vop(lambda e, st, h=hc, b=bk: e.tensor_mul(
                s_hy[:, 258 * h:258 * h + 258], bank(b, 258),
                s_bc[:, 0:258]), waits=[tmm, tsbv, tk_hy_wr_ok, NS["rb_free"]])
            rel((bk,), tev)
            if "hpre" in tapap and tc == 0:
                tev = fence_v(dma("sync", tapap["hpre"][:, 258 * hc:258 * hc + 258],
                                  s_hy[:, 258 * hc:258 * hc + 258], waits=[tev],
                                  sem="dTP"))
            tt0 = sop(lambda e, st, h=hc: e.activation(
                out=s_tc0[:, 0:256], in_=s_hy[:, 258 * h:258 * h + 256],
                func=AF.Identity, bias=s_cb[:, h:h + 1], scale=s_cw[:, h:h + 1]),
                waits=[tev, tk_consts, prev_tt1])
            tt1 = vop(lambda e, st, h=hc: e.scalar_tensor_tensor(
                out=s_tc0[:, 258:258 + 256],
                in0=s_hy[:, 258 * h + 1:258 * h + 257],
                scalar=s_cw[:, 32 + h:32 + h + 1], in1=s_tc0[:, 0:256],
                op0=ALU.mult, op1=ALU.add), waits=[tt0, tev, prev_tt2])
            tt2 = vop(lambda e, st, h=hc: e.scalar_tensor_tensor(
                out=s_tcb[:, 0:256],
                in0=s_hy[:, 258 * h + 2:258 * h + 258],
                scalar=s_cw[:, 64 + h:64 + h + 1], in1=s_tc0[:, 258:258 + 256],
                op0=ALU.mult, op1=ALU.add), waits=[tt1, prev_silu])
            tt3 = sop(lambda e, st, h=hc: e.activation(
                out=s_hy[:, 258 * h:258 * h + 256],
                in_=s_tcb[:, 0:256], func=AF.Silu), waits=[tt2])
            prev_tt1, prev_tt2, prev_silu = tt1, tt2, tt3
            tk_z2.append(tt3)
        if "h" in tapap and tc == 0:
            tk_z2 = [fence_v(dma("sync", tapap["h"][:, :], s_hy[:, :], waits=tk_z2,
                                 sem="dTP"))] * 32
        ab = vop(lambda e, st: e.scalar_tensor_tensor(
            out=s_red[:, 0:256], in0=s_hy[:, 0:256], scalar=-1.0,
            in1=s_hy[:, 0:256], op0=ALU.mult, op1=ALU.max),
            waits=[tk_z2[0]] + mns)
        for hc in range(1, 32):
            t_a = vop(lambda e, st, h=hc: e.scalar_tensor_tensor(
                out=s_red[:, 258:258 + 256], in0=s_hy[:, 258 * h:258 * h + 256],
                scalar=-1.0, in1=s_hy[:, 258 * h:258 * h + 256],
                op0=ALU.mult, op1=ALU.max),
                waits=[tk_z2[hc], ab])
            ab = vop(lambda e, st: e.tensor_tensor(out=s_red[:, 0:256],
                     in0=s_red[:, 0:256], in1=s_red[:, 258:258 + 256], op=ALU.max),
                     waits=[t_a])
        ab2 = vop(lambda e, st: e.memset(s_red[:, 256:258], 0.0), waits=[ab, cps[2]])
        cps2, mns2 = crosspart_max([ab, ab2] + cps)
        r1 = vop(lambda e, st: e.tensor_scalar(out=s_scl[0:1, 774:1032],
                 in0=s_scl[0:1, 0:258], scalar1=1e-5, scalar2=None, op0=ALU.max),
                 waits=cps2 + [q6])
        r2 = vop(lambda e, st: e.reciprocal(out=s_scl[0:1, 0:258],
                 in_=s_scl[0:1, 774:1032]), waits=[r1])
        r3 = vop(lambda e, st: e.tensor_scalar(out=s_scl[0:1, 0:258],
                 in0=s_scl[0:1, 0:258], scalar1=127.0, scalar2=None, op0=ALU.mult),
                 waits=[r2])
        r4 = vop(lambda e, st: e.tensor_scalar(out=s_scl[0:1, 516:774],
                 in0=s_scl[0:1, 774:1032], scalar1=s_scl[0:1, 1033:1034],
                 scalar2=1.0 / 127.0, op0=ALU.mult, op1=ALU.mult),
                 waits=[r1, qstate[("w2", "mean")], tsbv])
        ts2v = bcast_sb(s_scl[0:1, 0:256], 256, 4, s_bc[:, 516:516 + 256],
                        [r3, tcb, tq8[-1]])
        tg2v = bcast_sb(s_scl[0:1, 516:516 + 256], 256, 5, s_bc[:, 258:258 + 256],
                        [r4, tsbv])
        tk_scl_guard = [ts2v, tg2v]
        tcv_prev = [None, None]
        tmm2 = {oc: None for oc in range(8)}
        last_mm = None
        ty8_last = None
        for hc in range(32):
            sl2 = hc % 2
            ty8 = vop(lambda e, st, h=hc, s=sl2: e.tensor_mul(
                s_y8r[:, 260 * s:260 * s + 256],
                s_hy[:, 258 * h:258 * h + 256], s_bc[:, 516:516 + 256]),
                waits=[ts2v, tk_z2[hc], tcv_prev[sl2], tk_y8ms])
            tcv = vop(lambda e, st, s=sl2: e.tensor_copy(
                out=s_ybr[:, 260 * s:260 * s + 256],
                in_=s_y8r[:, 260 * s:260 * s + 256]),
                waits=[ty8, (last_mm if hc >= 2 else None)])
            for oc in range(8):
                tmm2[oc] = mm(bank(oc, 256),
                              s_w2q[:, 1024 * hc + 128 * oc:1024 * hc + 128 * oc + 128],
                              s_ybr[:, 260 * sl2:260 * sl2 + 256], start=(hc == 0),
                              stop=(hc == 31), waits=[tcv, tk_w2q.get(hc)],
                              banks=(oc,))
            tcv_prev[sl2] = tmm2[7]
            last_mm = tmm2[7]
            ty8_last = ty8
        tk_hy_wr_ok = ty8_last
        for oc in range(8):
            so_sl = oc % 2
            te1 = vop(lambda e, st, b=oc: e.tensor_mul(
                s_tc0[:, 0:256], bank(b, 256), s_bc[:, 258:258 + 256]),
                waits=[tmm2[oc], tg2v, prev_te1])
            rel((oc,), te1)
            te2 = vop(lambda e, st, o=oc, t=tc, s=so_sl: e.tensor_add(
                s_out[:, 256 * s:256 * s + 256], s_tc0[:, 0:256],
                s_x1w[:, 258 * o + 1:258 * o + 257]),
                waits=[te1, tk_out_dma[so_sl], tk_x1w[oc]])
            prev_te1 = te2
            tko = dma("sync", fm(ap["out"])[:, oc, 256 * tc:256 * tc + 256],
                      s_out[:, 256 * so_sl:256 * so_sl + 256], waits=[te2],
                      sem="dO%d" % so_sl)
            tk_out_dma[so_sl] = tko
            tk_out_done.append(tko)
        tk_x1w_rd = tk_out_done[-1]
        tk_red_guard = mns2 + cps2
        tk_xqb_rd = tk_z2[-1]
        if "xq2" in tapap and tc == 0:
            dma("sync", tapap["xq2"][:, :], s_ybr[:, :],
                waits=[tcv_prev[0], tcv_prev[1]], sem="dTP")

    p.emit("sync", lambda e, st: e.nop(),
           waits=[p.tok(s) for s in ["dC","dX0","dX1","dTP","dW0","dW1","dWD0","dWD1","dKD","dZN","dZI","dZB","dWO0","dWO1","dXR0","dXR1","dO0","dO1","tT","tV","tS","tG","CC"]])
    return _finish(nc, ctx, p)


def _finish(nc, ctx, p):
    sems = {}
    for s in ["dC", "dX0", "dX1", "dTP", "dW0", "dW1", "dWD0", "dWD1", "dKD", "dZN", "dZI",
              "dZB", "dWO0", "dWO1", "dXR0", "dXR1", "dO0", "dO1",
              "tT", "tV", "tS", "tG", "CC"]:
        sems[s] = ctx.enter_context(nc.semaphore(s))
    blk = ctx.enter_context(nc.Block())
    state = {}

    @blk.sync
    def _(e):
        p.run("sync", e, sems, state)

    @blk.tensor
    def _(e):
        p.run("tensor", e, sems, state)

    @blk.vector
    def _(e):
        p.run("vector", e, sems, state)

    @blk.scalar
    def _(e):
        p.run("scalar", e, sems, state)

    @blk.gpsimd
    def _(e):
        p.run("gpsimd", e, sems, state)

    ctx.close()
    return nc



def kernel(**inputs):
    from concourse.bass_utils import run_bass_kernel_spmd
    maps = stage_inputs(inputs, n_cores=8)
    nc = build_nc(n_cores=8, n_heads=NB, stop_after="full", taps=())
    res = run_bass_kernel_spmd(nc, maps, core_ids=list(range(8)))
    return gather_out(res.results)


# version 06
# speedup vs baseline: 1.3506x; 1.0529x over previous
import numpy as np
import ml_dtypes
from contextlib import ExitStack
import concourse.bass as bass
import concourse.mybir as mybir

F32 = mybir.dt.float32
BF16 = mybir.dt.bfloat16
I8 = mybir.dt.int8
F8 = mybir.dt.float8e4
F16 = mybir.dt.float16
AF = mybir.ActivationFunctionType
ALU = mybir.AluOpType
AX = mybir.AxisListType
BF = ml_dtypes.bfloat16
F16N = np.float16

D = 1024
S = 4096
T = 2048
NB = 8
BS = 256
H = 4096
W = 2050
WZ = 4098
EPS = 1e-5


def qkv_perm():
    perm = np.empty(3072, np.int64)
    for sec in range(3):
        for g in range(16):
            perm[1024 * sec + 64 * g: 1024 * sec + 64 * g + 64] = \
                np.arange(192 * g + 64 * sec, 192 * g + 64 * sec + 64)
    return perm

def zrow_to_cglob():
    out = np.empty(1024, np.int64)
    for r in range(1024):
        rank, rr = divmod(r, 512)
        chunk, part = divmod(rr, 128)
        m = (4 if part >= 64 else 0) + chunk
        out[r] = 64 * (8 * rank + m) + part % 64
    return out

def stage_inputs(inputs, n_cores=8):
    x = np.asarray(inputs["L0_attn_x"], np.float32)
    Wqkv = np.asarray(inputs["L0_attn_W_qkv"], np.float32)
    bqkv = np.asarray(inputs["L0_attn_b_qkv"], np.float32)
    Wo = np.asarray(inputs["L0_attn_W_o"], np.float32)
    bo = np.asarray(inputs["L0_attn_b_o"], np.float32)
    w1 = np.asarray(inputs["L0_ffn_w1"], np.float32)
    cw = np.asarray(inputs["L0_ffn_conv_w"], np.float32)
    cb = np.asarray(inputs["L0_ffn_conv_b"], np.float32)
    w2 = np.asarray(inputs["L0_ffn_w2"], np.float32)

    perm = qkv_perm()
    com = dict(
        WqT=np.ascontiguousarray(Wqkv.T[:, perm]),
        bq_pp=np.ascontiguousarray(bqkv[perm].reshape(24, 128).T),
        WoT=np.ascontiguousarray(Wo.T[zrow_to_cglob(), :]),
        bo_pp=np.ascontiguousarray(bo.reshape(8, 128).T),
        w1T=np.ascontiguousarray(w1.T),
        w2T=np.ascontiguousarray(w2.T),
        cw_pp=np.ascontiguousarray(cw[:, 0, :].reshape(32, 128, 3).transpose(1, 2, 0).reshape(128, 96)),
        cb_pp=np.ascontiguousarray(cb.reshape(32, 128).T),
        ident=np.eye(128, dtype=F16N),
        ident32=np.eye(128, dtype=np.float32),
        ones1=np.ones((128, 1), dtype=F16N),
        ones32=np.ones((128, 128), dtype=np.float32),
    )
    maps = []
    for c in range(n_cores):
        b, off = divmod(c, 2)
        xr = np.zeros((D, W), np.float32)
        if off == 0:
            xr[:, 1:2050] = x[b, 0:2049, :].T
        else:
            xr[:, 0:2049] = x[b, 2047:4096, :].T
        mask_l = np.ones((1, 258), np.float32)
        mask_r = np.ones((1, 258), np.float32)
        if off == 0:
            mask_l[0, 0] = 0.0
        else:
            mask_r[0, 257] = 0.0
        maps.append(dict(com, x_res=xr, mask_l=mask_l, mask_r=mask_r))
    return maps

def gather_out(results):
    out = np.empty((4, S, D), np.float32)
    for c, r in enumerate(results):
        b, off = divmod(c, 2)
        out[b, T * off:T * off + T, :] = np.asarray(r["out"]).T
    return out


class P:
    ENGS = ("sync", "tensor", "vector", "scalar", "gpsimd")

    def __init__(self):
        self.q = {e: [] for e in self.ENGS}
        self.cnt = {}
        self.water = {e: {} for e in self.ENGS}
        self.bkg = [None] * 8

    def tok(self, sem):
        return (sem, self.cnt.get(sem, 0))

    def emit(self, eng, fn, waits=(), inc=None, amt=1):
        best = {}
        for smv in waits:
            if smv is None:
                continue
            s, v = smv
            if v > best.get(s, 0):
                best[s] = v
        wl = []
        for s, v in best.items():
            if v > self.water[eng].get(s, 0):
                self.water[eng][s] = v
                wl.append((s, v))
        tk = None
        if inc is not None:
            self.cnt[inc] = self.cnt.get(inc, 0) + amt
            tk = (inc, self.cnt[inc])
        self.q[eng].append((fn, wl, inc, amt))
        return tk

    def run(self, eng_name, handle, sems, state):
        for fn, wl, inc, amt in self.q[eng_name]:
            for s, v in wl:
                handle.wait_ge(sems[s], v)
            inst = fn(handle, state)
            if inc is not None:
                inst.then_inc(sems[inc], amt)


def build_nc(n_cores=8, n_heads=NB, stop_after="full", taps=(), groups=None):
    nc = bass.Bass("TRN2", target_bir_lowering=False)
    p = P()
    if groups is None:
        groups = [[i, i + 1] for i in range(0, n_cores, 2)]

    ap = {}
    for nm, sh, dt in [("x_res", [D, W], F32), ("WqT", [D, 3072], F32),
                       ("bq_pp", [128, 24], F32), ("WoT", [D, D], F32),
                       ("bo_pp", [128, 8], F32), ("w1T", [D, H], F32),
                       ("w2T", [H, D], F32), ("cw_pp", [128, 96], F32),
                       ("cb_pp", [128, 32], F32), ("ident", [128, 128], F16),
                       ("ident32", [128, 128], F32), ("ones1", [128, 1], F16),
                       ("ones32", [128, 128], F32),
                       ("mask_l", [1, 258], F32), ("mask_r", [1, 258], F32)]:
        ap[nm] = nc.dram_tensor(nm, sh, dt, kind="ExternalInput").ap()
    ap["out"] = nc.dram_tensor("out", [D, T], F32, kind="ExternalOutput").ap()
    zin = nc.dram_tensor("zin", [512, WZ], F16)
    zgath = nc.dram_tensor("zgath", [1024, WZ], F16)
    wodram = nc.dram_tensor("wodram", [1024, 1024], F16)

    tap_shapes = {"stats": ([1, 2 * W], F32), "Y": ([128, 24 * BS], F16),
                  "ln1": ([128, 8 * BS], F16), "zin": ([512, WZ], F16),
                  "x1w": ([128, 8 * 258], F32), "xq": ([128, 8 * 258], BF16),
                  "h": ([128, 32 * 258], F32), "xq2": ([128, 2 * 260], BF16),
                  "hpre": ([128, 32 * 258], F32), "w1q": ([128, 32768], F8)}
    tapap = {}
    for tname in taps:
        sh, dt = tap_shapes[tname]
        tapap[tname] = nc.dram_tensor("tap_" + tname, sh, dt, kind="ExternalOutput").ap()

    ctx = ExitStack()
    sbt = lambda n, s, d: ctx.enter_context(nc.sbuf_tensor(n, s, d))

    s_ident = sbt("s_ident", [128, 128], F16)
    s_id32 = sbt("s_id32", [128, 128], F32)
    s_ones = sbt("s_ones", [128, 1], F16)
    s_on32 = sbt("s_on32", [128, 128], F32)
    s_bq = sbt("s_bq", [128, 24], F32)
    s_bo = sbt("s_bo", [128, 8], F32)
    s_cw = sbt("s_cw", [128, 96], F32)
    s_cb = sbt("s_cb", [128, 32], F32)
    s_maskl = sbt("s_maskl", [1, 258], F32)
    s_maskr = sbt("s_maskr", [1, 258], F32)
    s_mr = sbt("s_mr", [1, 2 * W], F32)
    s_scl = sbt("s_scl", [1, 1040], F32)
    s_row = s_scl[0:1, 0:512]
    s_ws = sbt("s_ws", [128, 8], F32)

    AR1 = sbt("AR1", [128, 12288], F32)
    s_wq = AR1[:, 0:12288].bitcast(F16)
    s_w1q = AR1[:, 0:8192].bitcast(F8)
    s_xqb = AR1[:, 8192:9224].bitcast(BF16)
    s_y8r = AR1[:, 9224:9354].bitcast(I8)
    s_ybr = AR1[:, 9354:9614].bitcast(BF16)
    s_bc = AR1[:, 9614:10388]
    s_cast1 = AR1[:, 10388:10644].bitcast(BF16)
    s_cast2 = AR1[:, 10644:10900].bitcast(BF16)

    AR2 = sbt("AR2", [128, 16704], F32)
    s_xblk = AR2[:, 0:4096]
    s_ln1 = AR2[:, 4096:6144].bitcast(F16)
    s_xsq = AR2[:, 6144:7168].bitcast(F16)
    s_Y = AR2[:, 7168:10240].bitcast(F16)
    s_kdup = AR2[:, 10240:11264].bitcast(F16)
    s_vt = AR2[:, 11264:12304].bitcast(BF16)
    s_E = AR2[:, 12304:14352].bitcast(BF16)
    s_zn = AR2[:, 14352:14864].bitcast(F16)
    s_zr = AR2[:, 14864:15888]
    s_qd = AR2[:, 6144:7168].bitcast(F16)
    s_atmp = AR2[:, 15888:16144]
    s_abc = AR2[:, 16144:16656]
    s_x1w = AR2[:, 0:2064]
    s_lt = AR2[:, 2064:4128]
    s_xq8 = AR2[:, 4128:4644].bitcast(I8)
    s_w8t = AR2[:, 4128:4640].bitcast(I8)
    s_tc0 = AR2[:, 4644:5160]
    s_red = AR2[:, 5160:5684]
    s_tcb = AR2[:, 5684:5942]
    s_out = AR2[:, 5942:6454]
    s_xr = AR2[:, 6454:7478]
    s_wos = AR2[:, 7478:8502].bitcast(F16)
    s_w2q = AR2[:, 8502:16694].bitcast(F8)

    AR3 = sbt("AR3", [128, 8256], F32)
    s_wst = AR3[:, 0:4096]
    s_hy = AR3[:, 0:8256]
    OT = AR3[:, 4096:6144]

    ARZ = sbt("ARZ", [128, 8200], F32)
    s_z = ARZ[:, :].bitcast(F16)

    ps = ctx.enter_context(nc.psum_tensor("ps", [128, 4096], F32))

    def bank(i, cols=512, n0=0):
        return ps[:, 512 * i + n0: 512 * i + n0 + cols]

    def dma(eng, out, in_, waits=(), sem=None):
        assert sem is not None
        return p.emit(eng, lambda e, st, o=out, i=in_: e.dma_start(out=o, in_=i),
                      waits=waits, inc=sem, amt=16)

    def mm(out, lhsT, rhs, start, stop, waits=(), banks=()):
        w = list(waits)
        if start:
            w += [p.bkg[b] for b in banks]
        return p.emit("tensor",
                      lambda e, st, o=out, l=lhsT, r=rhs, a=start, b=stop:
                      e.matmul(o, l, r, start=a, stop=b), waits=w, inc="tT")

    def tr(out, in_, idt, waits=(), banks=()):
        w = list(waits) + [p.bkg[b] for b in banks]
        return p.emit("tensor", lambda e, st, o=out, i=in_, d=idt: e.transpose(o, i, d),
                      waits=w, inc="tT")

    def rel(banks, tok):
        for b in banks:
            p.bkg[b] = tok

    def vop(fn, waits=()):
        return p.emit("vector", fn, waits=waits, inc="tV")

    def sop(fn, waits=()):
        return p.emit("scalar", fn, waits=waits, inc="tS")

    def pop(fn, waits=()):
        return p.emit("gpsimd", fn, waits=waits, inc="tG")

    def fence_v(tok):
        return vop(lambda e, st: e.memset(s_ws[:, 6:7], 0.0), waits=[tok])

    def fm(dram_ap):
        return dram_ap.rearrange("(c p) w -> p c w", p=128)

    for o_, i_ in [(s_ident[:, :], ap["ident"][:, :]), (s_id32[:, :], ap["ident32"][:, :]),
                   (s_ones[:, :], ap["ones1"][:, :]), (s_on32[:, :], ap["ones32"][:, :]),
                   (s_bq[:, :], ap["bq_pp"][:, :]), (s_bo[:, :], ap["bo_pp"][:, :]),
                   (s_cw[:, :], ap["cw_pp"][:, :]), (s_cb[:, :], ap["cb_pp"][:, :]),
                   (s_maskl[:, :], ap["mask_l"][:, :]), (s_maskr[:, :], ap["mask_r"][:, :])]:
        dma("sync", o_, i_, sem="dC")
    tk_consts = p.tok("dC")
    tk_id = tk_id32 = tk_ones = tk_on32 = tk_consts
    tk_vtones = vop(lambda e, st: e.memset(s_vt[:, 64:32 * 65:65], 1.0))
    tk_zpad = vop(lambda e, st: e.memset(s_z[:, :], 0.0))
    tk_y8ms = vop(lambda e, st: e.memset(s_y8r[:, :], 0))
    tk_mrms = vop(lambda e, st: e.memset(s_mr[0:1, :], 0.0))
    vop(lambda e, st: e.memset(s_ws[:, 4:5], -80.0))
    tk_expc = vop(lambda e, st: e.memset(s_ws[:, 5:6], 0.125))

    def bcast_sb(row_ap, n, bnk, dst, waits):
        tm = mm(bank(bnk, n), s_on32[0:1, 0:128], row_ap, True, True,
                waits=list(waits) + [tk_on32], banks=(bnk,))
        tv = vop(lambda e, st, d=dst, b=bnk, k=n: e.tensor_copy(out=d, in_=bank(b, k)),
                 waits=[tm])
        rel((bnk,), tv)
        return tv

    def bcast_ps(row_ap, n, bnk, waits):
        return mm(bank(bnk, n), s_on32[0:1, 0:128], row_ap, True, True,
                  waits=list(waits) + [tk_on32], banks=(bnk,))

    xr3 = fm(ap["x_res"])
    WqT3 = fm(ap["WqT"])
    WoT3 = fm(ap["WoT"])
    w1T3 = fm(ap["w1T"])
    w2T3 = fm(ap["w2T"])
    wod3 = fm(wodram.ap())

    xslot_free = [None, None]
    tk_stats = []
    for j in range(NB):
        sl = j % 2
        xs = s_xblk[:, 2048 * sl:2048 * sl + 2048]
        lns = s_ln1[:, 2048 * sl:2048 * sl + 2048]
        tkx = dma("sync", xs.rearrange("p (c w) -> p c w", c=8),
                  xr3[:, :, 1 + BS * j:1 + BS * j + BS], waits=[xslot_free[sl]],
                  sem="dX%d" % sl)
        tkb = vop(lambda e, st, o=lns, i=xs: e.tensor_copy(out=o, in_=i), waits=[tkx])
        tkq = vop(lambda e, st, i=lns: e.tensor_mul(s_xsq[:, :], i, i),
                  waits=[tkb, tk_stats[-1] if tk_stats else None])
        tm1 = tm2 = None
        for dc in range(8):
            tm1 = mm(ps[0:1, 1024:1024 + 256], s_ones[:, 0:1],
                     lns[:, 256 * dc:256 * dc + 256], start=(dc == 0), stop=(dc == 7),
                     waits=[tkb, tk_ones], banks=(2,))
            tm2 = mm(ps[0:1, 1536:1536 + 256], s_ones[:, 0:1],
                     s_xsq[:, 256 * dc:256 * dc + 256], start=(dc == 0), stop=(dc == 7),
                     waits=[tkq], banks=(3,))
        t1 = vop(lambda e, st, j=j: e.tensor_scalar(
            out=s_mr[0:1, BS * j:BS * j + BS], in0=ps[0:1, 1024:1024 + 256],
            scalar1=1.0 / D, scalar2=None, op0=ALU.mult), waits=[tm1, tk_mrms])
        t2 = vop(lambda e, st, j=j: e.tensor_scalar(
            out=s_mr[0:1, W + BS * j:W + BS * j + BS], in0=ps[0:1, 1536:1536 + 256],
            scalar1=1.0 / D, scalar2=None, op0=ALU.mult), waits=[tm2])
        rel((2,), t1)
        rel((3,), t2)
        t3 = vop(lambda e, st, j=j: e.tensor_mul(
            s_row[0:1, 0:BS], s_mr[0:1, BS * j:BS * j + BS],
            s_mr[0:1, BS * j:BS * j + BS]), waits=[t1])
        t4 = vop(lambda e, st, j=j: e.tensor_sub(
            s_mr[0:1, W + BS * j:W + BS * j + BS],
            s_mr[0:1, W + BS * j:W + BS * j + BS], s_row[0:1, 0:BS]), waits=[t2, t3])
        t5 = vop(lambda e, st, j=j: e.tensor_scalar(
            out=s_mr[0:1, W + BS * j:W + BS * j + BS],
            in0=s_mr[0:1, W + BS * j:W + BS * j + BS],
            scalar1=EPS, scalar2=None, op0=ALU.add), waits=[t4])
        t6 = vop(lambda e, st, j=j: e.reciprocal(
            out=s_mr[0:1, W + BS * j:W + BS * j + BS],
            in_=s_mr[0:1, W + BS * j:W + BS * j + BS]), waits=[t5])
        tk_stats.append(t6)
        xslot_free[sl] = tkb
    tk_rstd = sop(lambda e, st: e.activation(
        out=s_mr[0:1, W:W + T], in_=s_mr[0:1, W:W + T], func=AF.Sqrt),
        waits=[tk_stats[-1]])

    if "stats" in tapap:
        dma("sync", tapap["stats"][0:1, :], s_mr[0:1, :], waits=[tk_rstd], sem="dTP")

    wst_free = [None, None]
    slot_ctr = [0]
    qstate = {}

    def sub_chunks(width):
        return [(a, min(2048, width - a)) for a in range(0, width, 2048)]

    def j_cast(src_sl, dst_sl, toks, key):
        def go():
            sl = slot_ctr[0] & 1
            slot_ctr[0] += 1
            n = src_sl.shape[-1]
            tkd = dma("sync", s_wst[:, 2048 * sl:2048 * sl + n], src_sl,
                      waits=[wst_free[sl]], sem="dW%d" % sl)
            tkv = vop(lambda e, st, s=sl, d=dst_sl, nn=n: e.tensor_copy(
                out=d, in_=s_wst[:, 2048 * s:2048 * s + nn]), waits=[tkd])
            wst_free[sl] = tkv
            toks[key] = tkv
        return go

    def j_q1(src_sl, key, last, nel):
        def go():
            sl = slot_ctr[0] & 1
            slot_ctr[0] += 1
            n = src_sl.shape[-1]
            tkd = dma("sync", s_wst[:, 2048 * sl:2048 * sl + n], src_sl,
                      waits=[wst_free[sl]], sem="dW%d" % sl)
            tkr = vop(lambda e, st, s=sl, nn=n: e.reduce_sum(
                out=s_ws[:, 3:4], in_=s_wst[:, 2048 * s:2048 * s + nn],
                axis=AX.X, apply_absolute_value=True),
                waits=[tkd, qstate.get((key, "redfree"))])
            if (key, "acc") not in qstate:
                ta = vop(lambda e, st: e.tensor_copy(out=s_ws[:, 2:3], in_=s_ws[:, 3:4]),
                         waits=[tkr])
            else:
                ta = vop(lambda e, st: e.tensor_add(
                    s_ws[:, 2:3], s_ws[:, 2:3], s_ws[:, 3:4]),
                    waits=[tkr, qstate.get((key, "acc"))])
            qstate[(key, "acc")] = ta
            qstate[(key, "redfree")] = ta
            wst_free[sl] = tkr
            if last:
                tt = tr(ps[0:1, 1024:1152], s_ws[:, 2:3], s_id32[:, :],
                        waits=[ta, tk_id32], banks=(2,))
                tsum = vop(lambda e, st: e.reduce_sum(
                    out=s_scl[0:1, 1036:1037], in_=ps[0:1, 1024:1152], axis=AX.X),
                    waits=[tt, qstate.get("rowfree")])
                rel((2,), tsum)
                mcol = 1032 if key == "w1" else 1033
                tmean = vop(lambda e, st, mc=mcol, ne=nel: e.tensor_scalar(
                    out=s_scl[0:1, mc:mc + 1], in0=s_scl[0:1, 1036:1037],
                    scalar1=1.0 / ne, scalar2=None, op0=ALU.mult), waits=[tsum])
                trec = vop(lambda e, st, mc=mcol: e.reciprocal(
                    out=s_scl[0:1, 1037:1038], in_=s_scl[0:1, mc:mc + 1]),
                    waits=[tmean])
                wcol = 0 if key == "w1" else 1
                tbm = mm(ps[:, 1024:1025], s_on32[0:1, 0:128], s_scl[0:1, 1037:1038],
                         True, True, waits=[trec, tk_on32], banks=(2,))
                tbv = vop(lambda e, st, wc=wcol: e.tensor_copy(
                    out=s_ws[:, wc:wc + 1], in_=ps[:, 1024:1025]), waits=[tbm])
                rel((2,), tbv)
                qstate[(key, "winv")] = tbv
                qstate[(key, "mean")] = tmean
                qstate["rowfree"] = trec
        return go

    def j_q2(src_sl, dst_sl, key, toks, tkey):
        def go():
            sl = slot_ctr[0] & 1
            slot_ctr[0] += 1
            n = src_sl.shape[-1]
            wcol = 0 if key == "w1" else 1
            tkd = dma("sync", s_wst[:, 2048 * sl:2048 * sl + n], src_sl,
                      waits=[wst_free[sl]], sem="dW%d" % sl)
            ta = vop(lambda e, st, s=sl, nn=n, wc=wcol: e.tensor_scalar(
                out=s_wst[:, 2048 * s:2048 * s + nn],
                in0=s_wst[:, 2048 * s:2048 * s + nn],
                scalar1=s_ws[:, wc:wc + 1], scalar2=None, op0=ALU.mult),
                waits=[tkd, qstate[(key, "winv")]])
            tc8 = vop(lambda e, st, s=sl, nn=n: e.tensor_scalar(
                out=s_w8t[:, 0:nn], in0=s_wst[:, 2048 * s:2048 * s + nn],
                scalar1=-1.0, scalar2=1.0, op0=ALU.max, op1=ALU.min),
                waits=[ta, qstate.get("w8free")])
            td = vop(lambda e, st, d=dst_sl, nn=n: e.tensor_copy(
                out=d, in_=s_w8t[:, 0:nn]), waits=[tc8])
            qstate["w8free"] = td
            wst_free[sl] = tc8
            toks[tkey] = td
        return go

    jobsA = []
    tk_wq = {}
    for kc in range(8):
        for (a, n) in sub_chunks(3072):
            jobsA.append(j_cast(WqT3[:, kc, a:a + n],
                                s_wq[:, 3072 * kc + a:3072 * kc + a + n], tk_wq,
                                (kc, a)))
    w1subs = [(kc, a, n) for kc in range(8) for (a, n) in sub_chunks(4096)]
    for i, (kc, a, n) in enumerate(w1subs):
        jobsA.append(j_q1(w1T3[:, kc, a:a + n], "w1", i == len(w1subs) - 1, D * H))
    for i, kc in enumerate(range(32)):
        jobsA.append(j_q1(w2T3[:, kc, :], "w2", i == 31, D * H))

    jobsB0 = []
    tk_wocast = {}
    for kc in range(8):
        def mk(kc=kc):
            def go():
                sl = slot_ctr[0] & 1
                slot_ctr[0] += 1
                tkd = dma("sync", s_wst[:, 2048 * sl:2048 * sl + 1024],
                          WoT3[:, kc, :], waits=[wst_free[sl]], sem="dW%d" % sl)
                wsl = kc % 2
                tkv = vop(lambda e, st, s=sl, ws=wsl: e.tensor_copy(
                    out=s_wos[:, 1024 * ws:1024 * ws + 1024],
                    in_=s_wst[:, 2048 * s:2048 * s + 1024]),
                    waits=[tkd, qstate.get(("wos", wsl))])
                tks = dma("sync", wod3[:, kc, :],
                          s_wos[:, 1024 * wsl:1024 * wsl + 1024], waits=[tkv],
                          sem="dWD%d" % wsl)
                qstate[("wos", wsl)] = tks
                wst_free[sl] = tkv
                tk_wocast[kc] = tks
            return go
        jobsB0.append(mk())
    tk_w1q = {}
    for kc in range(8):
        for (a, n) in sub_chunks(4096):
            jobsB0.append(j_q2(w1T3[:, kc, a:a + n],
                               s_w1q[:, 4096 * kc + a:4096 * kc + a + n], "w1",
                               tk_w1q, kc))
    jobsB1 = []
    tk_w2q = {}
    for kc in range(32):
        jobsB1.append(j_q2(w2T3[:, kc, :], s_w2q[:, 1024 * kc:1024 * kc + 1024], "w2",
                           tk_w2q, kc))

    for jb in jobsA[:16]:
        jb()
    jobsA = jobsA[16:]
    nh = min(n_heads, NB)
    per_block = (len(jobsA) + nh - 1) // max(nh, 1)

    lnslot_free = [None, None]
    tk_Y_free = None
    zloc_last = None
    NS = {"prev_av": {(0, 0): None, (0, 1): None, (1, 0): None, (1, 1): None},
          "av_last": [None, None], "zn_free": None,
          "rb_free": None, "zr_free": [None, None], "ot_free": [None, None],
          "pend_norm": None, "zloc_last": None}

    def flush_norm():
        qcn, jn, pend = NS["pend_norm"]
        pmn = 64 * (jn // 4)
        zchn = jn % 4
        for (par, tevO, tevZ) in pend:
            obank = 4 + 2 * par
            zo = 1024 * par
            trb1 = mm(ps[0:64, 512 * obank:512 * obank + 512], s_on32[64:65, 0:64],
                      OT[64:65, zo:zo + 512], True, True, waits=[tevZ, tk_on32],
                      banks=(obank,))
            trb2 = mm(ps[0:64, 512 * (obank + 1):512 * (obank + 1) + 512],
                      s_on32[64:65, 0:64], OT[64:65, zo + 512:zo + 1024], True, True,
                      waits=[], banks=(obank + 1,))
            trbe = vop(lambda e, st, b=obank: e.tensor_copy(
                out=s_zr[0:64, 0:1024], in_=ps[0:64, 512 * b:512 * b + 1024]),
                waits=[trb1, trb2, NS["rb_free"]])
            rel((obank, obank + 1), trbe)
            NS["zr_free"][par] = trbe
            trec = vop(lambda e, st: e.reciprocal(
                out=s_zr[0:64, 0:1024], in_=s_zr[0:64, 0:1024]), waits=[trbe])
            tdiv = vop(lambda e, st, pp=par: e.tensor_mul(
                s_zn[0:64, 0:1024], OT[0:64, 1024 * pp:1024 * pp + 1024],
                s_zr[0:64, 0:1024]), waits=[trec, tevO, NS["zn_free"]])
            NS["rb_free"] = tdiv
            NS["ot_free"][par] = tdiv
            if jn < 4:
                srcz = s_zn[0:64, :]
                tk_shift = tdiv
            else:
                tk_shift = dma("gpsimd", s_zn[64:128, 0:1024], s_zn[0:64, 0:1024],
                               waits=[tdiv, NS["zn_free"]], sem="dZN")
                srcz = s_zn[64:128, :]
            evs = []
            for i in range(4):
                g = par + 2 * (4 * qcn + i)
                st0 = zchn * WZ + 1 + g
                evs.append(vop(lambda e, st, ii=i, s0=st0, sz=srcz, pb=pmn:
                    e.tensor_copy(
                        out=s_z[pb:pb + 64, s0:s0 + 16 * 255 + 1:16],
                        in_=sz[:, 256 * ii:256 * ii + 256]),
                    waits=[tk_shift, tk_zpad]))
            NS["zn_free"] = evs[-1]
            NS["zloc_last"] = evs[-1]
        NS["pend_norm"] = None

    for j in range(nh):
        sl = j % 2
        xs = s_xblk[:, 2048 * sl:2048 * sl + 2048]
        lns = s_ln1[:, 2048 * sl:2048 * sl + 2048]
        tkx = dma("sync", xs.rearrange("p (c w) -> p c w", c=8),
                  xr3[:, :, 1 + BS * j:1 + BS * j + BS], waits=[xslot_free[sl]],
                  sem="dX%d" % sl)
        tva = bcast_sb(s_mr[0:1, BS * j:BS * j + BS], BS, 2, s_abc[:, 0:256],
                       [tk_rstd, tk_Y_free])
        tvb = bcast_sb(s_mr[0:1, W + BS * j:W + BS * j + BS], BS, 3,
                       s_abc[:, 256:512], [tk_rstd])
        tk_norm = []
        tprev2 = None
        for dc in range(8):
            tn1 = vop(lambda e, st, d=dc, x=xs: e.tensor_sub(
                s_atmp[:, 0:256], x[:, 256 * d:256 * d + 256], s_abc[:, 0:256]),
                waits=[tva, tkx, lnslot_free[sl], tprev2])
            tn2 = vop(lambda e, st, d=dc, l=lns: e.tensor_mul(
                l[:, 256 * d:256 * d + 256], s_atmp[:, 0:256], s_abc[:, 256:512]),
                waits=[tvb, tn1])
            tprev2 = tn2
            tk_norm.append(tn2)
        xslot_free[sl] = tk_norm[-1]
        if "ln1" in tapap and j == 0:
            tk_norm[-1] = fence_v(dma("sync", tapap["ln1"][:, :], lns[:, :],
                                      waits=[tk_norm[-1]], sem="dTP"))

        tk_evY = [None, None]
        tk_kd1 = tk_kd2 = None
        for cc in range(24):
            bk = cc % 2
            tmm = None
            for dc in range(8):
                tmm = mm(bank(bk, 256),
                         s_wq[:, 3072 * dc + 128 * cc:3072 * dc + 128 * cc + 128],
                         lns[:, 256 * dc:256 * dc + 256], start=(dc == 0), stop=(dc == 7),
                         waits=[tk_norm[dc], tk_wq.get((dc, 0)), tk_wq.get((dc, 2048)),
                                tk_Y_free], banks=(bk,))
            tev = vop(lambda e, st, c=cc, b=bk: e.tensor_scalar(
                out=s_Y[:, 256 * c:256 * c + 256], in0=bank(b, 256),
                scalar1=s_bq[:, c:c + 1], scalar2=None, op0=ALU.add),
                waits=[tmm, tk_consts])
            rel((bk,), tev)
            tk_evY[bk] = tev
            if cc == 7:
                tk_kd2 = dma("gpsimd", s_qd[0:64, :], s_Y[64:128, 0:2048],
                             waits=[tev, tk_Y_free], sem="dKD")
            if cc == 15:
                tk_kd1 = dma("gpsimd", s_kdup[0:64, :], s_Y[64:128, 2048:4096],
                             waits=[tev, tk_Y_free], sem="dKD")
        tk_kd2 = tk_kd1
        tk_Yall = [tk_evY[0], tk_evY[1]]
        lnslot_free[sl] = tk_Yall[1]
        if "Y" in tapap and j == 0:
            tk_Yall = [fence_v(dma("sync", tapap["Y"][:, :], s_Y[:, :],
                                   waits=tk_Yall, sem="dTP"))]

        tk_vt = []
        tk_vtev = None
        for grp in range(8):
            tts = []
            for t4 in range(4):
                kc = 4 * grp + t4
                g, hf = kc // 2, kc % 2
                pb = 64 * (g % 2)
                src = s_Y[pb:pb + 64, 4096 + 256 * (g // 2) + 128 * hf:
                          4096 + 256 * (g // 2) + 128 * hf + 128]
                idt = s_ident[pb:pb + 64, pb:pb + 64]
                dst = ps[:, 512 * (4 + t4):512 * (4 + t4) + 32].bitcast(F16)
                tts.append(tr(dst, src, idt,
                              waits=tk_Yall + [tk_id, tk_Y_free], banks=(4 + t4,)))
            tk_vtev = vop(lambda e, st, gg=grp: e.tensor_copy(
                out=s_vt[:, 65 * 4 * gg:65 * 4 * gg + 65 * 4]
                    .rearrange("p (k c) -> p k c", k=4)[:, :, 0:64],
                in_=ps[:, 2048:4096].bitcast(F16)
                    .rearrange("p (k c) -> p k c", k=4)[:, :, 0:64]),
                waits=tts + [tk_vtones, tk_Y_free])
            rel((4, 5, 6, 7), tk_vtev)
            tk_vt.append(tk_vtev)

        pm = 64 * (j // 4)
        zch = j % 4
        texp_of = {}

        def emit_av(kc, texp_of=texp_of, tk_vt=tk_vt):
            slot = kc % 2
            for par in range(2):
                obank = 4 + 2 * par
                tav = None
                for half in range(2):
                    tav = mm(ps[0:65, 512 * obank + 512 * half:
                                 512 * obank + 512 * half + 512],
                             s_vt[:, 65 * kc:65 * kc + 65],
                             s_E[:, 2048 * slot + 1024 * par + 512 * half:
                                 2048 * slot + 1024 * par + 512 * half + 512],
                             start=(kc == 0), stop=(kc == 31),
                             waits=[texp_of[kc][par], tk_vt[kc // 4]],
                             banks=(obank, obank + 1))
                NS["av_last"][par] = tav
                NS["prev_av"][(slot, par)] = tav

        for qc in range(2):
            tk_O = [None, None]
            for kc in range(32):
                g_k, hf = kc // 2, kc % 2
                slot = kc % 2
                if (g_k % 2) == 0:
                    klhs = s_Y[0:64, 2048 + 256 * (g_k // 2) + 128 * hf:
                               2048 + 256 * (g_k // 2) + 128 * hf + 128]
                    kw = tk_Yall
                else:
                    klhs = s_kdup[0:64, 256 * (g_k // 2) + 128 * hf:
                                  256 * (g_k // 2) + 128 * hf + 128]
                    kw = [tk_kd1]
                ts4 = []
                for par in range(2):
                    if par == 0:
                        qrhs = s_Y[0:64, 1024 * qc:1024 * qc + 1024]
                        qw = tk_Yall
                    else:
                        qrhs = s_qd[0:64, 1024 * qc:1024 * qc + 1024]
                        qw = [tk_kd2]
                    sbk = 2 * par
                    ts4.append(mm(bank(sbk), klhs, qrhs[:, 0:512], True, True,
                                  waits=list(kw) + list(qw), banks=(sbk,)))
                    ts4.append(mm(bank(sbk + 1), klhs, qrhs[:, 512:1024], True, True,
                                  waits=[], banks=(sbk + 1,)))
                texp0 = sop(lambda e, st, sl2=slot: e.activation(
                    out=s_E[:, 2048 * sl2:2048 * sl2 + 1024],
                    in_=ps[:, 0:1024], func=AF.Exp,
                    scale=s_ws[:, 5:6], bias=s_ws[:, 4:5]),
                    waits=ts4[0:2] + [NS["prev_av"][(slot, 0)], tk_expc])
                rel((0, 1), texp0)
                texp1 = sop(lambda e, st, sl2=slot: e.activation(
                    out=s_E[:, 2048 * sl2 + 1024:2048 * sl2 + 2048],
                    in_=ps[:, 1024:2048], func=AF.Exp,
                    scale=s_ws[:, 5:6], bias=s_ws[:, 4:5]),
                    waits=ts4[2:4] + [NS["prev_av"][(slot, 1)], tk_expc])
                rel((2, 3), texp1)
                texp_of[kc] = (texp0, texp1)
                if kc == 1 and NS["pend_norm"]:
                    flush_norm()
                if kc > 0:
                    emit_av(kc - 1)
            emit_av(31)
            for par in range(2):
                tk_O[par] = NS["av_last"][par]
            pend = []
            for par in range(2):
                obank = 4 + 2 * par
                tevO = vop(lambda e, st, b=obank, pp=par: e.tensor_copy(
                    out=OT[0:64, 1024 * pp:1024 * pp + 1024],
                    in_=ps[0:64, 512 * b:512 * b + 1024]),
                    waits=[tk_O[par], NS["ot_free"][par]])
                tevZ = vop(lambda e, st, b=obank, pp=par: e.tensor_copy(
                    out=OT[64:65, 1024 * pp:1024 * pp + 1024],
                    in_=ps[64:65, 512 * b:512 * b + 1024]),
                    waits=[tk_O[par], NS["zr_free"][par]])
                rel((obank, obank + 1), tevZ)
                pend.append((par, tevO, tevZ))
            NS["pend_norm"] = (qc, j, pend)
            if qc == 1 and j == nh - 1:
                flush_norm()
        zloc_last = NS["zloc_last"]
        tk_Y_free = NS["av_last"][1]

        for _ in range(per_block):
            if jobsA:
                jobsA.pop(0)()
    while jobsA:
        jobsA.pop(0)()

    tk_zst = []
    for ch in range(4):
        tk_zst.append(dma("gpsimd", zin[128 * ch:128 * ch + 128, :],
                          s_z[:, ch * WZ:ch * WZ + WZ], waits=[zloc_last], sem="dZI"))
    if "zin" in tapap:
        for ch in range(4):
            tkzt = dma("sync", tapap["zin"][128 * ch:128 * ch + 128, :],
                       s_z[:, ch * WZ:ch * WZ + WZ], waits=[zloc_last], sem="dTP")
        zloc_last = fence_v(tkzt)

    if stop_after == "A":
        ms = vop(lambda e, st: e.memset(s_out[:, 0:512], 0.0), waits=[zloc_last])
        tko = dma("sync", ap["out"][0:128, 0:512], s_out[:, 0:512], waits=[ms], sem="dO0")
        p.emit("sync", lambda e, st: e.nop(),
               waits=[tko] + [p.tok(s) for s in ["dC","dX0","dX1","dTP","dW0","dW1","dWD0","dWD1","dKD","dZN","dZI","dZB","dWO0","dWO1","dXR0","dXR1","dO0","dO1","tT","tV","tS","tG","CC"]])
        return _finish(nc, ctx, p)

    tk_cc = p.emit("gpsimd", lambda e, st: e.collective_compute(
        "AllGather", ALU.bypass, replica_groups=groups,
        ins=[zin.ap().opt()], outs=[zgath.ap().opt()]),
        waits=tk_zst, inc="CC", amt=1)

    for jb in jobsB0:
        jb()

    tk_pid = p.emit("gpsimd", lambda e, st: (
        st.__setitem__("T0", e.compute_val((e.partition_id() % 2) * T)), e.nop())[-1],
        waits=[], inc="tG")
    tk_zb = []
    for zc in range(8):
        tk_zb.append(p.emit("gpsimd", lambda e, st, z=zc: e.dma_start(
            out=s_z[:, 2050 * z:2050 * z + 2050],
            in_=zgath[128 * z:128 * z + 128, bass.ds(st["T0"], 2050)]),
            waits=[tk_cc, tk_pid, zloc_last], inc="dZB", amt=16))

    wos_free = [qstate.get(("wos", 0)), qstate.get(("wos", 1))]
    tk_woL = {}

    def wo_load(oc):
        wsl = oc % 2
        tkd = dma("gpsimd", s_wos[:, 1024 * wsl:1024 * wsl + 1024], wod3[:, oc, :],
                  waits=[wos_free[wsl], tk_wocast.get(oc), zloc_last],
                  sem="dWO%d" % wsl)
        tk_woL[oc] = (tkd, wsl)
        return tkd, wsl

    xr_free = [None, None]
    for jb in jobsB1:
        jb()

    if stop_after == "B0":
        p.emit("sync", lambda e, st: e.nop(),
               waits=[p.tok(s) for s in ["dC","dX0","dX1","dTP","dW0","dW1","dWD0","dWD1","dKD","dZN","dZI","dZB","dWO0","dWO1","dXR0","dXR1","dO0","dO1","tT","tV","tS","tG","CC"]])
        return _finish(nc, ctx, p)

    def crosspart_max(waits):
        tt1 = tr(ps[:, 512 * 6:512 * 6 + 128], s_red[:, 0:128], s_id32[:, :],
                 waits=list(waits) + [tk_id32], banks=(6,))
        tt2 = tr(ps[:, 512 * 7:512 * 7 + 128], s_red[:, 128:256], s_id32[:, :],
                 waits=list(waits), banks=(7,))
        tt3 = tr(ps[0:2, 512 * 5:512 * 5 + 128], s_red[:, 256:258], s_id32[:, :],
                 waits=list(waits), banks=(5,))
        r1 = vop(lambda e, st: e.reduce_max(out=s_red[:, 520:521],
                 in_=ps[:, 512 * 6:512 * 6 + 128], axis=AX.X), waits=[tt1])
        r2 = vop(lambda e, st: e.reduce_max(out=s_red[:, 521:522],
                 in_=ps[:, 512 * 7:512 * 7 + 128], axis=AX.X), waits=[tt2])
        r3 = vop(lambda e, st: e.reduce_max(out=s_red[0:2, 522:523],
                 in_=ps[0:2, 512 * 5:512 * 5 + 128], axis=AX.X), waits=[tt3])
        rel((6,), r1)
        rel((7,), r2)
        rel((5,), r3)
        m1 = tr(ps[0:1, 512 * 6:512 * 6 + 128], s_red[:, 520:521], s_id32[:, :],
                waits=[r1], banks=(6,))
        m2 = tr(ps[0:1, 512 * 7:512 * 7 + 128], s_red[:, 521:522], s_id32[:, :],
                waits=[r2], banks=(7,))
        m3 = tr(ps[0:1, 512 * 5:512 * 5 + 2], s_red[0:2, 522:523], s_id32[0:2, 0:2],
                waits=[r3], banks=(5,))
        c1 = vop(lambda e, st: e.tensor_copy(
            out=s_scl[0:1, 0:128], in_=ps[0:1, 512 * 6:512 * 6 + 128]), waits=[m1])
        c2 = vop(lambda e, st: e.tensor_copy(
            out=s_scl[0:1, 128:256], in_=ps[0:1, 512 * 7:512 * 7 + 128]), waits=[m2])
        c3 = vop(lambda e, st: e.tensor_copy(
            out=s_scl[0:1, 256:258], in_=ps[0:1, 512 * 5:512 * 5 + 2]), waits=[m3])
        rel((6,), c1)
        rel((7,), c2)
        rel((5,), c3)
        return [c1, c2, c3], [m1, m2, m3]

    tk_lt_rd = None
    st_guard = [None, None]
    row_guard = qstate.get("rowfree")
    tk_red_guard = []
    tk_scl_guard = [qstate.get("rowfree")]
    tk_xq8_rd = [tk_w2q.get(31)]
    tk_xqb_rd = None
    tk_hy_wr_ok = tk_w2q.get(31)
    tk_x1w_rd = None
    tk_out_dma = [None, None]
    tk_out_done = []
    prev_te1 = None
    for tc in range(8):
        j0 = 256 * tc
        tk_x1w = []
        tk_wm = {}
        for zc in range(8):
            tkwd, wsl = wo_load(zc)
            for oc in range(8):
                tk_wm[oc] = mm(ps[:, 512 * oc:512 * oc + 258],
                               s_wos[:, 1024 * wsl + 128 * oc:1024 * wsl + 128 * oc + 128],
                               s_z[:, 2050 * zc + j0:2050 * zc + j0 + 258],
                               start=(zc == 0), stop=(zc == 7),
                               waits=[tkwd, tk_zb[7]], banks=(oc,))
            wos_free[wsl] = tk_wm[7]
        for oc in range(8):
            sl = oc % 2
            tkxr = dma("sync", s_xr[:, 512 * sl:512 * sl + 258],
                       xr3[:, oc, j0:j0 + 258], waits=[xr_free[sl]], sem="dXR%d" % sl)
            tstt = vop(lambda e, st, o=oc: e.scalar_tensor_tensor(
                out=s_x1w[:, 258 * o:258 * o + 258], in0=ps[:, 512 * o:512 * o + 258],
                scalar=s_bo[:, o:o + 1], in1=s_xr[:, 512 * (o % 2):512 * (o % 2) + 258],
                op0=ALU.add, op1=ALU.add),
                waits=[tk_wm[oc], tkxr, tk_consts, tk_x1w_rd])
            rel((oc,), tstt)
            xr_free[sl] = tstt
            tk_x1w.append(tstt)
        if "x1w" in tapap and tc == 0:
            tk_x1w = [fence_v(dma("sync", tapap["x1w"][:, :], s_x1w[:, :],
                                  waits=tk_x1w, sem="dTP"))] * 8
        tm1 = tm2 = None
        tk_c1rd = [st_guard[0]]
        tk_c2rd = [st_guard[1]]
        for dc in range(8):
            tcst = sop(lambda e, st, d=dc: e.activation(
                out=s_cast1[:, 0:258], in_=s_x1w[:, 258 * d:258 * d + 258],
                func=AF.Copy), waits=[tk_x1w[dc]] + tk_c1rd)
            tsq = sop(lambda e, st, d=dc: e.activation(
                out=s_cast2[:, 0:258], in_=s_x1w[:, 258 * d:258 * d + 258],
                func=AF.Square), waits=[tk_x1w[dc], tcst] + tk_c2rd)
            tm1 = mm(ps[0:1, 1024:1024 + 258], s_ones[:, 0:1], s_cast1[:, 0:258],
                     start=(dc == 0), stop=(dc == 7), waits=[tcst, tk_ones],
                     banks=(2,))
            tm2 = mm(ps[0:1, 1536:1536 + 258], s_ones[:, 0:1], s_cast2[:, 0:258],
                     start=(dc == 0), stop=(dc == 7), waits=[tsq], banks=(3,))
            tk_c1rd, tk_c2rd = [tsq, tm1], [tm2]
        st_guard = [tm1, tm2]
        u1 = vop(lambda e, st, a=j0: e.tensor_scalar(
            out=s_mr[0:1, a:a + 258], in0=ps[0:1, 1024:1024 + 258],
            scalar1=1.0 / D, scalar2=None, op0=ALU.mult), waits=[tm1, tk_mrms])
        u2 = vop(lambda e, st, a=j0: e.tensor_scalar(
            out=s_mr[0:1, W + a:W + a + 258], in0=ps[0:1, 1536:1536 + 258],
            scalar1=1.0 / D, scalar2=None, op0=ALU.mult), waits=[tm2])
        rel((2,), u1)
        rel((3,), u2)
        u3 = vop(lambda e, st, a=j0: e.tensor_mul(
            s_row[0:1, 0:258], s_mr[0:1, a:a + 258], s_mr[0:1, a:a + 258]),
            waits=[u1, row_guard])
        u4 = vop(lambda e, st, a=j0: e.tensor_sub(
            s_mr[0:1, W + a:W + a + 258], s_mr[0:1, W + a:W + a + 258],
            s_row[0:1, 0:258]), waits=[u2, u3])
        u5 = vop(lambda e, st, a=j0: e.tensor_scalar(
            out=s_mr[0:1, W + a:W + a + 258], in0=s_mr[0:1, W + a:W + a + 258],
            scalar1=EPS, scalar2=None, op0=ALU.add), waits=[u4])
        u6 = vop(lambda e, st, a=j0: e.reciprocal(
            out=s_mr[0:1, W + a:W + a + 258], in_=s_mr[0:1, W + a:W + a + 258]),
            waits=[u5])
        row_guard = u3
        tk_rstd2 = sop(lambda e, st, a=j0: e.activation(
            out=s_mr[0:1, W + a:W + a + 258], in_=s_mr[0:1, W + a:W + a + 258],
            func=AF.Sqrt), waits=[u6, tk_rstd])
        tmb = bcast_ps(s_mr[0:1, j0:j0 + 258], 258, 4, [tk_rstd2, u1])
        tk_sub = []
        for dc in range(8):
            ts_ = vop(lambda e, st, d=dc: e.tensor_sub(
                s_lt[:, 258 * d:258 * d + 258], s_x1w[:, 258 * d:258 * d + 258],
                bank(4, 258)), waits=[tmb, tk_lt_rd, tk_x1w[dc]])
            tk_sub.append(ts_)
        rel((4,), tk_sub[-1])
        tb_ = vop(lambda e, st: e.scalar_tensor_tensor(
            out=s_red[:, 0:258], in0=s_lt[:, 0:258], scalar=-1.0,
            in1=s_lt[:, 0:258], op0=ALU.mult, op1=ALU.max),
            waits=[tk_sub[0]] + tk_red_guard)
        for dci in range(1, 8):
            t_a = vop(lambda e, st, d=dci: e.scalar_tensor_tensor(
                out=s_red[:, 258:516], in0=s_lt[:, 258 * d:258 * d + 258],
                scalar=-1.0, in1=s_lt[:, 258 * d:258 * d + 258],
                op0=ALU.mult, op1=ALU.max),
                waits=[tk_sub[dci], tb_])
            tb_ = vop(lambda e, st: e.tensor_tensor(out=s_red[:, 0:258],
                      in0=s_red[:, 0:258], in1=s_red[:, 258:516], op=ALU.max),
                      waits=[t_a])
        cps, mns = crosspart_max([tb_] + tk_scl_guard)
        q1 = vop(lambda e, st, jj=j0: e.tensor_mul(s_scl[0:1, 258:516],
                 s_scl[0:1, 0:258], s_mr[0:1, W + jj:W + jj + 258]),
                 waits=cps + [tk_rstd2])
        q2 = vop(lambda e, st: e.tensor_scalar(out=s_scl[0:1, 774:1032],
                 in0=s_scl[0:1, 258:516], scalar1=1e-5, scalar2=None, op0=ALU.max),
                 waits=[q1])
        q3 = vop(lambda e, st: e.reciprocal(out=s_scl[0:1, 258:516],
                 in_=s_scl[0:1, 774:1032]), waits=[q2])
        q4 = vop(lambda e, st, jj=j0: e.tensor_mul(s_scl[0:1, 258:516],
                 s_scl[0:1, 258:516], s_mr[0:1, W + jj:W + jj + 258]), waits=[q3])
        q5 = vop(lambda e, st: e.tensor_scalar(out=s_scl[0:1, 258:516],
                 in0=s_scl[0:1, 258:516], scalar1=127.0, scalar2=None, op0=ALU.mult),
                 waits=[q4])
        q6 = vop(lambda e, st: e.tensor_scalar(out=s_scl[0:1, 516:774],
                 in0=s_scl[0:1, 774:1032], scalar1=s_scl[0:1, 1032:1033],
                 scalar2=1.0 / 127.0, op0=ALU.mult, op1=ALU.mult),
                 waits=[q2, qstate[("w1", "mean")]])
        if tc == 0:
            q6 = vop(lambda e, st: e.tensor_mul(s_scl[0:1, 516:774], s_scl[0:1, 516:774],
                     s_maskl[0:1, :]), waits=[q6, tk_consts])
        if tc == 7:
            q6 = vop(lambda e, st: e.tensor_mul(s_scl[0:1, 516:774], s_scl[0:1, 516:774],
                     s_maskr[0:1, :]), waits=[q6, tk_consts])
        tcb = bcast_ps(s_scl[0:1, 258:516], 258, 4, [q5])
        tsbv = bcast_sb(s_scl[0:1, 516:774], 258, 5, s_bc[:, 0:258], [q6])
        tq8 = []
        for dc in range(8):
            tq8.append(vop(lambda e, st, d=dc: e.tensor_mul(
                s_xq8[:, 258 * d:258 * d + 258], s_lt[:, 258 * d:258 * d + 258],
                bank(4, 258)), waits=[tcb, tk_sub[dc]] + tk_xq8_rd))
        tk_xq8_rd = []
        rel((4,), tq8[-1])
        tqb = vop(lambda e, st: e.tensor_copy(out=s_xqb[:, :], in_=s_xq8[:, :]),
                  waits=tq8 + [tk_xqb_rd])
        if "w1q" in tapap and tc == 0:
            tqb = fence_v(dma("sync", tapap["w1q"][:, :], s_w1q[:, :],
                              waits=[tqb, tk_w1q.get(7)], sem="dTP"))
        tk_lt_rd = tq8[-1]
        if "xq" in tapap and tc == 0:
            tqb = fence_v(dma("sync", tapap["xq"][:, :], s_xqb[:, :], waits=[tqb],
                              sem="dTP"))
        tk_z2 = []
        prev_tt1 = None
        prev_tt2 = None
        prev_silu = None
        for hc in range(32):
            bk = hc % 4
            tmm = None
            for dc in range(8):
                tmm = mm(bank(bk, 258),
                         s_w1q[:, 4096 * dc + 128 * hc:4096 * dc + 128 * hc + 128],
                         s_xqb[:, 258 * dc:258 * dc + 258], start=(dc == 0),
                         stop=(dc == 7), waits=[tqb, tk_w1q.get(dc)], banks=(bk,))
            tev = vop(lambda e, st, h=hc, b=bk: e.tensor_mul(
                s_hy[:, 258 * h:258 * h + 258], bank(b, 258),
                s_bc[:, 0:258]), waits=[tmm, tsbv, tk_hy_wr_ok, NS["rb_free"]])
            rel((bk,), tev)
            if "hpre" in tapap and tc == 0:
                tev = fence_v(dma("sync", tapap["hpre"][:, 258 * hc:258 * hc + 258],
                                  s_hy[:, 258 * hc:258 * hc + 258], waits=[tev],
                                  sem="dTP"))
            tt0 = sop(lambda e, st, h=hc: e.activation(
                out=s_tc0[:, 0:256], in_=s_hy[:, 258 * h:258 * h + 256],
                func=AF.Identity, bias=s_cb[:, h:h + 1], scale=s_cw[:, h:h + 1]),
                waits=[tev, tk_consts, prev_tt1])
            tt1 = vop(lambda e, st, h=hc: e.scalar_tensor_tensor(
                out=s_tc0[:, 258:258 + 256],
                in0=s_hy[:, 258 * h + 1:258 * h + 257],
                scalar=s_cw[:, 32 + h:32 + h + 1], in1=s_tc0[:, 0:256],
                op0=ALU.mult, op1=ALU.add), waits=[tt0, tev, prev_tt2])
            tt2 = vop(lambda e, st, h=hc: e.scalar_tensor_tensor(
                out=s_tcb[:, 0:256],
                in0=s_hy[:, 258 * h + 2:258 * h + 258],
                scalar=s_cw[:, 64 + h:64 + h + 1], in1=s_tc0[:, 258:258 + 256],
                op0=ALU.mult, op1=ALU.add), waits=[tt1, prev_silu])
            tt3 = sop(lambda e, st, h=hc: e.activation(
                out=s_hy[:, 258 * h:258 * h + 256],
                in_=s_tcb[:, 0:256], func=AF.Silu), waits=[tt2])
            prev_tt1, prev_tt2, prev_silu = tt1, tt2, tt3
            tk_z2.append(tt3)
        if "h" in tapap and tc == 0:
            tk_z2 = [fence_v(dma("sync", tapap["h"][:, :], s_hy[:, :], waits=tk_z2,
                                 sem="dTP"))] * 32
        ab = vop(lambda e, st: e.scalar_tensor_tensor(
            out=s_red[:, 0:256], in0=s_hy[:, 0:256], scalar=-1.0,
            in1=s_hy[:, 0:256], op0=ALU.mult, op1=ALU.max),
            waits=[tk_z2[0]] + mns)
        for hc in range(1, 32):
            t_a = vop(lambda e, st, h=hc: e.scalar_tensor_tensor(
                out=s_red[:, 258:258 + 256], in0=s_hy[:, 258 * h:258 * h + 256],
                scalar=-1.0, in1=s_hy[:, 258 * h:258 * h + 256],
                op0=ALU.mult, op1=ALU.max),
                waits=[tk_z2[hc], ab])
            ab = vop(lambda e, st: e.tensor_tensor(out=s_red[:, 0:256],
                     in0=s_red[:, 0:256], in1=s_red[:, 258:258 + 256], op=ALU.max),
                     waits=[t_a])
        ab2 = vop(lambda e, st: e.memset(s_red[:, 256:258], 0.0), waits=[ab, cps[2]])
        cps2, mns2 = crosspart_max([ab, ab2] + cps)
        r1 = vop(lambda e, st: e.tensor_scalar(out=s_scl[0:1, 774:1032],
                 in0=s_scl[0:1, 0:258], scalar1=1e-5, scalar2=None, op0=ALU.max),
                 waits=cps2 + [q6])
        r2 = vop(lambda e, st: e.reciprocal(out=s_scl[0:1, 0:258],
                 in_=s_scl[0:1, 774:1032]), waits=[r1])
        r3 = vop(lambda e, st: e.tensor_scalar(out=s_scl[0:1, 0:258],
                 in0=s_scl[0:1, 0:258], scalar1=127.0, scalar2=None, op0=ALU.mult),
                 waits=[r2])
        r4 = vop(lambda e, st: e.tensor_scalar(out=s_scl[0:1, 516:774],
                 in0=s_scl[0:1, 774:1032], scalar1=s_scl[0:1, 1033:1034],
                 scalar2=1.0 / 127.0, op0=ALU.mult, op1=ALU.mult),
                 waits=[r1, qstate[("w2", "mean")], tsbv])
        ts2v = bcast_sb(s_scl[0:1, 0:256], 256, 4, s_bc[:, 516:516 + 256],
                        [r3, tcb, tq8[-1]])
        tg2v = bcast_sb(s_scl[0:1, 516:516 + 256], 256, 5, s_bc[:, 258:258 + 256],
                        [r4, tsbv])
        tk_scl_guard = [ts2v, tg2v]
        tcv_prev = [None, None]
        tmm2 = {oc: None for oc in range(8)}
        last_mm = None
        ty8_last = None
        for hc in range(32):
            sl2 = hc % 2
            ty8 = vop(lambda e, st, h=hc, s=sl2: e.tensor_mul(
                s_y8r[:, 260 * s:260 * s + 256],
                s_hy[:, 258 * h:258 * h + 256], s_bc[:, 516:516 + 256]),
                waits=[ts2v, tk_z2[hc], tcv_prev[sl2], tk_y8ms])
            tcv = vop(lambda e, st, s=sl2: e.tensor_copy(
                out=s_ybr[:, 260 * s:260 * s + 256],
                in_=s_y8r[:, 260 * s:260 * s + 256]),
                waits=[ty8, (last_mm if hc >= 2 else None)])
            for oc in range(8):
                tmm2[oc] = mm(bank(oc, 256),
                              s_w2q[:, 1024 * hc + 128 * oc:1024 * hc + 128 * oc + 128],
                              s_ybr[:, 260 * sl2:260 * sl2 + 256], start=(hc == 0),
                              stop=(hc == 31), waits=[tcv, tk_w2q.get(hc)],
                              banks=(oc,))
            tcv_prev[sl2] = tmm2[7]
            last_mm = tmm2[7]
            ty8_last = ty8
        tk_hy_wr_ok = ty8_last
        for oc in range(8):
            so_sl = oc % 2
            te1 = vop(lambda e, st, b=oc: e.tensor_mul(
                s_tc0[:, 0:256], bank(b, 256), s_bc[:, 258:258 + 256]),
                waits=[tmm2[oc], tg2v, prev_te1])
            rel((oc,), te1)
            te2 = vop(lambda e, st, o=oc, t=tc, s=so_sl: e.tensor_add(
                s_out[:, 256 * s:256 * s + 256], s_tc0[:, 0:256],
                s_x1w[:, 258 * o + 1:258 * o + 257]),
                waits=[te1, tk_out_dma[so_sl], tk_x1w[oc]])
            prev_te1 = te2
            tko = dma("sync", fm(ap["out"])[:, oc, 256 * tc:256 * tc + 256],
                      s_out[:, 256 * so_sl:256 * so_sl + 256], waits=[te2],
                      sem="dO%d" % so_sl)
            tk_out_dma[so_sl] = tko
            tk_out_done.append(tko)
        tk_x1w_rd = tk_out_done[-1]
        tk_red_guard = mns2 + cps2
        tk_xqb_rd = tk_z2[-1]
        if "xq2" in tapap and tc == 0:
            dma("sync", tapap["xq2"][:, :], s_ybr[:, :],
                waits=[tcv_prev[0], tcv_prev[1]], sem="dTP")

    p.emit("sync", lambda e, st: e.nop(),
           waits=[p.tok(s) for s in ["dC","dX0","dX1","dTP","dW0","dW1","dWD0","dWD1","dKD","dZN","dZI","dZB","dWO0","dWO1","dXR0","dXR1","dO0","dO1","tT","tV","tS","tG","CC"]])
    return _finish(nc, ctx, p)


def _finish(nc, ctx, p):
    sems = {}
    for s in ["dC", "dX0", "dX1", "dTP", "dW0", "dW1", "dWD0", "dWD1", "dKD", "dZN", "dZI",
              "dZB", "dWO0", "dWO1", "dXR0", "dXR1", "dO0", "dO1",
              "tT", "tV", "tS", "tG", "CC"]:
        sems[s] = ctx.enter_context(nc.semaphore(s))
    blk = ctx.enter_context(nc.Block())
    state = {}

    @blk.sync
    def _(e):
        p.run("sync", e, sems, state)

    @blk.tensor
    def _(e):
        p.run("tensor", e, sems, state)

    @blk.vector
    def _(e):
        p.run("vector", e, sems, state)

    @blk.scalar
    def _(e):
        p.run("scalar", e, sems, state)

    @blk.gpsimd
    def _(e):
        p.run("gpsimd", e, sems, state)

    ctx.close()
    return nc



def kernel(**inputs):
    from concourse.bass_utils import run_bass_kernel_spmd
    maps = stage_inputs(inputs, n_cores=8)
    nc = build_nc(n_cores=8, n_heads=NB, stop_after="full", taps=())
    res = run_bass_kernel_spmd(nc, maps, core_ids=list(range(8)))
    return gather_out(res.results)


# version 07
# speedup vs baseline: 1.3991x; 1.0359x over previous
import numpy as np
import ml_dtypes
from contextlib import ExitStack
import concourse.bass as bass
import concourse.mybir as mybir

F32 = mybir.dt.float32
BF16 = mybir.dt.bfloat16
I8 = mybir.dt.int8
F8 = mybir.dt.float8e4
F16 = mybir.dt.float16
AF = mybir.ActivationFunctionType
ALU = mybir.AluOpType
AX = mybir.AxisListType
BF = ml_dtypes.bfloat16
F16N = np.float16

D = 1024
S = 4096
T = 2048
NB = 8
BS = 256
H = 4096
W = 2050
WZ = 4098
EPS = 1e-5


def qkv_perm():
    perm = np.empty(3072, np.int64)
    for sec in range(3):
        for g in range(16):
            perm[1024 * sec + 64 * g: 1024 * sec + 64 * g + 64] = \
                np.arange(192 * g + 64 * sec, 192 * g + 64 * sec + 64)
    return perm

def zrow_to_cglob():
    out = np.empty(1024, np.int64)
    for r in range(1024):
        rank, rr = divmod(r, 512)
        chunk, part = divmod(rr, 128)
        m = (4 if part >= 64 else 0) + chunk
        out[r] = 64 * (8 * rank + m) + part % 64
    return out

def stage_inputs(inputs, n_cores=8):
    x = np.asarray(inputs["L0_attn_x"], np.float32)
    Wqkv = np.asarray(inputs["L0_attn_W_qkv"], np.float32)
    bqkv = np.asarray(inputs["L0_attn_b_qkv"], np.float32)
    Wo = np.asarray(inputs["L0_attn_W_o"], np.float32)
    bo = np.asarray(inputs["L0_attn_b_o"], np.float32)
    w1 = np.asarray(inputs["L0_ffn_w1"], np.float32)
    cw = np.asarray(inputs["L0_ffn_conv_w"], np.float32)
    cb = np.asarray(inputs["L0_ffn_conv_b"], np.float32)
    w2 = np.asarray(inputs["L0_ffn_w2"], np.float32)

    perm = qkv_perm()
    com = dict(
        WqT=np.ascontiguousarray(Wqkv.T[:, perm]),
        bq_pp=np.ascontiguousarray(bqkv[perm].reshape(24, 128).T),
        WoT=np.ascontiguousarray(Wo.T[zrow_to_cglob(), :]),
        bo_pp=np.ascontiguousarray(bo.reshape(8, 128).T),
        w1T=np.ascontiguousarray(w1.T),
        w2T=np.ascontiguousarray(w2.T),
        cw_pp=np.ascontiguousarray(cw[:, 0, :].reshape(32, 128, 3).transpose(1, 2, 0).reshape(128, 96)),
        cb_pp=np.ascontiguousarray(cb.reshape(32, 128).T),
        ident=np.eye(128, dtype=F16N),
        ident32=np.eye(128, dtype=np.float32),
        ones1=np.ones((128, 1), dtype=F16N),
        ones32=np.ones((128, 128), dtype=np.float32),
    )
    maps = []
    for c in range(n_cores):
        b, off = divmod(c, 2)
        xr = np.zeros((D, W), np.float32)
        if off == 0:
            xr[:, 1:2050] = x[b, 0:2049, :].T
        else:
            xr[:, 0:2049] = x[b, 2047:4096, :].T
        mask_l = np.ones((1, 258), np.float32)
        mask_r = np.ones((1, 258), np.float32)
        if off == 0:
            mask_l[0, 0] = 0.0
        else:
            mask_r[0, 257] = 0.0
        maps.append(dict(com, x_res=xr, mask_l=mask_l, mask_r=mask_r))
    return maps

def gather_out(results):
    out = np.empty((4, S, D), np.float32)
    for c, r in enumerate(results):
        b, off = divmod(c, 2)
        out[b, T * off:T * off + T, :] = np.asarray(r["out"]).T
    return out


class P:
    ENGS = ("sync", "tensor", "vector", "scalar", "gpsimd")

    def __init__(self):
        self.q = {e: [] for e in self.ENGS}
        self.cnt = {}
        self.water = {e: {} for e in self.ENGS}
        self.bkg = [None] * 8

    def tok(self, sem):
        return (sem, self.cnt.get(sem, 0))

    def emit(self, eng, fn, waits=(), inc=None, amt=1):
        best = {}
        for smv in waits:
            if smv is None:
                continue
            s, v = smv
            if v > best.get(s, 0):
                best[s] = v
        wl = []
        for s, v in best.items():
            if v > self.water[eng].get(s, 0):
                self.water[eng][s] = v
                wl.append((s, v))
        tk = None
        if inc is not None:
            self.cnt[inc] = self.cnt.get(inc, 0) + amt
            tk = (inc, self.cnt[inc])
        self.q[eng].append((fn, wl, inc, amt))
        return tk

    def run(self, eng_name, handle, sems, state):
        for fn, wl, inc, amt in self.q[eng_name]:
            for s, v in wl:
                handle.wait_ge(sems[s], v)
            inst = fn(handle, state)
            if inc is not None:
                inst.then_inc(sems[inc], amt)


def build_nc(n_cores=8, n_heads=NB, stop_after="full", taps=(), groups=None):
    nc = bass.Bass("TRN2", target_bir_lowering=False)
    p = P()
    if groups is None:
        groups = [[i, i + 1] for i in range(0, n_cores, 2)]

    ap = {}
    for nm, sh, dt in [("x_res", [D, W], F32), ("WqT", [D, 3072], F32),
                       ("bq_pp", [128, 24], F32), ("WoT", [D, D], F32),
                       ("bo_pp", [128, 8], F32), ("w1T", [D, H], F32),
                       ("w2T", [H, D], F32), ("cw_pp", [128, 96], F32),
                       ("cb_pp", [128, 32], F32), ("ident", [128, 128], F16),
                       ("ident32", [128, 128], F32), ("ones1", [128, 1], F16),
                       ("ones32", [128, 128], F32),
                       ("mask_l", [1, 258], F32), ("mask_r", [1, 258], F32)]:
        ap[nm] = nc.dram_tensor(nm, sh, dt, kind="ExternalInput").ap()
    ap["out"] = nc.dram_tensor("out", [D, T], F32, kind="ExternalOutput").ap()
    zin = nc.dram_tensor("zin", [512, WZ], F16)
    zgath = nc.dram_tensor("zgath", [1024, WZ], F16)
    wodram = nc.dram_tensor("wodram", [1024, 1024], F16)

    tap_shapes = {"stats": ([1, 2 * W], F32), "Y": ([128, 24 * BS], F16),
                  "ln1": ([128, 8 * BS], F16), "zin": ([512, WZ], F16),
                  "x1w": ([128, 8 * 258], F32), "xq": ([128, 8 * 258], BF16),
                  "h": ([128, 32 * 258], F32), "xq2": ([128, 2 * 260], BF16),
                  "hpre": ([128, 32 * 258], F32), "w1q": ([128, 32768], F8)}
    tapap = {}
    for tname in taps:
        sh, dt = tap_shapes[tname]
        tapap[tname] = nc.dram_tensor("tap_" + tname, sh, dt, kind="ExternalOutput").ap()

    ctx = ExitStack()
    sbt = lambda n, s, d: ctx.enter_context(nc.sbuf_tensor(n, s, d))

    s_ident = sbt("s_ident", [128, 128], F16)
    s_id32 = sbt("s_id32", [128, 128], F32)
    s_ones = sbt("s_ones", [128, 1], F16)
    s_on32 = sbt("s_on32", [128, 128], F32)
    s_bq = sbt("s_bq", [128, 24], F32)
    s_bo = sbt("s_bo", [128, 8], F32)
    s_cw = sbt("s_cw", [128, 96], F32)
    s_cb = sbt("s_cb", [128, 32], F32)
    s_maskl = sbt("s_maskl", [1, 258], F32)
    s_maskr = sbt("s_maskr", [1, 258], F32)
    s_mr = sbt("s_mr", [1, 2 * W], F32)
    s_scl = sbt("s_scl", [1, 1040], F32)
    s_row = s_scl[0:1, 0:512]
    s_ws = sbt("s_ws", [128, 8], F32)

    AR1 = sbt("AR1", [128, 12288], F32)
    s_wq = AR1[:, 0:12288].bitcast(F16)
    s_w1q = AR1[:, 0:8192].bitcast(F8)
    s_xqb = AR1[:, 8192:9224].bitcast(BF16)
    s_y8r = AR1[:, 10900:11160].bitcast(I8)
    s_ybr = AR1[:, 11160:11680].bitcast(BF16)
    s_bc = AR1[:, 9614:10388]
    s_cast1 = AR1[:, 10388:10644].bitcast(BF16)
    s_cast2 = AR1[:, 10644:10900].bitcast(BF16)

    AR2 = sbt("AR2", [128, 16704], F32)
    s_xblk = AR2[:, 0:4096]
    s_ln1 = AR2[:, 4096:6144].bitcast(F16)
    s_xsq = AR2[:, 6144:7168].bitcast(F16)
    s_Y = AR2[:, 7168:10240].bitcast(F16)
    s_kdup = AR2[:, 10240:11264].bitcast(F16)
    s_vt = AR2[:, 11264:12304].bitcast(BF16)
    s_E = AR2[:, 12304:14352].bitcast(BF16)
    s_zn = AR2[:, 14352:14864].bitcast(F16)
    s_zr = AR2[:, 14864:15888]
    s_qd = AR2[:, 6144:7168].bitcast(F16)
    s_atmp = AR2[:, 15888:16144]
    s_abc = AR2[:, 16144:16656]
    s_x1w = AR2[:, 0:2064]
    s_lt = AR2[:, 2064:4128]
    s_xq8 = AR2[:, 4128:4644].bitcast(I8)
    s_w8t = AR2[:, 4128:4640].bitcast(I8)
    s_tc0 = AR2[:, 4644:5160]
    s_red = AR2[:, 5160:5684]
    s_tcb = AR2[:, 5684:5942]
    s_out = AR2[:, 5942:6454]
    s_xr = AR2[:, 6454:7478]
    s_wos = AR2[:, 7478:8502].bitcast(F16)
    s_w2q = AR2[:, 8502:16694].bitcast(F8)

    AR3 = sbt("AR3", [128, 8256], F32)
    s_wst = AR3[:, 0:4096]
    s_hy = AR3[:, 0:8256]
    OT = AR3[:, 4096:6144]

    ARZ = sbt("ARZ", [128, 8200], F32)
    s_z = ARZ[:, :].bitcast(F16)

    ps = ctx.enter_context(nc.psum_tensor("ps", [128, 4096], F32))

    def bank(i, cols=512, n0=0):
        return ps[:, 512 * i + n0: 512 * i + n0 + cols]

    def dma(eng, out, in_, waits=(), sem=None):
        assert sem is not None
        return p.emit(eng, lambda e, st, o=out, i=in_: e.dma_start(out=o, in_=i),
                      waits=waits, inc=sem, amt=16)

    def mm(out, lhsT, rhs, start, stop, waits=(), banks=()):
        w = list(waits)
        if start:
            w += [p.bkg[b] for b in banks]
        return p.emit("tensor",
                      lambda e, st, o=out, l=lhsT, r=rhs, a=start, b=stop:
                      e.matmul(o, l, r, start=a, stop=b), waits=w, inc="tT")

    def tr(out, in_, idt, waits=(), banks=()):
        w = list(waits) + [p.bkg[b] for b in banks]
        return p.emit("tensor", lambda e, st, o=out, i=in_, d=idt: e.transpose(o, i, d),
                      waits=w, inc="tT")

    def rel(banks, tok):
        for b in banks:
            p.bkg[b] = tok

    def vop(fn, waits=()):
        return p.emit("vector", fn, waits=waits, inc="tV")

    def sop(fn, waits=()):
        return p.emit("scalar", fn, waits=waits, inc="tS")

    def pop(fn, waits=()):
        return p.emit("gpsimd", fn, waits=waits, inc="tG")

    def fence_v(tok):
        return vop(lambda e, st: e.memset(s_ws[:, 6:7], 0.0), waits=[tok])

    def fm(dram_ap):
        return dram_ap.rearrange("(c p) w -> p c w", p=128)

    for o_, i_ in [(s_ident[:, :], ap["ident"][:, :]), (s_id32[:, :], ap["ident32"][:, :]),
                   (s_ones[:, :], ap["ones1"][:, :]), (s_on32[:, :], ap["ones32"][:, :]),
                   (s_bq[:, :], ap["bq_pp"][:, :]), (s_bo[:, :], ap["bo_pp"][:, :]),
                   (s_cw[:, :], ap["cw_pp"][:, :]), (s_cb[:, :], ap["cb_pp"][:, :]),
                   (s_maskl[:, :], ap["mask_l"][:, :]), (s_maskr[:, :], ap["mask_r"][:, :])]:
        dma("sync", o_, i_, sem="dC")
    tk_consts = p.tok("dC")
    tk_id = tk_id32 = tk_ones = tk_on32 = tk_consts
    tk_vtones = vop(lambda e, st: e.memset(s_vt[:, 64:32 * 65:65], 1.0))
    tk_zpad = vop(lambda e, st: e.memset(s_z[:, :], 0.0))
    tk_y8ms = vop(lambda e, st: e.memset(s_y8r[:, :], 0))
    tk_mrms = vop(lambda e, st: e.memset(s_mr[0:1, :], 0.0))
    vop(lambda e, st: e.memset(s_ws[:, 4:5], -80.0))
    tk_expc = vop(lambda e, st: e.memset(s_ws[:, 5:6], 0.125))

    def bcast_sb(row_ap, n, bnk, dst, waits):
        tm = mm(bank(bnk, n), s_on32[0:1, 0:128], row_ap, True, True,
                waits=list(waits) + [tk_on32], banks=(bnk,))
        tv = vop(lambda e, st, d=dst, b=bnk, k=n: e.tensor_copy(out=d, in_=bank(b, k)),
                 waits=[tm])
        rel((bnk,), tv)
        return tv

    def bcast_ps(row_ap, n, bnk, waits):
        return mm(bank(bnk, n), s_on32[0:1, 0:128], row_ap, True, True,
                  waits=list(waits) + [tk_on32], banks=(bnk,))

    xr3 = fm(ap["x_res"])
    WqT3 = fm(ap["WqT"])
    WoT3 = fm(ap["WoT"])
    w1T3 = fm(ap["w1T"])
    w2T3 = fm(ap["w2T"])
    wod3 = fm(wodram.ap())

    xslot_free = [None, None]
    tk_stats = []
    for j in range(NB):
        sl = j % 2
        xs = s_xblk[:, 2048 * sl:2048 * sl + 2048]
        lns = s_ln1[:, 2048 * sl:2048 * sl + 2048]
        tkx = dma("sync", xs.rearrange("p (c w) -> p c w", c=8),
                  xr3[:, :, 1 + BS * j:1 + BS * j + BS], waits=[xslot_free[sl]],
                  sem="dX%d" % sl)
        tkb = vop(lambda e, st, o=lns, i=xs: e.tensor_copy(out=o, in_=i), waits=[tkx])
        tkq = vop(lambda e, st, i=lns: e.tensor_mul(s_xsq[:, :], i, i),
                  waits=[tkb, tk_stats[-1] if tk_stats else None])
        tm1 = tm2 = None
        for dc in range(8):
            tm1 = mm(ps[0:1, 1024:1024 + 256], s_ones[:, 0:1],
                     lns[:, 256 * dc:256 * dc + 256], start=(dc == 0), stop=(dc == 7),
                     waits=[tkb, tk_ones], banks=(2,))
            tm2 = mm(ps[0:1, 1536:1536 + 256], s_ones[:, 0:1],
                     s_xsq[:, 256 * dc:256 * dc + 256], start=(dc == 0), stop=(dc == 7),
                     waits=[tkq], banks=(3,))
        t1 = vop(lambda e, st, j=j: e.tensor_scalar(
            out=s_mr[0:1, BS * j:BS * j + BS], in0=ps[0:1, 1024:1024 + 256],
            scalar1=1.0 / D, scalar2=None, op0=ALU.mult), waits=[tm1, tk_mrms])
        t2 = vop(lambda e, st, j=j: e.tensor_scalar(
            out=s_mr[0:1, W + BS * j:W + BS * j + BS], in0=ps[0:1, 1536:1536 + 256],
            scalar1=1.0 / D, scalar2=None, op0=ALU.mult), waits=[tm2])
        rel((2,), t1)
        rel((3,), t2)
        t3 = vop(lambda e, st, j=j: e.tensor_mul(
            s_row[0:1, 0:BS], s_mr[0:1, BS * j:BS * j + BS],
            s_mr[0:1, BS * j:BS * j + BS]), waits=[t1])
        t4 = vop(lambda e, st, j=j: e.tensor_sub(
            s_mr[0:1, W + BS * j:W + BS * j + BS],
            s_mr[0:1, W + BS * j:W + BS * j + BS], s_row[0:1, 0:BS]), waits=[t2, t3])
        t5 = vop(lambda e, st, j=j: e.tensor_scalar(
            out=s_mr[0:1, W + BS * j:W + BS * j + BS],
            in0=s_mr[0:1, W + BS * j:W + BS * j + BS],
            scalar1=EPS, scalar2=None, op0=ALU.add), waits=[t4])
        t6 = vop(lambda e, st, j=j: e.reciprocal(
            out=s_mr[0:1, W + BS * j:W + BS * j + BS],
            in_=s_mr[0:1, W + BS * j:W + BS * j + BS]), waits=[t5])
        tk_stats.append(t6)
        xslot_free[sl] = tkb
    tk_rstd = sop(lambda e, st: e.activation(
        out=s_mr[0:1, W:W + T], in_=s_mr[0:1, W:W + T], func=AF.Sqrt),
        waits=[tk_stats[-1]])

    if "stats" in tapap:
        dma("sync", tapap["stats"][0:1, :], s_mr[0:1, :], waits=[tk_rstd], sem="dTP")

    wst_free = [None, None]
    slot_ctr = [0]
    qstate = {}

    def sub_chunks(width):
        return [(a, min(2048, width - a)) for a in range(0, width, 2048)]

    def j_cast(src_sl, dst_sl, toks, key):
        def go():
            sl = slot_ctr[0] & 1
            slot_ctr[0] += 1
            n = src_sl.shape[-1]
            tkd = dma("sync", s_wst[:, 2048 * sl:2048 * sl + n], src_sl,
                      waits=[wst_free[sl]], sem="dW%d" % sl)
            tkv = vop(lambda e, st, s=sl, d=dst_sl, nn=n: e.tensor_copy(
                out=d, in_=s_wst[:, 2048 * s:2048 * s + nn]), waits=[tkd])
            wst_free[sl] = tkv
            toks[key] = tkv
        return go

    def j_q1(src_sl, key, last, nel):
        def go():
            sl = slot_ctr[0] & 1
            slot_ctr[0] += 1
            n = src_sl.shape[-1]
            tkd = dma("sync", s_wst[:, 2048 * sl:2048 * sl + n], src_sl,
                      waits=[wst_free[sl]], sem="dW%d" % sl)
            tkr = vop(lambda e, st, s=sl, nn=n: e.reduce_sum(
                out=s_ws[:, 3:4], in_=s_wst[:, 2048 * s:2048 * s + nn],
                axis=AX.X, apply_absolute_value=True),
                waits=[tkd, qstate.get((key, "redfree"))])
            if (key, "acc") not in qstate:
                ta = vop(lambda e, st: e.tensor_copy(out=s_ws[:, 2:3], in_=s_ws[:, 3:4]),
                         waits=[tkr])
            else:
                ta = vop(lambda e, st: e.tensor_add(
                    s_ws[:, 2:3], s_ws[:, 2:3], s_ws[:, 3:4]),
                    waits=[tkr, qstate.get((key, "acc"))])
            qstate[(key, "acc")] = ta
            qstate[(key, "redfree")] = ta
            wst_free[sl] = tkr
            if last:
                tt = tr(ps[0:1, 1024:1152], s_ws[:, 2:3], s_id32[:, :],
                        waits=[ta, tk_id32], banks=(2,))
                tsum = vop(lambda e, st: e.reduce_sum(
                    out=s_scl[0:1, 1036:1037], in_=ps[0:1, 1024:1152], axis=AX.X),
                    waits=[tt, qstate.get("rowfree")])
                rel((2,), tsum)
                mcol = 1032 if key == "w1" else 1033
                tmean = vop(lambda e, st, mc=mcol, ne=nel: e.tensor_scalar(
                    out=s_scl[0:1, mc:mc + 1], in0=s_scl[0:1, 1036:1037],
                    scalar1=1.0 / ne, scalar2=None, op0=ALU.mult), waits=[tsum])
                trec = vop(lambda e, st, mc=mcol: e.reciprocal(
                    out=s_scl[0:1, 1037:1038], in_=s_scl[0:1, mc:mc + 1]),
                    waits=[tmean])
                wcol = 0 if key == "w1" else 1
                tbm = mm(ps[:, 1024:1025], s_on32[0:1, 0:128], s_scl[0:1, 1037:1038],
                         True, True, waits=[trec, tk_on32], banks=(2,))
                tbv = vop(lambda e, st, wc=wcol: e.tensor_copy(
                    out=s_ws[:, wc:wc + 1], in_=ps[:, 1024:1025]), waits=[tbm])
                rel((2,), tbv)
                qstate[(key, "winv")] = tbv
                qstate[(key, "mean")] = tmean
                qstate["rowfree"] = trec
        return go

    def j_q2(src_sl, dst_sl, key, toks, tkey):
        def go():
            sl = slot_ctr[0] & 1
            slot_ctr[0] += 1
            n = src_sl.shape[-1]
            wcol = 0 if key == "w1" else 1
            tkd = dma("sync", s_wst[:, 2048 * sl:2048 * sl + n], src_sl,
                      waits=[wst_free[sl]], sem="dW%d" % sl)
            ta = vop(lambda e, st, s=sl, nn=n, wc=wcol: e.tensor_scalar(
                out=s_wst[:, 2048 * s:2048 * s + nn],
                in0=s_wst[:, 2048 * s:2048 * s + nn],
                scalar1=s_ws[:, wc:wc + 1], scalar2=None, op0=ALU.mult),
                waits=[tkd, qstate[(key, "winv")]])
            tc8 = vop(lambda e, st, s=sl, nn=n: e.tensor_scalar(
                out=s_w8t[:, 0:nn], in0=s_wst[:, 2048 * s:2048 * s + nn],
                scalar1=-1.0, scalar2=1.0, op0=ALU.max, op1=ALU.min),
                waits=[ta, qstate.get("w8free")])
            td = vop(lambda e, st, d=dst_sl, nn=n: e.tensor_copy(
                out=d, in_=s_w8t[:, 0:nn]), waits=[tc8])
            qstate["w8free"] = td
            wst_free[sl] = tc8
            toks[tkey] = td
        return go

    jobsA = []
    tk_wq = {}
    for kc in range(8):
        for (a, n) in sub_chunks(3072):
            jobsA.append(j_cast(WqT3[:, kc, a:a + n],
                                s_wq[:, 3072 * kc + a:3072 * kc + a + n], tk_wq,
                                (kc, a)))
    w1subs = [(kc, a, n) for kc in range(8) for (a, n) in sub_chunks(4096)]
    for i, (kc, a, n) in enumerate(w1subs):
        jobsA.append(j_q1(w1T3[:, kc, a:a + n], "w1", i == len(w1subs) - 1, D * H))
    for i, kc in enumerate(range(32)):
        jobsA.append(j_q1(w2T3[:, kc, :], "w2", i == 31, D * H))

    jobsB0 = []
    tk_wocast = {}
    for kc in range(8):
        def mk(kc=kc):
            def go():
                sl = slot_ctr[0] & 1
                slot_ctr[0] += 1
                tkd = dma("sync", s_wst[:, 2048 * sl:2048 * sl + 1024],
                          WoT3[:, kc, :], waits=[wst_free[sl]], sem="dW%d" % sl)
                wsl = kc % 2
                tkv = vop(lambda e, st, s=sl, ws=wsl: e.tensor_copy(
                    out=s_wos[:, 1024 * ws:1024 * ws + 1024],
                    in_=s_wst[:, 2048 * s:2048 * s + 1024]),
                    waits=[tkd, qstate.get(("wos", wsl))])
                tks = dma("sync", wod3[:, kc, :],
                          s_wos[:, 1024 * wsl:1024 * wsl + 1024], waits=[tkv],
                          sem="dWD%d" % wsl)
                qstate[("wos", wsl)] = tks
                wst_free[sl] = tkv
                tk_wocast[kc] = tks
            return go
        jobsB0.append(mk())
    tk_w1q = {}
    for kc in range(8):
        for (a, n) in sub_chunks(4096):
            jobsB0.append(j_q2(w1T3[:, kc, a:a + n],
                               s_w1q[:, 4096 * kc + a:4096 * kc + a + n], "w1",
                               tk_w1q, kc))
    jobsB1 = []
    tk_w2q = {}
    for kc in range(32):
        jobsB1.append(j_q2(w2T3[:, kc, :], s_w2q[:, 1024 * kc:1024 * kc + 1024], "w2",
                           tk_w2q, kc))

    for jb in jobsA[:16]:
        jb()
    jobsA = jobsA[16:]
    nh = min(n_heads, NB)
    per_block = (len(jobsA) + nh - 1) // max(nh, 1)

    lnslot_free = [None, None]
    tk_Y_free = None
    zloc_last = None
    NS = {"prev_av": {(0, 0): None, (0, 1): None, (1, 0): None, (1, 1): None},
          "av_last": [None, None], "zn_free": None,
          "rb_free": None, "zr_free": [None, None], "ot_free": [None, None],
          "pend_norm": None, "zloc_last": None}

    def flush_norm():
        qcn, jn, pend = NS["pend_norm"]
        pmn = 64 * (jn // 4)
        zchn = jn % 4
        for (par, tevO, tevZ) in pend:
            obank = 4 + 2 * par
            zo = 1024 * par
            trb1 = mm(ps[0:64, 512 * obank:512 * obank + 512], s_on32[64:65, 0:64],
                      OT[64:65, zo:zo + 512], True, True, waits=[tevZ, tk_on32],
                      banks=(obank,))
            trb2 = mm(ps[0:64, 512 * (obank + 1):512 * (obank + 1) + 512],
                      s_on32[64:65, 0:64], OT[64:65, zo + 512:zo + 1024], True, True,
                      waits=[], banks=(obank + 1,))
            trbe = vop(lambda e, st, b=obank: e.tensor_copy(
                out=s_zr[0:64, 0:1024], in_=ps[0:64, 512 * b:512 * b + 1024]),
                waits=[trb1, trb2, NS["rb_free"]])
            rel((obank, obank + 1), trbe)
            NS["zr_free"][par] = trbe
            trec = vop(lambda e, st: e.reciprocal(
                out=s_zr[0:64, 0:1024], in_=s_zr[0:64, 0:1024]), waits=[trbe])
            tdiv = vop(lambda e, st, pp=par: e.tensor_mul(
                s_zn[0:64, 0:1024], OT[0:64, 1024 * pp:1024 * pp + 1024],
                s_zr[0:64, 0:1024]), waits=[trec, tevO, NS["zn_free"]])
            NS["rb_free"] = tdiv
            NS["ot_free"][par] = tdiv
            if jn < 4:
                srcz = s_zn[0:64, :]
                tk_shift = tdiv
            else:
                tk_shift = dma("gpsimd", s_zn[64:128, 0:1024], s_zn[0:64, 0:1024],
                               waits=[tdiv, NS["zn_free"]], sem="dZN")
                srcz = s_zn[64:128, :]
            evs = []
            for i in range(4):
                g = par + 2 * (4 * qcn + i)
                st0 = zchn * WZ + 1 + g
                evs.append(vop(lambda e, st, ii=i, s0=st0, sz=srcz, pb=pmn:
                    e.tensor_copy(
                        out=s_z[pb:pb + 64, s0:s0 + 16 * 255 + 1:16],
                        in_=sz[:, 256 * ii:256 * ii + 256]),
                    waits=[tk_shift, tk_zpad]))
            NS["zn_free"] = evs[-1]
            NS["zloc_last"] = evs[-1]
        NS["pend_norm"] = None

    for j in range(nh):
        sl = j % 2
        xs = s_xblk[:, 2048 * sl:2048 * sl + 2048]
        lns = s_ln1[:, 2048 * sl:2048 * sl + 2048]
        tkx = dma("sync", xs.rearrange("p (c w) -> p c w", c=8),
                  xr3[:, :, 1 + BS * j:1 + BS * j + BS], waits=[xslot_free[sl]],
                  sem="dX%d" % sl)
        tva = bcast_sb(s_mr[0:1, BS * j:BS * j + BS], BS, 2, s_abc[:, 0:256],
                       [tk_rstd, tk_Y_free])
        tvb = bcast_sb(s_mr[0:1, W + BS * j:W + BS * j + BS], BS, 3,
                       s_abc[:, 256:512], [tk_rstd])
        tk_norm = []
        tprev2 = None
        for dc in range(8):
            tn1 = vop(lambda e, st, d=dc, x=xs: e.tensor_sub(
                s_atmp[:, 0:256], x[:, 256 * d:256 * d + 256], s_abc[:, 0:256]),
                waits=[tva, tkx, lnslot_free[sl], tprev2])
            tn2 = vop(lambda e, st, d=dc, l=lns: e.tensor_mul(
                l[:, 256 * d:256 * d + 256], s_atmp[:, 0:256], s_abc[:, 256:512]),
                waits=[tvb, tn1])
            tprev2 = tn2
            tk_norm.append(tn2)
        xslot_free[sl] = tk_norm[-1]
        if "ln1" in tapap and j == 0:
            tk_norm[-1] = fence_v(dma("sync", tapap["ln1"][:, :], lns[:, :],
                                      waits=[tk_norm[-1]], sem="dTP"))

        tk_evY = [None, None]
        tk_kd1 = tk_kd2 = None
        for cc in range(24):
            bk = cc % 2
            tmm = None
            for dc in range(8):
                tmm = mm(bank(bk, 256),
                         s_wq[:, 3072 * dc + 128 * cc:3072 * dc + 128 * cc + 128],
                         lns[:, 256 * dc:256 * dc + 256], start=(dc == 0), stop=(dc == 7),
                         waits=[tk_norm[dc], tk_wq.get((dc, 0)), tk_wq.get((dc, 2048)),
                                tk_Y_free], banks=(bk,))
            tev = vop(lambda e, st, c=cc, b=bk: e.tensor_scalar(
                out=s_Y[:, 256 * c:256 * c + 256], in0=bank(b, 256),
                scalar1=s_bq[:, c:c + 1], scalar2=None, op0=ALU.add),
                waits=[tmm, tk_consts])
            rel((bk,), tev)
            tk_evY[bk] = tev
            if cc == 7:
                tk_kd2 = dma("gpsimd", s_qd[0:64, :], s_Y[64:128, 0:2048],
                             waits=[tev, tk_Y_free], sem="dKD")
            if cc == 15:
                tk_kd1 = dma("gpsimd", s_kdup[0:64, :], s_Y[64:128, 2048:4096],
                             waits=[tev, tk_Y_free], sem="dKD")
        tk_kd2 = tk_kd1
        tk_Yall = [tk_evY[0], tk_evY[1]]
        lnslot_free[sl] = tk_Yall[1]
        if "Y" in tapap and j == 0:
            tk_Yall = [fence_v(dma("sync", tapap["Y"][:, :], s_Y[:, :],
                                   waits=tk_Yall, sem="dTP"))]

        tk_vt = []
        tk_vtev = None
        for grp in range(8):
            tts = []
            for t4 in range(4):
                kc = 4 * grp + t4
                g, hf = kc // 2, kc % 2
                pb = 64 * (g % 2)
                src = s_Y[pb:pb + 64, 4096 + 256 * (g // 2) + 128 * hf:
                          4096 + 256 * (g // 2) + 128 * hf + 128]
                idt = s_ident[pb:pb + 64, pb:pb + 64]
                dst = ps[:, 512 * (4 + t4):512 * (4 + t4) + 32].bitcast(F16)
                tts.append(tr(dst, src, idt,
                              waits=tk_Yall + [tk_id, tk_Y_free], banks=(4 + t4,)))
            tk_vtev = vop(lambda e, st, gg=grp: e.tensor_copy(
                out=s_vt[:, 65 * 4 * gg:65 * 4 * gg + 65 * 4]
                    .rearrange("p (k c) -> p k c", k=4)[:, :, 0:64],
                in_=ps[:, 2048:4096].bitcast(F16)
                    .rearrange("p (k c) -> p k c", k=4)[:, :, 0:64]),
                waits=tts + [tk_vtones, tk_Y_free])
            rel((4, 5, 6, 7), tk_vtev)
            tk_vt.append(tk_vtev)

        pm = 64 * (j // 4)
        zch = j % 4
        texp_of = {}

        def emit_av(kc, texp_of=texp_of, tk_vt=tk_vt):
            slot = kc % 2
            for par in range(2):
                obank = 4 + 2 * par
                tav = None
                for half in range(2):
                    tav = mm(ps[0:65, 512 * obank + 512 * half:
                                 512 * obank + 512 * half + 512],
                             s_vt[:, 65 * kc:65 * kc + 65],
                             s_E[:, 2048 * slot + 1024 * par + 512 * half:
                                 2048 * slot + 1024 * par + 512 * half + 512],
                             start=(kc == 0), stop=(kc == 31),
                             waits=[texp_of[kc][par], tk_vt[kc // 4]],
                             banks=(obank, obank + 1))
                NS["av_last"][par] = tav
                NS["prev_av"][(slot, par)] = tav

        for qc in range(2):
            tk_O = [None, None]
            for kc in range(32):
                g_k, hf = kc // 2, kc % 2
                slot = kc % 2
                if (g_k % 2) == 0:
                    klhs = s_Y[0:64, 2048 + 256 * (g_k // 2) + 128 * hf:
                               2048 + 256 * (g_k // 2) + 128 * hf + 128]
                    kw = tk_Yall
                else:
                    klhs = s_kdup[0:64, 256 * (g_k // 2) + 128 * hf:
                                  256 * (g_k // 2) + 128 * hf + 128]
                    kw = [tk_kd1]
                ts4 = []
                for par in range(2):
                    if par == 0:
                        qrhs = s_Y[0:64, 1024 * qc:1024 * qc + 1024]
                        qw = tk_Yall
                    else:
                        qrhs = s_qd[0:64, 1024 * qc:1024 * qc + 1024]
                        qw = [tk_kd2]
                    sbk = 2 * par
                    ts4.append(mm(bank(sbk), klhs, qrhs[:, 0:512], True, True,
                                  waits=list(kw) + list(qw), banks=(sbk,)))
                    ts4.append(mm(bank(sbk + 1), klhs, qrhs[:, 512:1024], True, True,
                                  waits=[], banks=(sbk + 1,)))
                texp0 = sop(lambda e, st, sl2=slot: e.activation(
                    out=s_E[:, 2048 * sl2:2048 * sl2 + 1024],
                    in_=ps[:, 0:1024], func=AF.Exp,
                    scale=s_ws[:, 5:6], bias=s_ws[:, 4:5]),
                    waits=ts4[0:2] + [NS["prev_av"][(slot, 0)], tk_expc])
                rel((0, 1), texp0)
                texp1 = sop(lambda e, st, sl2=slot: e.activation(
                    out=s_E[:, 2048 * sl2 + 1024:2048 * sl2 + 2048],
                    in_=ps[:, 1024:2048], func=AF.Exp,
                    scale=s_ws[:, 5:6], bias=s_ws[:, 4:5]),
                    waits=ts4[2:4] + [NS["prev_av"][(slot, 1)], tk_expc])
                rel((2, 3), texp1)
                texp_of[kc] = (texp0, texp1)
                if kc == 1 and NS["pend_norm"]:
                    flush_norm()
                if kc > 0:
                    emit_av(kc - 1)
            emit_av(31)
            for par in range(2):
                tk_O[par] = NS["av_last"][par]
            pend = []
            for par in range(2):
                obank = 4 + 2 * par
                tevO = vop(lambda e, st, b=obank, pp=par: e.tensor_copy(
                    out=OT[0:64, 1024 * pp:1024 * pp + 1024],
                    in_=ps[0:64, 512 * b:512 * b + 1024]),
                    waits=[tk_O[par], NS["ot_free"][par]])
                tevZ = vop(lambda e, st, b=obank, pp=par: e.tensor_copy(
                    out=OT[64:65, 1024 * pp:1024 * pp + 1024],
                    in_=ps[64:65, 512 * b:512 * b + 1024]),
                    waits=[tk_O[par], NS["zr_free"][par]])
                rel((obank, obank + 1), tevZ)
                pend.append((par, tevO, tevZ))
            NS["pend_norm"] = (qc, j, pend)
            if qc == 1 and j == nh - 1:
                flush_norm()
        zloc_last = NS["zloc_last"]
        tk_Y_free = NS["av_last"][1]

        for _ in range(per_block):
            if jobsA:
                jobsA.pop(0)()
    while jobsA:
        jobsA.pop(0)()

    tk_zst = []
    for ch in range(4):
        tk_zst.append(dma("gpsimd", zin[128 * ch:128 * ch + 128, :],
                          s_z[:, ch * WZ:ch * WZ + WZ], waits=[zloc_last], sem="dZI"))
    if "zin" in tapap:
        for ch in range(4):
            tkzt = dma("sync", tapap["zin"][128 * ch:128 * ch + 128, :],
                       s_z[:, ch * WZ:ch * WZ + WZ], waits=[zloc_last], sem="dTP")
        zloc_last = fence_v(tkzt)

    if stop_after == "A":
        ms = vop(lambda e, st: e.memset(s_out[:, 0:512], 0.0), waits=[zloc_last])
        tko = dma("sync", ap["out"][0:128, 0:512], s_out[:, 0:512], waits=[ms], sem="dO0")
        p.emit("sync", lambda e, st: e.nop(),
               waits=[tko] + [p.tok(s) for s in ["dC","dX0","dX1","dTP","dW0","dW1","dWD0","dWD1","dKD","dZN","dZI","dZB","dWO0","dWO1","dXR0","dXR1","dO0","dO1","tT","tV","tS","tG","CC"]])
        return _finish(nc, ctx, p)

    tk_cc = p.emit("gpsimd", lambda e, st: e.collective_compute(
        "AllGather", ALU.bypass, replica_groups=groups,
        ins=[zin.ap().opt()], outs=[zgath.ap().opt()]),
        waits=tk_zst, inc="CC", amt=1)

    for jb in jobsB0:
        jb()

    tk_pid = p.emit("gpsimd", lambda e, st: (
        st.__setitem__("T0", e.compute_val((e.partition_id() % 2) * T)), e.nop())[-1],
        waits=[], inc="tG")
    tk_zb = []
    for zc in range(8):
        tk_zb.append(p.emit("gpsimd", lambda e, st, z=zc: e.dma_start(
            out=s_z[:, 2050 * z:2050 * z + 2050],
            in_=zgath[128 * z:128 * z + 128, bass.ds(st["T0"], 2050)]),
            waits=[tk_cc, tk_pid, zloc_last], inc="dZB", amt=16))

    wos_free = [qstate.get(("wos", 0)), qstate.get(("wos", 1))]
    tk_woL = {}

    def wo_load(oc):
        wsl = oc % 2
        tkd = dma("gpsimd", s_wos[:, 1024 * wsl:1024 * wsl + 1024], wod3[:, oc, :],
                  waits=[wos_free[wsl], tk_wocast.get(oc), zloc_last],
                  sem="dWO%d" % wsl)
        tk_woL[oc] = (tkd, wsl)
        return tkd, wsl

    xr_free = [None, None]
    for jb in jobsB1:
        jb()

    if stop_after == "B0":
        p.emit("sync", lambda e, st: e.nop(),
               waits=[p.tok(s) for s in ["dC","dX0","dX1","dTP","dW0","dW1","dWD0","dWD1","dKD","dZN","dZI","dZB","dWO0","dWO1","dXR0","dXR1","dO0","dO1","tT","tV","tS","tG","CC"]])
        return _finish(nc, ctx, p)

    def crosspart_max(waits):
        tt1 = tr(ps[:, 512 * 6:512 * 6 + 128], s_red[:, 0:128], s_id32[:, :],
                 waits=list(waits) + [tk_id32], banks=(6,))
        tt2 = tr(ps[:, 512 * 7:512 * 7 + 128], s_red[:, 128:256], s_id32[:, :],
                 waits=list(waits), banks=(7,))
        tt3 = tr(ps[0:2, 512 * 5:512 * 5 + 128], s_red[:, 256:258], s_id32[:, :],
                 waits=list(waits), banks=(5,))
        r1 = vop(lambda e, st: e.reduce_max(out=s_red[:, 520:521],
                 in_=ps[:, 512 * 6:512 * 6 + 128], axis=AX.X), waits=[tt1])
        r2 = vop(lambda e, st: e.reduce_max(out=s_red[:, 521:522],
                 in_=ps[:, 512 * 7:512 * 7 + 128], axis=AX.X), waits=[tt2])
        r3 = vop(lambda e, st: e.reduce_max(out=s_red[0:2, 522:523],
                 in_=ps[0:2, 512 * 5:512 * 5 + 128], axis=AX.X), waits=[tt3])
        rel((6,), r1)
        rel((7,), r2)
        rel((5,), r3)
        m1 = tr(ps[0:1, 512 * 6:512 * 6 + 128], s_red[:, 520:521], s_id32[:, :],
                waits=[r1], banks=(6,))
        m2 = tr(ps[0:1, 512 * 7:512 * 7 + 128], s_red[:, 521:522], s_id32[:, :],
                waits=[r2], banks=(7,))
        m3 = tr(ps[0:1, 512 * 5:512 * 5 + 2], s_red[0:2, 522:523], s_id32[0:2, 0:2],
                waits=[r3], banks=(5,))
        c1 = vop(lambda e, st: e.tensor_copy(
            out=s_scl[0:1, 0:128], in_=ps[0:1, 512 * 6:512 * 6 + 128]), waits=[m1])
        c2 = vop(lambda e, st: e.tensor_copy(
            out=s_scl[0:1, 128:256], in_=ps[0:1, 512 * 7:512 * 7 + 128]), waits=[m2])
        c3 = vop(lambda e, st: e.tensor_copy(
            out=s_scl[0:1, 256:258], in_=ps[0:1, 512 * 5:512 * 5 + 2]), waits=[m3])
        rel((6,), c1)
        rel((7,), c2)
        rel((5,), c3)
        return [c1, c2, c3], [m1, m2, m3]

    tk_lt_rd = None
    st_guard = [None, None]
    row_guard = qstate.get("rowfree")
    tk_red_guard = []
    tk_scl_guard = [qstate.get("rowfree")]
    tk_xq8_rd = [tk_w2q.get(31)]
    tk_xqb_rd = None
    tk_hy_wr_ok = tk_w2q.get(31)
    tk_x1w_rd = None
    tk_out_dma = [None, None]
    tk_out_done = []
    prev_te1 = None
    for tc in range(8):
        j0 = 256 * tc
        tk_x1w = []
        tk_wm = {}
        for zc in range(8):
            tkwd, wsl = wo_load(zc)
            for oc in range(8):
                tk_wm[oc] = mm(ps[:, 512 * oc:512 * oc + 258],
                               s_wos[:, 1024 * wsl + 128 * oc:1024 * wsl + 128 * oc + 128],
                               s_z[:, 2050 * zc + j0:2050 * zc + j0 + 258],
                               start=(zc == 0), stop=(zc == 7),
                               waits=[tkwd, tk_zb[7]], banks=(oc,))
            wos_free[wsl] = tk_wm[7]
        for oc in range(8):
            sl = oc % 2
            tkxr = dma("sync", s_xr[:, 512 * sl:512 * sl + 258],
                       xr3[:, oc, j0:j0 + 258], waits=[xr_free[sl]], sem="dXR%d" % sl)
            tstt = vop(lambda e, st, o=oc: e.scalar_tensor_tensor(
                out=s_x1w[:, 258 * o:258 * o + 258], in0=ps[:, 512 * o:512 * o + 258],
                scalar=s_bo[:, o:o + 1], in1=s_xr[:, 512 * (o % 2):512 * (o % 2) + 258],
                op0=ALU.add, op1=ALU.add),
                waits=[tk_wm[oc], tkxr, tk_consts, tk_x1w_rd])
            rel((oc,), tstt)
            xr_free[sl] = tstt
            tk_x1w.append(tstt)
        if "x1w" in tapap and tc == 0:
            tk_x1w = [fence_v(dma("sync", tapap["x1w"][:, :], s_x1w[:, :],
                                  waits=tk_x1w, sem="dTP"))] * 8
        tm1 = tm2 = None
        tk_c1rd = [st_guard[0]]
        tk_c2rd = [st_guard[1]]
        for dc in range(8):
            tcst = sop(lambda e, st, d=dc: e.activation(
                out=s_cast1[:, 0:258], in_=s_x1w[:, 258 * d:258 * d + 258],
                func=AF.Copy), waits=[tk_x1w[dc]] + tk_c1rd)
            tsq = sop(lambda e, st, d=dc: e.activation(
                out=s_cast2[:, 0:258], in_=s_x1w[:, 258 * d:258 * d + 258],
                func=AF.Square), waits=[tk_x1w[dc], tcst] + tk_c2rd)
            tm1 = mm(ps[0:1, 1024:1024 + 258], s_ones[:, 0:1], s_cast1[:, 0:258],
                     start=(dc == 0), stop=(dc == 7), waits=[tcst, tk_ones],
                     banks=(2,))
            tm2 = mm(ps[0:1, 1536:1536 + 258], s_ones[:, 0:1], s_cast2[:, 0:258],
                     start=(dc == 0), stop=(dc == 7), waits=[tsq], banks=(3,))
            tk_c1rd, tk_c2rd = [tsq, tm1], [tm2]
        st_guard = [tm1, tm2]
        u1 = vop(lambda e, st, a=j0: e.tensor_scalar(
            out=s_mr[0:1, a:a + 258], in0=ps[0:1, 1024:1024 + 258],
            scalar1=1.0 / D, scalar2=None, op0=ALU.mult), waits=[tm1, tk_mrms])
        u2 = vop(lambda e, st, a=j0: e.tensor_scalar(
            out=s_mr[0:1, W + a:W + a + 258], in0=ps[0:1, 1536:1536 + 258],
            scalar1=1.0 / D, scalar2=None, op0=ALU.mult), waits=[tm2])
        rel((2,), u1)
        rel((3,), u2)
        u3 = vop(lambda e, st, a=j0: e.tensor_mul(
            s_row[0:1, 0:258], s_mr[0:1, a:a + 258], s_mr[0:1, a:a + 258]),
            waits=[u1, row_guard])
        u4 = vop(lambda e, st, a=j0: e.tensor_sub(
            s_mr[0:1, W + a:W + a + 258], s_mr[0:1, W + a:W + a + 258],
            s_row[0:1, 0:258]), waits=[u2, u3])
        u5 = vop(lambda e, st, a=j0: e.tensor_scalar(
            out=s_mr[0:1, W + a:W + a + 258], in0=s_mr[0:1, W + a:W + a + 258],
            scalar1=EPS, scalar2=None, op0=ALU.add), waits=[u4])
        u6 = vop(lambda e, st, a=j0: e.reciprocal(
            out=s_mr[0:1, W + a:W + a + 258], in_=s_mr[0:1, W + a:W + a + 258]),
            waits=[u5])
        row_guard = u3
        tk_rstd2 = sop(lambda e, st, a=j0: e.activation(
            out=s_mr[0:1, W + a:W + a + 258], in_=s_mr[0:1, W + a:W + a + 258],
            func=AF.Sqrt), waits=[u6, tk_rstd])
        tmb = bcast_ps(s_mr[0:1, j0:j0 + 258], 258, 4, [tk_rstd2, u1])
        tk_sub = []
        for dc in range(8):
            ts_ = vop(lambda e, st, d=dc: e.tensor_sub(
                s_lt[:, 258 * d:258 * d + 258], s_x1w[:, 258 * d:258 * d + 258],
                bank(4, 258)), waits=[tmb, tk_lt_rd, tk_x1w[dc]])
            tk_sub.append(ts_)
        rel((4,), tk_sub[-1])
        tb_ = vop(lambda e, st: e.scalar_tensor_tensor(
            out=s_red[:, 0:258], in0=s_lt[:, 0:258], scalar=-1.0,
            in1=s_lt[:, 0:258], op0=ALU.mult, op1=ALU.max),
            waits=[tk_sub[0]] + tk_red_guard)
        for dci in range(1, 8):
            t_a = vop(lambda e, st, d=dci: e.scalar_tensor_tensor(
                out=s_red[:, 258:516], in0=s_lt[:, 258 * d:258 * d + 258],
                scalar=-1.0, in1=s_lt[:, 258 * d:258 * d + 258],
                op0=ALU.mult, op1=ALU.max),
                waits=[tk_sub[dci], tb_])
            tb_ = vop(lambda e, st: e.tensor_tensor(out=s_red[:, 0:258],
                      in0=s_red[:, 0:258], in1=s_red[:, 258:516], op=ALU.max),
                      waits=[t_a])
        cps, mns = crosspart_max([tb_] + tk_scl_guard)
        q1 = vop(lambda e, st, jj=j0: e.tensor_mul(s_scl[0:1, 258:516],
                 s_scl[0:1, 0:258], s_mr[0:1, W + jj:W + jj + 258]),
                 waits=cps + [tk_rstd2])
        q2 = vop(lambda e, st: e.tensor_scalar(out=s_scl[0:1, 774:1032],
                 in0=s_scl[0:1, 258:516], scalar1=1e-5, scalar2=None, op0=ALU.max),
                 waits=[q1])
        q3 = vop(lambda e, st: e.reciprocal(out=s_scl[0:1, 258:516],
                 in_=s_scl[0:1, 774:1032]), waits=[q2])
        q4 = vop(lambda e, st, jj=j0: e.tensor_mul(s_scl[0:1, 258:516],
                 s_scl[0:1, 258:516], s_mr[0:1, W + jj:W + jj + 258]), waits=[q3])
        q5 = vop(lambda e, st: e.tensor_scalar(out=s_scl[0:1, 258:516],
                 in0=s_scl[0:1, 258:516], scalar1=127.0, scalar2=None, op0=ALU.mult),
                 waits=[q4])
        q6 = vop(lambda e, st: e.tensor_scalar(out=s_scl[0:1, 516:774],
                 in0=s_scl[0:1, 774:1032], scalar1=s_scl[0:1, 1032:1033],
                 scalar2=1.0 / 127.0, op0=ALU.mult, op1=ALU.mult),
                 waits=[q2, qstate[("w1", "mean")]])
        if tc == 0:
            q6 = vop(lambda e, st: e.tensor_mul(s_scl[0:1, 516:774], s_scl[0:1, 516:774],
                     s_maskl[0:1, :]), waits=[q6, tk_consts])
        if tc == 7:
            q6 = vop(lambda e, st: e.tensor_mul(s_scl[0:1, 516:774], s_scl[0:1, 516:774],
                     s_maskr[0:1, :]), waits=[q6, tk_consts])
        tcb = bcast_ps(s_scl[0:1, 258:516], 258, 4, [q5])
        tsbv = bcast_sb(s_scl[0:1, 516:774], 258, 5, s_bc[:, 0:258], [q6])
        tq8 = []
        for dc in range(8):
            tq8.append(vop(lambda e, st, d=dc: e.tensor_mul(
                s_xq8[:, 258 * d:258 * d + 258], s_lt[:, 258 * d:258 * d + 258],
                bank(4, 258)), waits=[tcb, tk_sub[dc]] + tk_xq8_rd))
        tk_xq8_rd = []
        rel((4,), tq8[-1])
        tqb = vop(lambda e, st: e.tensor_copy(out=s_xqb[:, :], in_=s_xq8[:, :]),
                  waits=tq8 + [tk_xqb_rd])
        if "w1q" in tapap and tc == 0:
            tqb = fence_v(dma("sync", tapap["w1q"][:, :], s_w1q[:, :],
                              waits=[tqb, tk_w1q.get(7)], sem="dTP"))
        tk_lt_rd = tq8[-1]
        if "xq" in tapap and tc == 0:
            tqb = fence_v(dma("sync", tapap["xq"][:, :], s_xqb[:, :], waits=[tqb],
                              sem="dTP"))
        tk_z2 = []
        prev_tt1 = None
        prev_tt2 = None
        prev_silu = None
        for hc in range(32):
            bk = hc % 4
            tmm = None
            for dc in range(8):
                tmm = mm(bank(bk, 258),
                         s_w1q[:, 4096 * dc + 128 * hc:4096 * dc + 128 * hc + 128],
                         s_xqb[:, 258 * dc:258 * dc + 258], start=(dc == 0),
                         stop=(dc == 7), waits=[tqb, tk_w1q.get(dc)], banks=(bk,))
            tev = vop(lambda e, st, h=hc, b=bk: e.tensor_mul(
                s_hy[:, 258 * h:258 * h + 258], bank(b, 258),
                s_bc[:, 0:258]), waits=[tmm, tsbv, tk_hy_wr_ok, NS["rb_free"]])
            rel((bk,), tev)
            if "hpre" in tapap and tc == 0:
                tev = fence_v(dma("sync", tapap["hpre"][:, 258 * hc:258 * hc + 258],
                                  s_hy[:, 258 * hc:258 * hc + 258], waits=[tev],
                                  sem="dTP"))
            tt0 = sop(lambda e, st, h=hc: e.activation(
                out=s_tc0[:, 0:256], in_=s_hy[:, 258 * h:258 * h + 256],
                func=AF.Identity, bias=s_cb[:, h:h + 1], scale=s_cw[:, h:h + 1]),
                waits=[tev, tk_consts, prev_tt1])
            tt1 = vop(lambda e, st, h=hc: e.scalar_tensor_tensor(
                out=s_tc0[:, 258:258 + 256],
                in0=s_hy[:, 258 * h + 1:258 * h + 257],
                scalar=s_cw[:, 32 + h:32 + h + 1], in1=s_tc0[:, 0:256],
                op0=ALU.mult, op1=ALU.add), waits=[tt0, tev, prev_tt2])
            tt2 = vop(lambda e, st, h=hc: e.scalar_tensor_tensor(
                out=s_tcb[:, 0:256],
                in0=s_hy[:, 258 * h + 2:258 * h + 258],
                scalar=s_cw[:, 64 + h:64 + h + 1], in1=s_tc0[:, 258:258 + 256],
                op0=ALU.mult, op1=ALU.add), waits=[tt1, prev_silu])
            tt3 = sop(lambda e, st, h=hc: e.activation(
                out=s_hy[:, 258 * h:258 * h + 256],
                in_=s_tcb[:, 0:256], func=AF.Silu), waits=[tt2])
            prev_tt1, prev_tt2, prev_silu = tt1, tt2, tt3
            tk_z2.append(tt3)
        if "h" in tapap and tc == 0:
            tk_z2 = [fence_v(dma("sync", tapap["h"][:, :], s_hy[:, :], waits=tk_z2,
                                 sem="dTP"))] * 32
        ab = vop(lambda e, st: e.scalar_tensor_tensor(
            out=s_red[:, 0:256], in0=s_hy[:, 0:256], scalar=-1.0,
            in1=s_hy[:, 0:256], op0=ALU.mult, op1=ALU.max),
            waits=[tk_z2[0]] + mns)
        for hc in range(1, 32):
            t_a = vop(lambda e, st, h=hc: e.scalar_tensor_tensor(
                out=s_red[:, 258:258 + 256], in0=s_hy[:, 258 * h:258 * h + 256],
                scalar=-1.0, in1=s_hy[:, 258 * h:258 * h + 256],
                op0=ALU.mult, op1=ALU.max),
                waits=[tk_z2[hc], ab])
            ab = vop(lambda e, st: e.tensor_tensor(out=s_red[:, 0:256],
                     in0=s_red[:, 0:256], in1=s_red[:, 258:258 + 256], op=ALU.max),
                     waits=[t_a])
        ab2 = vop(lambda e, st: e.memset(s_red[:, 256:258], 0.0), waits=[ab, cps[2]])
        cps2, mns2 = crosspart_max([ab, ab2] + cps)
        r1 = vop(lambda e, st: e.tensor_scalar(out=s_scl[0:1, 774:1032],
                 in0=s_scl[0:1, 0:258], scalar1=1e-5, scalar2=None, op0=ALU.max),
                 waits=cps2 + [q6])
        r2 = vop(lambda e, st: e.reciprocal(out=s_scl[0:1, 0:258],
                 in_=s_scl[0:1, 774:1032]), waits=[r1])
        r3 = vop(lambda e, st: e.tensor_scalar(out=s_scl[0:1, 0:258],
                 in0=s_scl[0:1, 0:258], scalar1=127.0, scalar2=None, op0=ALU.mult),
                 waits=[r2])
        r4 = vop(lambda e, st: e.tensor_scalar(out=s_scl[0:1, 516:774],
                 in0=s_scl[0:1, 774:1032], scalar1=s_scl[0:1, 1033:1034],
                 scalar2=1.0 / 127.0, op0=ALU.mult, op1=ALU.mult),
                 waits=[r1, qstate[("w2", "mean")], tsbv])
        ts2v = bcast_sb(s_scl[0:1, 0:256], 256, 4, s_bc[:, 516:516 + 256],
                        [r3, tcb, tq8[-1]])
        tg2v = bcast_sb(s_scl[0:1, 516:516 + 256], 256, 5, s_bc[:, 258:258 + 256],
                        [r4, tsbv])
        tk_scl_guard = [ts2v, tg2v]
        tcv_prev = [None, None, None, None]
        tmm2 = {oc: None for oc in range(8)}
        last_mm = None
        ty8_last = None
        for hc in range(32):
            sl2 = hc % 4
            ty8 = vop(lambda e, st, h=hc, s=sl2: e.tensor_mul(
                s_y8r[:, 260 * s:260 * s + 256],
                s_hy[:, 258 * h:258 * h + 256], s_bc[:, 516:516 + 256]),
                waits=[ts2v, tk_z2[hc], tcv_prev[sl2], tk_y8ms])
            tcv = vop(lambda e, st, s=sl2: e.tensor_copy(
                out=s_ybr[:, 260 * s:260 * s + 256],
                in_=s_y8r[:, 260 * s:260 * s + 256]),
                waits=[ty8])
            for oc in range(8):
                tmm2[oc] = mm(bank(oc, 256),
                              s_w2q[:, 1024 * hc + 128 * oc:1024 * hc + 128 * oc + 128],
                              s_ybr[:, 260 * sl2:260 * sl2 + 256], start=(hc == 0),
                              stop=(hc == 31), waits=[tcv, tk_w2q.get(hc)],
                              banks=(oc,))
            tcv_prev[sl2] = tmm2[7]
            last_mm = tmm2[7]
            ty8_last = ty8
        tk_hy_wr_ok = ty8_last
        for oc in range(8):
            so_sl = oc % 2
            te1 = vop(lambda e, st, b=oc: e.tensor_mul(
                s_tc0[:, 0:256], bank(b, 256), s_bc[:, 258:258 + 256]),
                waits=[tmm2[oc], tg2v, prev_te1])
            rel((oc,), te1)
            te2 = vop(lambda e, st, o=oc, t=tc, s=so_sl: e.tensor_add(
                s_out[:, 256 * s:256 * s + 256], s_tc0[:, 0:256],
                s_x1w[:, 258 * o + 1:258 * o + 257]),
                waits=[te1, tk_out_dma[so_sl], tk_x1w[oc]])
            prev_te1 = te2
            tko = dma("sync", fm(ap["out"])[:, oc, 256 * tc:256 * tc + 256],
                      s_out[:, 256 * so_sl:256 * so_sl + 256], waits=[te2],
                      sem="dO%d" % so_sl)
            tk_out_dma[so_sl] = tko
            tk_out_done.append(tko)
        tk_x1w_rd = tk_out_done[-1]
        tk_red_guard = mns2 + cps2
        tk_xqb_rd = tk_z2[-1]
        if "xq2" in tapap and tc == 0:
            dma("sync", tapap["xq2"][:, :], s_ybr[:, 0:520],
                waits=[tcv_prev[0], tcv_prev[1]], sem="dTP")

    p.emit("sync", lambda e, st: e.nop(),
           waits=[p.tok(s) for s in ["dC","dX0","dX1","dTP","dW0","dW1","dWD0","dWD1","dKD","dZN","dZI","dZB","dWO0","dWO1","dXR0","dXR1","dO0","dO1","tT","tV","tS","tG","CC"]])
    return _finish(nc, ctx, p)


def _finish(nc, ctx, p):
    sems = {}
    for s in ["dC", "dX0", "dX1", "dTP", "dW0", "dW1", "dWD0", "dWD1", "dKD", "dZN", "dZI",
              "dZB", "dWO0", "dWO1", "dXR0", "dXR1", "dO0", "dO1",
              "tT", "tV", "tS", "tG", "CC"]:
        sems[s] = ctx.enter_context(nc.semaphore(s))
    blk = ctx.enter_context(nc.Block())
    state = {}

    @blk.sync
    def _(e):
        p.run("sync", e, sems, state)

    @blk.tensor
    def _(e):
        p.run("tensor", e, sems, state)

    @blk.vector
    def _(e):
        p.run("vector", e, sems, state)

    @blk.scalar
    def _(e):
        p.run("scalar", e, sems, state)

    @blk.gpsimd
    def _(e):
        p.run("gpsimd", e, sems, state)

    ctx.close()
    return nc



def kernel(**inputs):
    from concourse.bass_utils import run_bass_kernel_spmd
    maps = stage_inputs(inputs, n_cores=8)
    nc = build_nc(n_cores=8, n_heads=NB, stop_after="full", taps=())
    res = run_bass_kernel_spmd(nc, maps, core_ids=list(range(8)))
    return gather_out(res.results)


# version 08 (final)
# speedup vs baseline: 1.3998x; 1.0005x over previous
import numpy as np
import ml_dtypes
from contextlib import ExitStack
import concourse.bass as bass
import concourse.mybir as mybir

F32 = mybir.dt.float32
BF16 = mybir.dt.bfloat16
I8 = mybir.dt.int8
F8 = mybir.dt.float8e4
F16 = mybir.dt.float16
AF = mybir.ActivationFunctionType
ALU = mybir.AluOpType
AX = mybir.AxisListType
BF = ml_dtypes.bfloat16
F16N = np.float16

D = 1024
S = 4096
T = 2048
NB = 8
BS = 256
H = 4096
W = 2050
WZ = 4098
EPS = 1e-5


def qkv_perm():
    perm = np.empty(3072, np.int64)
    for sec in range(3):
        for g in range(16):
            perm[1024 * sec + 64 * g: 1024 * sec + 64 * g + 64] = \
                np.arange(192 * g + 64 * sec, 192 * g + 64 * sec + 64)
    return perm

def zrow_to_cglob():
    out = np.empty(1024, np.int64)
    for r in range(1024):
        rank, rr = divmod(r, 512)
        chunk, part = divmod(rr, 128)
        m = (4 if part >= 64 else 0) + chunk
        out[r] = 64 * (8 * rank + m) + part % 64
    return out

def stage_inputs(inputs, n_cores=8):
    x = np.asarray(inputs["L0_attn_x"], np.float32)
    Wqkv = np.asarray(inputs["L0_attn_W_qkv"], np.float32)
    bqkv = np.asarray(inputs["L0_attn_b_qkv"], np.float32)
    Wo = np.asarray(inputs["L0_attn_W_o"], np.float32)
    bo = np.asarray(inputs["L0_attn_b_o"], np.float32)
    w1 = np.asarray(inputs["L0_ffn_w1"], np.float32)
    cw = np.asarray(inputs["L0_ffn_conv_w"], np.float32)
    cb = np.asarray(inputs["L0_ffn_conv_b"], np.float32)
    w2 = np.asarray(inputs["L0_ffn_w2"], np.float32)

    perm = qkv_perm()
    com = dict(
        WqT=np.ascontiguousarray(Wqkv.T[:, perm]),
        bq_pp=np.ascontiguousarray(bqkv[perm].reshape(24, 128).T),
        WoT=np.ascontiguousarray(Wo.T[zrow_to_cglob(), :]),
        bo_pp=np.ascontiguousarray(bo.reshape(8, 128).T),
        w1T=np.ascontiguousarray(w1.T),
        w2T=np.ascontiguousarray(w2.T),
        cw_pp=np.ascontiguousarray(cw[:, 0, :].reshape(32, 128, 3).transpose(1, 2, 0).reshape(128, 96)),
        cb_pp=np.ascontiguousarray(cb.reshape(32, 128).T),
        ident=np.eye(128, dtype=F16N),
        ident32=np.eye(128, dtype=np.float32),
        ones1=np.ones((128, 1), dtype=F16N),
        ones32=np.ones((128, 128), dtype=np.float32),
    )
    maps = []
    for c in range(n_cores):
        b, off = divmod(c, 2)
        xr = np.zeros((D, W), np.float32)
        if off == 0:
            xr[:, 1:2050] = x[b, 0:2049, :].T
        else:
            xr[:, 0:2049] = x[b, 2047:4096, :].T
        mask_l = np.ones((1, 258), np.float32)
        mask_r = np.ones((1, 258), np.float32)
        if off == 0:
            mask_l[0, 0] = 0.0
        else:
            mask_r[0, 257] = 0.0
        maps.append(dict(com, x_res=xr, mask_l=mask_l, mask_r=mask_r))
    return maps

def gather_out(results):
    out = np.empty((4, S, D), np.float32)
    for c, r in enumerate(results):
        b, off = divmod(c, 2)
        out[b, T * off:T * off + T, :] = np.asarray(r["out"]).T
    return out


class P:
    ENGS = ("sync", "tensor", "vector", "scalar", "gpsimd")

    def __init__(self):
        self.q = {e: [] for e in self.ENGS}
        self.cnt = {}
        self.water = {e: {} for e in self.ENGS}
        self.bkg = [None] * 8

    def tok(self, sem):
        return (sem, self.cnt.get(sem, 0))

    def emit(self, eng, fn, waits=(), inc=None, amt=1):
        best = {}
        for smv in waits:
            if smv is None:
                continue
            s, v = smv
            if v > best.get(s, 0):
                best[s] = v
        wl = []
        for s, v in best.items():
            if v > self.water[eng].get(s, 0):
                self.water[eng][s] = v
                wl.append((s, v))
        tk = None
        if inc is not None:
            self.cnt[inc] = self.cnt.get(inc, 0) + amt
            tk = (inc, self.cnt[inc])
        self.q[eng].append((fn, wl, inc, amt))
        return tk

    def run(self, eng_name, handle, sems, state):
        for fn, wl, inc, amt in self.q[eng_name]:
            for s, v in wl:
                handle.wait_ge(sems[s], v)
            inst = fn(handle, state)
            if inc is not None:
                inst.then_inc(sems[inc], amt)


def build_nc(n_cores=8, n_heads=NB, stop_after="full", taps=(), groups=None):
    nc = bass.Bass("TRN2", target_bir_lowering=False)
    p = P()
    if groups is None:
        groups = [[i, i + 1] for i in range(0, n_cores, 2)]

    ap = {}
    for nm, sh, dt in [("x_res", [D, W], F32), ("WqT", [D, 3072], F32),
                       ("bq_pp", [128, 24], F32), ("WoT", [D, D], F32),
                       ("bo_pp", [128, 8], F32), ("w1T", [D, H], F32),
                       ("w2T", [H, D], F32), ("cw_pp", [128, 96], F32),
                       ("cb_pp", [128, 32], F32), ("ident", [128, 128], F16),
                       ("ident32", [128, 128], F32), ("ones1", [128, 1], F16),
                       ("ones32", [128, 128], F32),
                       ("mask_l", [1, 258], F32), ("mask_r", [1, 258], F32)]:
        ap[nm] = nc.dram_tensor(nm, sh, dt, kind="ExternalInput").ap()
    ap["out"] = nc.dram_tensor("out", [D, T], F32, kind="ExternalOutput").ap()
    zinA = nc.dram_tensor("zinA", [256, WZ], F16)
    zinB = nc.dram_tensor("zinB", [256, WZ], F16)
    zgathA = nc.dram_tensor("zgathA", [512, WZ], F16)
    zgathB = nc.dram_tensor("zgathB", [512, WZ], F16)
    wodram = nc.dram_tensor("wodram", [1024, 1024], F16)

    tap_shapes = {"stats": ([1, 2 * W], F32), "Y": ([128, 24 * BS], F16),
                  "ln1": ([128, 8 * BS], F16), "zin": ([512, WZ], F16),
                  "x1w": ([128, 8 * 258], F32), "xq": ([128, 8 * 258], BF16),
                  "h": ([128, 32 * 258], F32), "xq2": ([128, 2 * 260], BF16),
                  "hpre": ([128, 32 * 258], F32), "w1q": ([128, 32768], F8)}
    tapap = {}
    for tname in taps:
        sh, dt = tap_shapes[tname]
        tapap[tname] = nc.dram_tensor("tap_" + tname, sh, dt, kind="ExternalOutput").ap()

    ctx = ExitStack()
    sbt = lambda n, s, d: ctx.enter_context(nc.sbuf_tensor(n, s, d))

    s_ident = sbt("s_ident", [128, 128], F16)
    s_id32 = sbt("s_id32", [128, 128], F32)
    s_ones = sbt("s_ones", [128, 1], F16)
    s_on32 = sbt("s_on32", [128, 128], F32)
    s_bq = sbt("s_bq", [128, 24], F32)
    s_bo = sbt("s_bo", [128, 8], F32)
    s_cw = sbt("s_cw", [128, 96], F32)
    s_cb = sbt("s_cb", [128, 32], F32)
    s_maskl = sbt("s_maskl", [1, 258], F32)
    s_maskr = sbt("s_maskr", [1, 258], F32)
    s_mr = sbt("s_mr", [1, 2 * W], F32)
    s_scl = sbt("s_scl", [1, 1040], F32)
    s_row = s_scl[0:1, 0:512]
    s_ws = sbt("s_ws", [128, 8], F32)

    AR1 = sbt("AR1", [128, 12288], F32)
    s_wq = AR1[:, 0:12288].bitcast(F16)
    s_w1q = AR1[:, 0:8192].bitcast(F8)
    s_xqb = AR1[:, 8192:9224].bitcast(BF16)
    s_y8r = AR1[:, 10900:11160].bitcast(I8)
    s_ybr = AR1[:, 11160:11680].bitcast(BF16)
    s_bc = AR1[:, 9614:10388]
    s_cast1 = AR1[:, 10388:10644].bitcast(BF16)
    s_cast2 = AR1[:, 10644:10900].bitcast(BF16)

    AR2 = sbt("AR2", [128, 16704], F32)
    s_xblk = AR2[:, 0:4096]
    s_ln1 = AR2[:, 4096:6144].bitcast(F16)
    s_xsq = AR2[:, 6144:7168].bitcast(F16)
    s_Y = AR2[:, 7168:10240].bitcast(F16)
    s_kdup = AR2[:, 10240:11264].bitcast(F16)
    s_vt = AR2[:, 11264:12304].bitcast(BF16)
    s_E = AR2[:, 12304:14352].bitcast(BF16)
    s_zn = AR2[:, 14352:14864].bitcast(F16)
    s_zr = AR2[:, 14864:15888]
    s_qd = AR2[:, 6144:7168].bitcast(F16)
    s_atmp = AR2[:, 15888:16144]
    s_abc = AR2[:, 16144:16656]
    s_x1w = AR2[:, 0:2064]
    s_lt = AR2[:, 2064:4128]
    s_xq8 = AR2[:, 4128:4644].bitcast(I8)
    s_w8t = AR2[:, 4128:4640].bitcast(I8)
    s_tc0 = AR2[:, 4644:5160]
    s_red = AR2[:, 5160:5684]
    s_tcb = AR2[:, 5684:5942]
    s_out = AR2[:, 5942:6454]
    s_xr = AR2[:, 6454:7478]
    s_wos = AR2[:, 7478:8502].bitcast(F16)
    s_w2q = AR2[:, 8502:16694].bitcast(F8)

    AR3 = sbt("AR3", [128, 8256], F32)
    s_wst = AR3[:, 0:4096]
    s_hy = AR3[:, 0:8256]
    OT = AR3[:, 4096:6144]

    ARZ = sbt("ARZ", [128, 8200], F32)
    s_z = ARZ[:, :].bitcast(F16)

    ps = ctx.enter_context(nc.psum_tensor("ps", [128, 4096], F32))

    def bank(i, cols=512, n0=0):
        return ps[:, 512 * i + n0: 512 * i + n0 + cols]

    def dma(eng, out, in_, waits=(), sem=None):
        assert sem is not None
        return p.emit(eng, lambda e, st, o=out, i=in_: e.dma_start(out=o, in_=i),
                      waits=waits, inc=sem, amt=16)

    def mm(out, lhsT, rhs, start, stop, waits=(), banks=()):
        w = list(waits)
        if start:
            w += [p.bkg[b] for b in banks]
        return p.emit("tensor",
                      lambda e, st, o=out, l=lhsT, r=rhs, a=start, b=stop:
                      e.matmul(o, l, r, start=a, stop=b), waits=w, inc="tT")

    def tr(out, in_, idt, waits=(), banks=()):
        w = list(waits) + [p.bkg[b] for b in banks]
        return p.emit("tensor", lambda e, st, o=out, i=in_, d=idt: e.transpose(o, i, d),
                      waits=w, inc="tT")

    def rel(banks, tok):
        for b in banks:
            p.bkg[b] = tok

    def vop(fn, waits=()):
        return p.emit("vector", fn, waits=waits, inc="tV")

    def sop(fn, waits=()):
        return p.emit("scalar", fn, waits=waits, inc="tS")

    def pop(fn, waits=()):
        return p.emit("gpsimd", fn, waits=waits, inc="tG")

    def fence_v(tok):
        return vop(lambda e, st: e.memset(s_ws[:, 6:7], 0.0), waits=[tok])

    def fm(dram_ap):
        return dram_ap.rearrange("(c p) w -> p c w", p=128)

    for o_, i_ in [(s_ident[:, :], ap["ident"][:, :]), (s_id32[:, :], ap["ident32"][:, :]),
                   (s_ones[:, :], ap["ones1"][:, :]), (s_on32[:, :], ap["ones32"][:, :]),
                   (s_bq[:, :], ap["bq_pp"][:, :]), (s_bo[:, :], ap["bo_pp"][:, :]),
                   (s_cw[:, :], ap["cw_pp"][:, :]), (s_cb[:, :], ap["cb_pp"][:, :]),
                   (s_maskl[:, :], ap["mask_l"][:, :]), (s_maskr[:, :], ap["mask_r"][:, :])]:
        dma("sync", o_, i_, sem="dC")
    tk_consts = p.tok("dC")
    tk_id = tk_id32 = tk_ones = tk_on32 = tk_consts
    tk_vtones = vop(lambda e, st: e.memset(s_vt[:, 64:32 * 65:65], 1.0))
    tk_zpad = vop(lambda e, st: e.memset(s_z[:, :], 0.0))
    tk_y8ms = vop(lambda e, st: e.memset(s_y8r[:, :], 0))
    tk_mrms = vop(lambda e, st: e.memset(s_mr[0:1, :], 0.0))
    vop(lambda e, st: e.memset(s_ws[:, 4:5], -80.0))
    tk_expc = vop(lambda e, st: e.memset(s_ws[:, 5:6], 0.125))

    def bcast_sb(row_ap, n, bnk, dst, waits):
        tm = mm(bank(bnk, n), s_on32[0:1, 0:128], row_ap, True, True,
                waits=list(waits) + [tk_on32], banks=(bnk,))
        tv = vop(lambda e, st, d=dst, b=bnk, k=n: e.tensor_copy(out=d, in_=bank(b, k)),
                 waits=[tm])
        rel((bnk,), tv)
        return tv

    def bcast_ps(row_ap, n, bnk, waits):
        return mm(bank(bnk, n), s_on32[0:1, 0:128], row_ap, True, True,
                  waits=list(waits) + [tk_on32], banks=(bnk,))

    xr3 = fm(ap["x_res"])
    WqT3 = fm(ap["WqT"])
    WoT3 = fm(ap["WoT"])
    w1T3 = fm(ap["w1T"])
    w2T3 = fm(ap["w2T"])
    wod3 = fm(wodram.ap())

    xslot_free = [None, None]
    tk_stats = []
    for j in range(NB):
        sl = j % 2
        xs = s_xblk[:, 2048 * sl:2048 * sl + 2048]
        lns = s_ln1[:, 2048 * sl:2048 * sl + 2048]
        tkx = dma("sync", xs.rearrange("p (c w) -> p c w", c=8),
                  xr3[:, :, 1 + BS * j:1 + BS * j + BS], waits=[xslot_free[sl]],
                  sem="dX%d" % sl)
        tkb = vop(lambda e, st, o=lns, i=xs: e.tensor_copy(out=o, in_=i), waits=[tkx])
        tkq = vop(lambda e, st, i=lns: e.tensor_mul(s_xsq[:, :], i, i),
                  waits=[tkb, tk_stats[-1] if tk_stats else None])
        tm1 = tm2 = None
        for dc in range(8):
            tm1 = mm(ps[0:1, 1024:1024 + 256], s_ones[:, 0:1],
                     lns[:, 256 * dc:256 * dc + 256], start=(dc == 0), stop=(dc == 7),
                     waits=[tkb, tk_ones], banks=(2,))
            tm2 = mm(ps[0:1, 1536:1536 + 256], s_ones[:, 0:1],
                     s_xsq[:, 256 * dc:256 * dc + 256], start=(dc == 0), stop=(dc == 7),
                     waits=[tkq], banks=(3,))
        t1 = vop(lambda e, st, j=j: e.tensor_scalar(
            out=s_mr[0:1, BS * j:BS * j + BS], in0=ps[0:1, 1024:1024 + 256],
            scalar1=1.0 / D, scalar2=None, op0=ALU.mult), waits=[tm1, tk_mrms])
        t2 = vop(lambda e, st, j=j: e.tensor_scalar(
            out=s_mr[0:1, W + BS * j:W + BS * j + BS], in0=ps[0:1, 1536:1536 + 256],
            scalar1=1.0 / D, scalar2=None, op0=ALU.mult), waits=[tm2])
        rel((2,), t1)
        rel((3,), t2)
        t3 = vop(lambda e, st, j=j: e.tensor_mul(
            s_row[0:1, 0:BS], s_mr[0:1, BS * j:BS * j + BS],
            s_mr[0:1, BS * j:BS * j + BS]), waits=[t1])
        t4 = vop(lambda e, st, j=j: e.tensor_sub(
            s_mr[0:1, W + BS * j:W + BS * j + BS],
            s_mr[0:1, W + BS * j:W + BS * j + BS], s_row[0:1, 0:BS]), waits=[t2, t3])
        t5 = vop(lambda e, st, j=j: e.tensor_scalar(
            out=s_mr[0:1, W + BS * j:W + BS * j + BS],
            in0=s_mr[0:1, W + BS * j:W + BS * j + BS],
            scalar1=EPS, scalar2=None, op0=ALU.add), waits=[t4])
        t6 = vop(lambda e, st, j=j: e.reciprocal(
            out=s_mr[0:1, W + BS * j:W + BS * j + BS],
            in_=s_mr[0:1, W + BS * j:W + BS * j + BS]), waits=[t5])
        tk_stats.append(t6)
        xslot_free[sl] = tkb
    tk_rstd = sop(lambda e, st: e.activation(
        out=s_mr[0:1, W:W + T], in_=s_mr[0:1, W:W + T], func=AF.Sqrt),
        waits=[tk_stats[-1]])

    if "stats" in tapap:
        dma("sync", tapap["stats"][0:1, :], s_mr[0:1, :], waits=[tk_rstd], sem="dTP")

    wst_free = [None, None]
    slot_ctr = [0]
    qstate = {}

    def sub_chunks(width):
        return [(a, min(2048, width - a)) for a in range(0, width, 2048)]

    def j_cast(src_sl, dst_sl, toks, key):
        def go():
            sl = slot_ctr[0] & 1
            slot_ctr[0] += 1
            n = src_sl.shape[-1]
            tkd = dma("sync", s_wst[:, 2048 * sl:2048 * sl + n], src_sl,
                      waits=[wst_free[sl]], sem="dW%d" % sl)
            tkv = vop(lambda e, st, s=sl, d=dst_sl, nn=n: e.tensor_copy(
                out=d, in_=s_wst[:, 2048 * s:2048 * s + nn]), waits=[tkd])
            wst_free[sl] = tkv
            toks[key] = tkv
        return go

    def j_q1(src_sl, key, last, nel):
        def go():
            sl = slot_ctr[0] & 1
            slot_ctr[0] += 1
            n = src_sl.shape[-1]
            tkd = dma("sync", s_wst[:, 2048 * sl:2048 * sl + n], src_sl,
                      waits=[wst_free[sl]], sem="dW%d" % sl)
            tkr = vop(lambda e, st, s=sl, nn=n: e.reduce_sum(
                out=s_ws[:, 3:4], in_=s_wst[:, 2048 * s:2048 * s + nn],
                axis=AX.X, apply_absolute_value=True),
                waits=[tkd, qstate.get((key, "redfree"))])
            if (key, "acc") not in qstate:
                ta = vop(lambda e, st: e.tensor_copy(out=s_ws[:, 2:3], in_=s_ws[:, 3:4]),
                         waits=[tkr])
            else:
                ta = vop(lambda e, st: e.tensor_add(
                    s_ws[:, 2:3], s_ws[:, 2:3], s_ws[:, 3:4]),
                    waits=[tkr, qstate.get((key, "acc"))])
            qstate[(key, "acc")] = ta
            qstate[(key, "redfree")] = ta
            wst_free[sl] = tkr
            if last:
                tt = tr(ps[0:1, 1024:1152], s_ws[:, 2:3], s_id32[:, :],
                        waits=[ta, tk_id32], banks=(2,))
                tsum = vop(lambda e, st: e.reduce_sum(
                    out=s_scl[0:1, 1036:1037], in_=ps[0:1, 1024:1152], axis=AX.X),
                    waits=[tt, qstate.get("rowfree")])
                rel((2,), tsum)
                mcol = 1032 if key == "w1" else 1033
                tmean = vop(lambda e, st, mc=mcol, ne=nel: e.tensor_scalar(
                    out=s_scl[0:1, mc:mc + 1], in0=s_scl[0:1, 1036:1037],
                    scalar1=1.0 / ne, scalar2=None, op0=ALU.mult), waits=[tsum])
                trec = vop(lambda e, st, mc=mcol: e.reciprocal(
                    out=s_scl[0:1, 1037:1038], in_=s_scl[0:1, mc:mc + 1]),
                    waits=[tmean])
                wcol = 0 if key == "w1" else 1
                tbm = mm(ps[:, 1024:1025], s_on32[0:1, 0:128], s_scl[0:1, 1037:1038],
                         True, True, waits=[trec, tk_on32], banks=(2,))
                tbv = vop(lambda e, st, wc=wcol: e.tensor_copy(
                    out=s_ws[:, wc:wc + 1], in_=ps[:, 1024:1025]), waits=[tbm])
                rel((2,), tbv)
                qstate[(key, "winv")] = tbv
                qstate[(key, "mean")] = tmean
                qstate["rowfree"] = trec
        return go

    def j_q2(src_sl, dst_sl, key, toks, tkey):
        def go():
            sl = slot_ctr[0] & 1
            slot_ctr[0] += 1
            n = src_sl.shape[-1]
            wcol = 0 if key == "w1" else 1
            tkd = dma("sync", s_wst[:, 2048 * sl:2048 * sl + n], src_sl,
                      waits=[wst_free[sl]], sem="dW%d" % sl)
            ta = vop(lambda e, st, s=sl, nn=n, wc=wcol: e.tensor_scalar(
                out=s_wst[:, 2048 * s:2048 * s + nn],
                in0=s_wst[:, 2048 * s:2048 * s + nn],
                scalar1=s_ws[:, wc:wc + 1], scalar2=None, op0=ALU.mult),
                waits=[tkd, qstate[(key, "winv")]])
            tc8 = vop(lambda e, st, s=sl, nn=n: e.tensor_scalar(
                out=s_w8t[:, 0:nn], in0=s_wst[:, 2048 * s:2048 * s + nn],
                scalar1=-1.0, scalar2=1.0, op0=ALU.max, op1=ALU.min),
                waits=[ta, qstate.get("w8free")])
            td = vop(lambda e, st, d=dst_sl, nn=n: e.tensor_copy(
                out=d, in_=s_w8t[:, 0:nn]), waits=[tc8])
            qstate["w8free"] = td
            wst_free[sl] = tc8
            toks[tkey] = td
        return go

    jobsA = []
    tk_wq = {}
    for kc in range(8):
        for (a, n) in sub_chunks(3072):
            jobsA.append(j_cast(WqT3[:, kc, a:a + n],
                                s_wq[:, 3072 * kc + a:3072 * kc + a + n], tk_wq,
                                (kc, a)))
    w1subs = [(kc, a, n) for kc in range(8) for (a, n) in sub_chunks(4096)]
    for i, (kc, a, n) in enumerate(w1subs):
        jobsA.append(j_q1(w1T3[:, kc, a:a + n], "w1", i == len(w1subs) - 1, D * H))
    for i, kc in enumerate(range(32)):
        jobsA.append(j_q1(w2T3[:, kc, :], "w2", i == 31, D * H))

    jobsB0 = []
    tk_wocast = {}
    for kc in range(8):
        def mk(kc=kc):
            def go():
                sl = slot_ctr[0] & 1
                slot_ctr[0] += 1
                tkd = dma("sync", s_wst[:, 2048 * sl:2048 * sl + 1024],
                          WoT3[:, kc, :], waits=[wst_free[sl]], sem="dW%d" % sl)
                wsl = kc % 2
                tkv = vop(lambda e, st, s=sl, ws=wsl: e.tensor_copy(
                    out=s_wos[:, 1024 * ws:1024 * ws + 1024],
                    in_=s_wst[:, 2048 * s:2048 * s + 1024]),
                    waits=[tkd, qstate.get(("wos", wsl))])
                tks = dma("sync", wod3[:, kc, :],
                          s_wos[:, 1024 * wsl:1024 * wsl + 1024], waits=[tkv],
                          sem="dWD%d" % wsl)
                qstate[("wos", wsl)] = tks
                wst_free[sl] = tkv
                tk_wocast[kc] = tks
            return go
        jobsB0.append(mk())
    tk_w1q = {}
    for kc in range(8):
        for (a, n) in sub_chunks(4096):
            jobsB0.append(j_q2(w1T3[:, kc, a:a + n],
                               s_w1q[:, 4096 * kc + a:4096 * kc + a + n], "w1",
                               tk_w1q, kc))
    jobsB1 = []
    tk_w2q = {}
    for kc in range(32):
        jobsB1.append(j_q2(w2T3[:, kc, :], s_w2q[:, 1024 * kc:1024 * kc + 1024], "w2",
                           tk_w2q, kc))

    for jb in jobsA[:16]:
        jb()
    jobsA = jobsA[16:]
    nh = min(n_heads, NB)
    per_block = (len(jobsA) + nh - 1) // max(nh, 1)

    lnslot_free = [None, None]
    tk_Y_free = None
    zloc_last = None
    NS = {"prev_av": {(0, 0): None, (0, 1): None, (1, 0): None, (1, 1): None},
          "av_last": [None, None], "zn_free": None,
          "rb_free": None, "zr_free": [None, None], "ot_free": [None, None],
          "pend_norm": None, "zloc_last": None}

    def flush_norm():
        qcn, jn, pend = NS["pend_norm"]
        pmn = 64 * (jn // 4)
        zchn = jn % 4
        for (par, tevO, tevZ) in pend:
            obank = 4 + 2 * par
            zo = 1024 * par
            trb1 = mm(ps[0:64, 512 * obank:512 * obank + 512], s_on32[64:65, 0:64],
                      OT[64:65, zo:zo + 512], True, True, waits=[tevZ, tk_on32],
                      banks=(obank,))
            trb2 = mm(ps[0:64, 512 * (obank + 1):512 * (obank + 1) + 512],
                      s_on32[64:65, 0:64], OT[64:65, zo + 512:zo + 1024], True, True,
                      waits=[], banks=(obank + 1,))
            trbe = vop(lambda e, st, b=obank: e.tensor_copy(
                out=s_zr[0:64, 0:1024], in_=ps[0:64, 512 * b:512 * b + 1024]),
                waits=[trb1, trb2, NS["rb_free"]])
            rel((obank, obank + 1), trbe)
            NS["zr_free"][par] = trbe
            trec = vop(lambda e, st: e.reciprocal(
                out=s_zr[0:64, 0:1024], in_=s_zr[0:64, 0:1024]), waits=[trbe])
            tdiv = vop(lambda e, st, pp=par: e.tensor_mul(
                s_zn[0:64, 0:1024], OT[0:64, 1024 * pp:1024 * pp + 1024],
                s_zr[0:64, 0:1024]), waits=[trec, tevO, NS["zn_free"]])
            NS["rb_free"] = tdiv
            NS["ot_free"][par] = tdiv
            if jn < 4:
                srcz = s_zn[0:64, :]
                tk_shift = tdiv
            else:
                tk_shift = dma("gpsimd", s_zn[64:128, 0:1024], s_zn[0:64, 0:1024],
                               waits=[tdiv, NS["zn_free"]], sem="dZN")
                srcz = s_zn[64:128, :]
            evs = []
            for i in range(4):
                g = par + 2 * (4 * qcn + i)
                st0 = zchn * WZ + 1 + g
                evs.append(vop(lambda e, st, ii=i, s0=st0, sz=srcz, pb=pmn:
                    e.tensor_copy(
                        out=s_z[pb:pb + 64, s0:s0 + 16 * 255 + 1:16],
                        in_=sz[:, 256 * ii:256 * ii + 256]),
                    waits=[tk_shift, tk_zpad]))
            NS["zn_free"] = evs[-1]
            NS["zloc_last"] = evs[-1]
        NS["pend_norm"] = None

    for j in range(nh):
        sl = j % 2
        xs = s_xblk[:, 2048 * sl:2048 * sl + 2048]
        lns = s_ln1[:, 2048 * sl:2048 * sl + 2048]
        tkx = dma("sync", xs.rearrange("p (c w) -> p c w", c=8),
                  xr3[:, :, 1 + BS * j:1 + BS * j + BS], waits=[xslot_free[sl]],
                  sem="dX%d" % sl)
        tva = bcast_sb(s_mr[0:1, BS * j:BS * j + BS], BS, 2, s_abc[:, 0:256],
                       [tk_rstd, tk_Y_free])
        tvb = bcast_sb(s_mr[0:1, W + BS * j:W + BS * j + BS], BS, 3,
                       s_abc[:, 256:512], [tk_rstd])
        tk_norm = []
        tprev2 = None
        for dc in range(8):
            tn1 = vop(lambda e, st, d=dc, x=xs: e.tensor_sub(
                s_atmp[:, 0:256], x[:, 256 * d:256 * d + 256], s_abc[:, 0:256]),
                waits=[tva, tkx, lnslot_free[sl], tprev2])
            tn2 = vop(lambda e, st, d=dc, l=lns: e.tensor_mul(
                l[:, 256 * d:256 * d + 256], s_atmp[:, 0:256], s_abc[:, 256:512]),
                waits=[tvb, tn1])
            tprev2 = tn2
            tk_norm.append(tn2)
        xslot_free[sl] = tk_norm[-1]
        if "ln1" in tapap and j == 0:
            tk_norm[-1] = fence_v(dma("sync", tapap["ln1"][:, :], lns[:, :],
                                      waits=[tk_norm[-1]], sem="dTP"))

        tk_evY = [None, None]
        tk_kd1 = tk_kd2 = None
        for cc in range(24):
            bk = cc % 2
            tmm = None
            for dc in range(8):
                tmm = mm(bank(bk, 256),
                         s_wq[:, 3072 * dc + 128 * cc:3072 * dc + 128 * cc + 128],
                         lns[:, 256 * dc:256 * dc + 256], start=(dc == 0), stop=(dc == 7),
                         waits=[tk_norm[dc], tk_wq.get((dc, 0)), tk_wq.get((dc, 2048)),
                                tk_Y_free], banks=(bk,))
            tev = vop(lambda e, st, c=cc, b=bk: e.tensor_scalar(
                out=s_Y[:, 256 * c:256 * c + 256], in0=bank(b, 256),
                scalar1=s_bq[:, c:c + 1], scalar2=None, op0=ALU.add),
                waits=[tmm, tk_consts])
            rel((bk,), tev)
            tk_evY[bk] = tev
            if cc == 7:
                tk_kd2 = dma("gpsimd", s_qd[0:64, :], s_Y[64:128, 0:2048],
                             waits=[tev, tk_Y_free], sem="dKD")
            if cc == 15:
                tk_kd1 = dma("gpsimd", s_kdup[0:64, :], s_Y[64:128, 2048:4096],
                             waits=[tev, tk_Y_free], sem="dKD")
        tk_kd2 = tk_kd1
        tk_Yall = [tk_evY[0], tk_evY[1]]
        lnslot_free[sl] = tk_Yall[1]
        if "Y" in tapap and j == 0:
            tk_Yall = [fence_v(dma("sync", tapap["Y"][:, :], s_Y[:, :],
                                   waits=tk_Yall, sem="dTP"))]

        tk_vt = []
        tk_vtev = None
        for grp in range(8):
            tts = []
            for t4 in range(4):
                kc = 4 * grp + t4
                g, hf = kc // 2, kc % 2
                pb = 64 * (g % 2)
                src = s_Y[pb:pb + 64, 4096 + 256 * (g // 2) + 128 * hf:
                          4096 + 256 * (g // 2) + 128 * hf + 128]
                idt = s_ident[pb:pb + 64, pb:pb + 64]
                dst = ps[:, 512 * (4 + t4):512 * (4 + t4) + 32].bitcast(F16)
                tts.append(tr(dst, src, idt,
                              waits=tk_Yall + [tk_id, tk_Y_free], banks=(4 + t4,)))
            tk_vtev = vop(lambda e, st, gg=grp: e.tensor_copy(
                out=s_vt[:, 65 * 4 * gg:65 * 4 * gg + 65 * 4]
                    .rearrange("p (k c) -> p k c", k=4)[:, :, 0:64],
                in_=ps[:, 2048:4096].bitcast(F16)
                    .rearrange("p (k c) -> p k c", k=4)[:, :, 0:64]),
                waits=tts + [tk_vtones, tk_Y_free])
            rel((4, 5, 6, 7), tk_vtev)
            tk_vt.append(tk_vtev)

        pm = 64 * (j // 4)
        zch = j % 4
        texp_of = {}

        def emit_av(kc, texp_of=texp_of, tk_vt=tk_vt):
            slot = kc % 2
            for par in range(2):
                obank = 4 + 2 * par
                tav = None
                for half in range(2):
                    tav = mm(ps[0:65, 512 * obank + 512 * half:
                                 512 * obank + 512 * half + 512],
                             s_vt[:, 65 * kc:65 * kc + 65],
                             s_E[:, 2048 * slot + 1024 * par + 512 * half:
                                 2048 * slot + 1024 * par + 512 * half + 512],
                             start=(kc == 0), stop=(kc == 31),
                             waits=[texp_of[kc][par], tk_vt[kc // 4]],
                             banks=(obank, obank + 1))
                NS["av_last"][par] = tav
                NS["prev_av"][(slot, par)] = tav

        for qc in range(2):
            tk_O = [None, None]
            for kc in range(32):
                g_k, hf = kc // 2, kc % 2
                slot = kc % 2
                if (g_k % 2) == 0:
                    klhs = s_Y[0:64, 2048 + 256 * (g_k // 2) + 128 * hf:
                               2048 + 256 * (g_k // 2) + 128 * hf + 128]
                    kw = tk_Yall
                else:
                    klhs = s_kdup[0:64, 256 * (g_k // 2) + 128 * hf:
                                  256 * (g_k // 2) + 128 * hf + 128]
                    kw = [tk_kd1]
                ts4 = []
                for par in range(2):
                    if par == 0:
                        qrhs = s_Y[0:64, 1024 * qc:1024 * qc + 1024]
                        qw = tk_Yall
                    else:
                        qrhs = s_qd[0:64, 1024 * qc:1024 * qc + 1024]
                        qw = [tk_kd2]
                    sbk = 2 * par
                    ts4.append(mm(bank(sbk), klhs, qrhs[:, 0:512], True, True,
                                  waits=list(kw) + list(qw), banks=(sbk,)))
                    ts4.append(mm(bank(sbk + 1), klhs, qrhs[:, 512:1024], True, True,
                                  waits=[], banks=(sbk + 1,)))
                texp0 = sop(lambda e, st, sl2=slot: e.activation(
                    out=s_E[:, 2048 * sl2:2048 * sl2 + 1024],
                    in_=ps[:, 0:1024], func=AF.Exp,
                    scale=s_ws[:, 5:6], bias=s_ws[:, 4:5]),
                    waits=ts4[0:2] + [NS["prev_av"][(slot, 0)], tk_expc])
                rel((0, 1), texp0)
                texp1 = sop(lambda e, st, sl2=slot: e.activation(
                    out=s_E[:, 2048 * sl2 + 1024:2048 * sl2 + 2048],
                    in_=ps[:, 1024:2048], func=AF.Exp,
                    scale=s_ws[:, 5:6], bias=s_ws[:, 4:5]),
                    waits=ts4[2:4] + [NS["prev_av"][(slot, 1)], tk_expc])
                rel((2, 3), texp1)
                texp_of[kc] = (texp0, texp1)
                if kc == 1 and NS["pend_norm"]:
                    flush_norm()
                if kc > 0:
                    emit_av(kc - 1)
            emit_av(31)
            for par in range(2):
                tk_O[par] = NS["av_last"][par]
            pend = []
            for par in range(2):
                obank = 4 + 2 * par
                tevO = vop(lambda e, st, b=obank, pp=par: e.tensor_copy(
                    out=OT[0:64, 1024 * pp:1024 * pp + 1024],
                    in_=ps[0:64, 512 * b:512 * b + 1024]),
                    waits=[tk_O[par], NS["ot_free"][par]])
                tevZ = vop(lambda e, st, b=obank, pp=par: e.tensor_copy(
                    out=OT[64:65, 1024 * pp:1024 * pp + 1024],
                    in_=ps[64:65, 512 * b:512 * b + 1024]),
                    waits=[tk_O[par], NS["zr_free"][par]])
                rel((obank, obank + 1), tevZ)
                pend.append((par, tevO, tevZ))
            NS["pend_norm"] = (qc, j, pend)
            if qc == 1 and j == nh - 1:
                flush_norm()
        if j == 3 and NS["pend_norm"]:
            flush_norm()
            tkA = []
            for ch in range(4):
                tkA.append(dma("gpsimd", zinA[64 * ch:64 * ch + 64, :],
                               s_z[0:64, ch * WZ:ch * WZ + WZ],
                               waits=[NS["zloc_last"]], sem="dZI"))
            NS["tk_cc1"] = p.emit("gpsimd", lambda e, st: e.collective_compute(
                "AllGather", ALU.bypass, replica_groups=groups,
                ins=[zinA.ap().opt()], outs=[zgathA.ap().opt()]),
                waits=tkA, inc="CC", amt=1)
        zloc_last = NS["zloc_last"]
        tk_Y_free = NS["av_last"][1]

        for _ in range(per_block):
            if jobsA:
                jobsA.pop(0)()
    while jobsA:
        jobsA.pop(0)()

    tk_zst = []
    for ch in range(4):
        tk_zst.append(dma("gpsimd", zinB[64 * ch:64 * ch + 64, :],
                          s_z[64:128, ch * WZ:ch * WZ + WZ], waits=[zloc_last],
                          sem="dZI"))
    if "zin" in tapap:
        for ch in range(4):
            tkzt = dma("sync", tapap["zin"][128 * ch:128 * ch + 128, :],
                       s_z[:, ch * WZ:ch * WZ + WZ], waits=[zloc_last], sem="dTP")
        zloc_last = fence_v(tkzt)

    if stop_after == "A":
        ms = vop(lambda e, st: e.memset(s_out[:, 0:512], 0.0), waits=[zloc_last])
        tko = dma("sync", ap["out"][0:128, 0:512], s_out[:, 0:512], waits=[ms], sem="dO0")
        p.emit("sync", lambda e, st: e.nop(),
               waits=[tko] + [p.tok(s) for s in ["dC","dX0","dX1","dTP","dW0","dW1","dWD0","dWD1","dKD","dZN","dZI","dZB","dWO0","dWO1","dXR0","dXR1","dO0","dO1","tT","tV","tS","tG","CC"]])
        return _finish(nc, ctx, p)

    tk_cc = p.emit("gpsimd", lambda e, st: e.collective_compute(
        "AllGather", ALU.bypass, replica_groups=groups,
        ins=[zinB.ap().opt()], outs=[zgathB.ap().opt()]),
        waits=tk_zst + [NS["tk_cc1"]], inc="CC", amt=1)

    for jb in jobsB0:
        jb()

    tk_pid = p.emit("gpsimd", lambda e, st: (
        st.__setitem__("T0", e.compute_val((e.partition_id() % 2) * T)), e.nop())[-1],
        waits=[], inc="tG")
    tk_zb = []
    for zc in range(8):
        r, zc4 = zc // 4, zc % 4
        tk_zb.append(p.emit("gpsimd", lambda e, st, z=zc, rr=r, z4=zc4: e.dma_start(
            out=s_z[0:64, 2050 * z:2050 * z + 2050],
            in_=zgathA[256 * rr + 64 * z4:256 * rr + 64 * z4 + 64,
                       bass.ds(st["T0"], 2050)]),
            waits=[tk_cc, tk_pid, zloc_last], inc="dZB", amt=16))
        tk_zb.append(p.emit("gpsimd", lambda e, st, z=zc, rr=r, z4=zc4: e.dma_start(
            out=s_z[64:128, 2050 * z:2050 * z + 2050],
            in_=zgathB[256 * rr + 64 * z4:256 * rr + 64 * z4 + 64,
                       bass.ds(st["T0"], 2050)]),
            waits=[tk_cc, tk_pid, zloc_last], inc="dZB", amt=16))

    wos_free = [qstate.get(("wos", 0)), qstate.get(("wos", 1))]
    tk_woL = {}

    def wo_load(oc):
        wsl = oc % 2
        tkd = dma("gpsimd", s_wos[:, 1024 * wsl:1024 * wsl + 1024], wod3[:, oc, :],
                  waits=[wos_free[wsl], tk_wocast.get(oc), zloc_last],
                  sem="dWO%d" % wsl)
        tk_woL[oc] = (tkd, wsl)
        return tkd, wsl

    xr_free = [None, None]
    for jb in jobsB1:
        jb()

    if stop_after == "B0":
        p.emit("sync", lambda e, st: e.nop(),
               waits=[p.tok(s) for s in ["dC","dX0","dX1","dTP","dW0","dW1","dWD0","dWD1","dKD","dZN","dZI","dZB","dWO0","dWO1","dXR0","dXR1","dO0","dO1","tT","tV","tS","tG","CC"]])
        return _finish(nc, ctx, p)

    def crosspart_max(waits):
        tt1 = tr(ps[:, 512 * 6:512 * 6 + 128], s_red[:, 0:128], s_id32[:, :],
                 waits=list(waits) + [tk_id32], banks=(6,))
        tt2 = tr(ps[:, 512 * 7:512 * 7 + 128], s_red[:, 128:256], s_id32[:, :],
                 waits=list(waits), banks=(7,))
        tt3 = tr(ps[0:2, 512 * 5:512 * 5 + 128], s_red[:, 256:258], s_id32[:, :],
                 waits=list(waits), banks=(5,))
        r1 = vop(lambda e, st: e.reduce_max(out=s_red[:, 520:521],
                 in_=ps[:, 512 * 6:512 * 6 + 128], axis=AX.X), waits=[tt1])
        r2 = vop(lambda e, st: e.reduce_max(out=s_red[:, 521:522],
                 in_=ps[:, 512 * 7:512 * 7 + 128], axis=AX.X), waits=[tt2])
        r3 = vop(lambda e, st: e.reduce_max(out=s_red[0:2, 522:523],
                 in_=ps[0:2, 512 * 5:512 * 5 + 128], axis=AX.X), waits=[tt3])
        rel((6,), r1)
        rel((7,), r2)
        rel((5,), r3)
        m1 = tr(ps[0:1, 512 * 6:512 * 6 + 128], s_red[:, 520:521], s_id32[:, :],
                waits=[r1], banks=(6,))
        m2 = tr(ps[0:1, 512 * 7:512 * 7 + 128], s_red[:, 521:522], s_id32[:, :],
                waits=[r2], banks=(7,))
        m3 = tr(ps[0:1, 512 * 5:512 * 5 + 2], s_red[0:2, 522:523], s_id32[0:2, 0:2],
                waits=[r3], banks=(5,))
        c1 = vop(lambda e, st: e.tensor_copy(
            out=s_scl[0:1, 0:128], in_=ps[0:1, 512 * 6:512 * 6 + 128]), waits=[m1])
        c2 = vop(lambda e, st: e.tensor_copy(
            out=s_scl[0:1, 128:256], in_=ps[0:1, 512 * 7:512 * 7 + 128]), waits=[m2])
        c3 = vop(lambda e, st: e.tensor_copy(
            out=s_scl[0:1, 256:258], in_=ps[0:1, 512 * 5:512 * 5 + 2]), waits=[m3])
        rel((6,), c1)
        rel((7,), c2)
        rel((5,), c3)
        return [c1, c2, c3], [m1, m2, m3]

    tk_lt_rd = None
    st_guard = [None, None]
    row_guard = qstate.get("rowfree")
    tk_red_guard = []
    tk_scl_guard = [qstate.get("rowfree")]
    tk_xq8_rd = [tk_w2q.get(31)]
    tk_xqb_rd = None
    tk_hy_wr_ok = tk_w2q.get(31)
    tk_x1w_rd = None
    tk_out_dma = [None, None]
    tk_out_done = []
    prev_te1 = None
    for tc in range(8):
        j0 = 256 * tc
        tk_x1w = []
        tk_wm = {}
        for zc in range(8):
            tkwd, wsl = wo_load(zc)
            for oc in range(8):
                tk_wm[oc] = mm(ps[:, 512 * oc:512 * oc + 258],
                               s_wos[:, 1024 * wsl + 128 * oc:1024 * wsl + 128 * oc + 128],
                               s_z[:, 2050 * zc + j0:2050 * zc + j0 + 258],
                               start=(zc == 0), stop=(zc == 7),
                               waits=[tkwd, tk_zb[-1]], banks=(oc,))
            wos_free[wsl] = tk_wm[7]
        for oc in range(8):
            sl = oc % 2
            tkxr = dma("sync", s_xr[:, 512 * sl:512 * sl + 258],
                       xr3[:, oc, j0:j0 + 258], waits=[xr_free[sl]], sem="dXR%d" % sl)
            tstt = vop(lambda e, st, o=oc: e.scalar_tensor_tensor(
                out=s_x1w[:, 258 * o:258 * o + 258], in0=ps[:, 512 * o:512 * o + 258],
                scalar=s_bo[:, o:o + 1], in1=s_xr[:, 512 * (o % 2):512 * (o % 2) + 258],
                op0=ALU.add, op1=ALU.add),
                waits=[tk_wm[oc], tkxr, tk_consts, tk_x1w_rd])
            rel((oc,), tstt)
            xr_free[sl] = tstt
            tk_x1w.append(tstt)
        if "x1w" in tapap and tc == 0:
            tk_x1w = [fence_v(dma("sync", tapap["x1w"][:, :], s_x1w[:, :],
                                  waits=tk_x1w, sem="dTP"))] * 8
        tm1 = tm2 = None
        tk_c1rd = [st_guard[0]]
        tk_c2rd = [st_guard[1]]
        for dc in range(8):
            tcst = sop(lambda e, st, d=dc: e.activation(
                out=s_cast1[:, 0:258], in_=s_x1w[:, 258 * d:258 * d + 258],
                func=AF.Copy), waits=[tk_x1w[dc]] + tk_c1rd)
            tsq = sop(lambda e, st, d=dc: e.activation(
                out=s_cast2[:, 0:258], in_=s_x1w[:, 258 * d:258 * d + 258],
                func=AF.Square), waits=[tk_x1w[dc], tcst] + tk_c2rd)
            tm1 = mm(ps[0:1, 1024:1024 + 258], s_ones[:, 0:1], s_cast1[:, 0:258],
                     start=(dc == 0), stop=(dc == 7), waits=[tcst, tk_ones],
                     banks=(2,))
            tm2 = mm(ps[0:1, 1536:1536 + 258], s_ones[:, 0:1], s_cast2[:, 0:258],
                     start=(dc == 0), stop=(dc == 7), waits=[tsq], banks=(3,))
            tk_c1rd, tk_c2rd = [tsq, tm1], [tm2]
        st_guard = [tm1, tm2]
        u1 = vop(lambda e, st, a=j0: e.tensor_scalar(
            out=s_mr[0:1, a:a + 258], in0=ps[0:1, 1024:1024 + 258],
            scalar1=1.0 / D, scalar2=None, op0=ALU.mult), waits=[tm1, tk_mrms])
        u2 = vop(lambda e, st, a=j0: e.tensor_scalar(
            out=s_mr[0:1, W + a:W + a + 258], in0=ps[0:1, 1536:1536 + 258],
            scalar1=1.0 / D, scalar2=None, op0=ALU.mult), waits=[tm2])
        rel((2,), u1)
        rel((3,), u2)
        u3 = vop(lambda e, st, a=j0: e.tensor_mul(
            s_row[0:1, 0:258], s_mr[0:1, a:a + 258], s_mr[0:1, a:a + 258]),
            waits=[u1, row_guard])
        u4 = vop(lambda e, st, a=j0: e.tensor_sub(
            s_mr[0:1, W + a:W + a + 258], s_mr[0:1, W + a:W + a + 258],
            s_row[0:1, 0:258]), waits=[u2, u3])
        u5 = vop(lambda e, st, a=j0: e.tensor_scalar(
            out=s_mr[0:1, W + a:W + a + 258], in0=s_mr[0:1, W + a:W + a + 258],
            scalar1=EPS, scalar2=None, op0=ALU.add), waits=[u4])
        u6 = vop(lambda e, st, a=j0: e.reciprocal(
            out=s_mr[0:1, W + a:W + a + 258], in_=s_mr[0:1, W + a:W + a + 258]),
            waits=[u5])
        row_guard = u3
        tk_rstd2 = sop(lambda e, st, a=j0: e.activation(
            out=s_mr[0:1, W + a:W + a + 258], in_=s_mr[0:1, W + a:W + a + 258],
            func=AF.Sqrt), waits=[u6, tk_rstd])
        tmb = bcast_ps(s_mr[0:1, j0:j0 + 258], 258, 4, [tk_rstd2, u1])
        tk_sub = []
        for dc in range(8):
            ts_ = vop(lambda e, st, d=dc: e.tensor_sub(
                s_lt[:, 258 * d:258 * d + 258], s_x1w[:, 258 * d:258 * d + 258],
                bank(4, 258)), waits=[tmb, tk_lt_rd, tk_x1w[dc]])
            tk_sub.append(ts_)
        rel((4,), tk_sub[-1])
        tb_ = vop(lambda e, st: e.scalar_tensor_tensor(
            out=s_red[:, 0:258], in0=s_lt[:, 0:258], scalar=-1.0,
            in1=s_lt[:, 0:258], op0=ALU.mult, op1=ALU.max),
            waits=[tk_sub[0]] + tk_red_guard)
        for dci in range(1, 8):
            t_a = vop(lambda e, st, d=dci: e.scalar_tensor_tensor(
                out=s_red[:, 258:516], in0=s_lt[:, 258 * d:258 * d + 258],
                scalar=-1.0, in1=s_lt[:, 258 * d:258 * d + 258],
                op0=ALU.mult, op1=ALU.max),
                waits=[tk_sub[dci], tb_])
            tb_ = vop(lambda e, st: e.tensor_tensor(out=s_red[:, 0:258],
                      in0=s_red[:, 0:258], in1=s_red[:, 258:516], op=ALU.max),
                      waits=[t_a])
        cps, mns = crosspart_max([tb_] + tk_scl_guard)
        q1 = vop(lambda e, st, jj=j0: e.tensor_mul(s_scl[0:1, 258:516],
                 s_scl[0:1, 0:258], s_mr[0:1, W + jj:W + jj + 258]),
                 waits=cps + [tk_rstd2])
        q2 = vop(lambda e, st: e.tensor_scalar(out=s_scl[0:1, 774:1032],
                 in0=s_scl[0:1, 258:516], scalar1=1e-5, scalar2=None, op0=ALU.max),
                 waits=[q1])
        q3 = vop(lambda e, st: e.reciprocal(out=s_scl[0:1, 258:516],
                 in_=s_scl[0:1, 774:1032]), waits=[q2])
        q4 = vop(lambda e, st, jj=j0: e.tensor_mul(s_scl[0:1, 258:516],
                 s_scl[0:1, 258:516], s_mr[0:1, W + jj:W + jj + 258]), waits=[q3])
        q5 = vop(lambda e, st: e.tensor_scalar(out=s_scl[0:1, 258:516],
                 in0=s_scl[0:1, 258:516], scalar1=127.0, scalar2=None, op0=ALU.mult),
                 waits=[q4])
        q6 = vop(lambda e, st: e.tensor_scalar(out=s_scl[0:1, 516:774],
                 in0=s_scl[0:1, 774:1032], scalar1=s_scl[0:1, 1032:1033],
                 scalar2=1.0 / 127.0, op0=ALU.mult, op1=ALU.mult),
                 waits=[q2, qstate[("w1", "mean")]])
        if tc == 0:
            q6 = vop(lambda e, st: e.tensor_mul(s_scl[0:1, 516:774], s_scl[0:1, 516:774],
                     s_maskl[0:1, :]), waits=[q6, tk_consts])
        if tc == 7:
            q6 = vop(lambda e, st: e.tensor_mul(s_scl[0:1, 516:774], s_scl[0:1, 516:774],
                     s_maskr[0:1, :]), waits=[q6, tk_consts])
        tcb = bcast_ps(s_scl[0:1, 258:516], 258, 4, [q5])
        tsbv = bcast_sb(s_scl[0:1, 516:774], 258, 5, s_bc[:, 0:258], [q6])
        tq8 = []
        for dc in range(8):
            tq8.append(vop(lambda e, st, d=dc: e.tensor_mul(
                s_xq8[:, 258 * d:258 * d + 258], s_lt[:, 258 * d:258 * d + 258],
                bank(4, 258)), waits=[tcb, tk_sub[dc]] + tk_xq8_rd))
        tk_xq8_rd = []
        rel((4,), tq8[-1])
        tqb = vop(lambda e, st: e.tensor_copy(out=s_xqb[:, :], in_=s_xq8[:, :]),
                  waits=tq8 + [tk_xqb_rd])
        if "w1q" in tapap and tc == 0:
            tqb = fence_v(dma("sync", tapap["w1q"][:, :], s_w1q[:, :],
                              waits=[tqb, tk_w1q.get(7)], sem="dTP"))
        tk_lt_rd = tq8[-1]
        if "xq" in tapap and tc == 0:
            tqb = fence_v(dma("sync", tapap["xq"][:, :], s_xqb[:, :], waits=[tqb],
                              sem="dTP"))
        tk_z2 = []
        prev_tt1 = None
        prev_tt2 = None
        prev_silu = None
        for hc in range(32):
            bk = hc % 4
            tmm = None
            for dc in range(8):
                tmm = mm(bank(bk, 258),
                         s_w1q[:, 4096 * dc + 128 * hc:4096 * dc + 128 * hc + 128],
                         s_xqb[:, 258 * dc:258 * dc + 258], start=(dc == 0),
                         stop=(dc == 7), waits=[tqb, tk_w1q.get(dc)], banks=(bk,))
            tev = vop(lambda e, st, h=hc, b=bk: e.tensor_mul(
                s_hy[:, 258 * h:258 * h + 258], bank(b, 258),
                s_bc[:, 0:258]), waits=[tmm, tsbv, tk_hy_wr_ok, NS["rb_free"]])
            rel((bk,), tev)
            if "hpre" in tapap and tc == 0:
                tev = fence_v(dma("sync", tapap["hpre"][:, 258 * hc:258 * hc + 258],
                                  s_hy[:, 258 * hc:258 * hc + 258], waits=[tev],
                                  sem="dTP"))
            tt0 = sop(lambda e, st, h=hc: e.activation(
                out=s_tc0[:, 0:256], in_=s_hy[:, 258 * h:258 * h + 256],
                func=AF.Identity, bias=s_cb[:, h:h + 1], scale=s_cw[:, h:h + 1]),
                waits=[tev, tk_consts, prev_tt1])
            tt1 = vop(lambda e, st, h=hc: e.scalar_tensor_tensor(
                out=s_tc0[:, 258:258 + 256],
                in0=s_hy[:, 258 * h + 1:258 * h + 257],
                scalar=s_cw[:, 32 + h:32 + h + 1], in1=s_tc0[:, 0:256],
                op0=ALU.mult, op1=ALU.add), waits=[tt0, tev, prev_tt2])
            tt2 = vop(lambda e, st, h=hc: e.scalar_tensor_tensor(
                out=s_tcb[:, 0:256],
                in0=s_hy[:, 258 * h + 2:258 * h + 258],
                scalar=s_cw[:, 64 + h:64 + h + 1], in1=s_tc0[:, 258:258 + 256],
                op0=ALU.mult, op1=ALU.add), waits=[tt1, prev_silu])
            tt3 = sop(lambda e, st, h=hc: e.activation(
                out=s_hy[:, 258 * h:258 * h + 256],
                in_=s_tcb[:, 0:256], func=AF.Silu), waits=[tt2])
            prev_tt1, prev_tt2, prev_silu = tt1, tt2, tt3
            tk_z2.append(tt3)
        if "h" in tapap and tc == 0:
            tk_z2 = [fence_v(dma("sync", tapap["h"][:, :], s_hy[:, :], waits=tk_z2,
                                 sem="dTP"))] * 32
        ab = vop(lambda e, st: e.scalar_tensor_tensor(
            out=s_red[:, 0:256], in0=s_hy[:, 0:256], scalar=-1.0,
            in1=s_hy[:, 0:256], op0=ALU.mult, op1=ALU.max),
            waits=[tk_z2[0]] + mns)
        for hc in range(1, 32):
            t_a = vop(lambda e, st, h=hc: e.scalar_tensor_tensor(
                out=s_red[:, 258:258 + 256], in0=s_hy[:, 258 * h:258 * h + 256],
                scalar=-1.0, in1=s_hy[:, 258 * h:258 * h + 256],
                op0=ALU.mult, op1=ALU.max),
                waits=[tk_z2[hc], ab])
            ab = vop(lambda e, st: e.tensor_tensor(out=s_red[:, 0:256],
                     in0=s_red[:, 0:256], in1=s_red[:, 258:258 + 256], op=ALU.max),
                     waits=[t_a])
        ab2 = vop(lambda e, st: e.memset(s_red[:, 256:258], 0.0), waits=[ab, cps[2]])
        cps2, mns2 = crosspart_max([ab, ab2] + cps)
        r1 = vop(lambda e, st: e.tensor_scalar(out=s_scl[0:1, 774:1032],
                 in0=s_scl[0:1, 0:258], scalar1=1e-5, scalar2=None, op0=ALU.max),
                 waits=cps2 + [q6])
        r2 = vop(lambda e, st: e.reciprocal(out=s_scl[0:1, 0:258],
                 in_=s_scl[0:1, 774:1032]), waits=[r1])
        r3 = vop(lambda e, st: e.tensor_scalar(out=s_scl[0:1, 0:258],
                 in0=s_scl[0:1, 0:258], scalar1=127.0, scalar2=None, op0=ALU.mult),
                 waits=[r2])
        r4 = vop(lambda e, st: e.tensor_scalar(out=s_scl[0:1, 516:774],
                 in0=s_scl[0:1, 774:1032], scalar1=s_scl[0:1, 1033:1034],
                 scalar2=1.0 / 127.0, op0=ALU.mult, op1=ALU.mult),
                 waits=[r1, qstate[("w2", "mean")], tsbv])
        ts2v = bcast_sb(s_scl[0:1, 0:256], 256, 4, s_bc[:, 516:516 + 256],
                        [r3, tcb, tq8[-1]])
        tg2v = bcast_sb(s_scl[0:1, 516:516 + 256], 256, 5, s_bc[:, 258:258 + 256],
                        [r4, tsbv])
        tk_scl_guard = [ts2v, tg2v]
        tcv_prev = [None, None, None, None]
        tmm2 = {oc: None for oc in range(8)}
        last_mm = None
        ty8_last = None
        for hc in range(32):
            sl2 = hc % 4
            ty8 = vop(lambda e, st, h=hc, s=sl2: e.tensor_mul(
                s_y8r[:, 260 * s:260 * s + 256],
                s_hy[:, 258 * h:258 * h + 256], s_bc[:, 516:516 + 256]),
                waits=[ts2v, tk_z2[hc], tcv_prev[sl2], tk_y8ms])
            tcv = vop(lambda e, st, s=sl2: e.tensor_copy(
                out=s_ybr[:, 260 * s:260 * s + 256],
                in_=s_y8r[:, 260 * s:260 * s + 256]),
                waits=[ty8])
            for oc in range(8):
                tmm2[oc] = mm(bank(oc, 256),
                              s_w2q[:, 1024 * hc + 128 * oc:1024 * hc + 128 * oc + 128],
                              s_ybr[:, 260 * sl2:260 * sl2 + 256], start=(hc == 0),
                              stop=(hc == 31), waits=[tcv, tk_w2q.get(hc)],
                              banks=(oc,))
            tcv_prev[sl2] = tmm2[7]
            last_mm = tmm2[7]
            ty8_last = ty8
        tk_hy_wr_ok = ty8_last
        for oc in range(8):
            so_sl = oc % 2
            te1 = vop(lambda e, st, b=oc: e.tensor_mul(
                s_tc0[:, 0:256], bank(b, 256), s_bc[:, 258:258 + 256]),
                waits=[tmm2[oc], tg2v, prev_te1])
            rel((oc,), te1)
            te2 = vop(lambda e, st, o=oc, t=tc, s=so_sl: e.tensor_add(
                s_out[:, 256 * s:256 * s + 256], s_tc0[:, 0:256],
                s_x1w[:, 258 * o + 1:258 * o + 257]),
                waits=[te1, tk_out_dma[so_sl], tk_x1w[oc]])
            prev_te1 = te2
            tko = dma("sync", fm(ap["out"])[:, oc, 256 * tc:256 * tc + 256],
                      s_out[:, 256 * so_sl:256 * so_sl + 256], waits=[te2],
                      sem="dO%d" % so_sl)
            tk_out_dma[so_sl] = tko
            tk_out_done.append(tko)
        tk_x1w_rd = tk_out_done[-1]
        tk_red_guard = mns2 + cps2
        tk_xqb_rd = tk_z2[-1]
        if "xq2" in tapap and tc == 0:
            dma("sync", tapap["xq2"][:, :], s_ybr[:, 0:520],
                waits=[tcv_prev[0], tcv_prev[1]], sem="dTP")

    p.emit("sync", lambda e, st: e.nop(),
           waits=[p.tok(s) for s in ["dC","dX0","dX1","dTP","dW0","dW1","dWD0","dWD1","dKD","dZN","dZI","dZB","dWO0","dWO1","dXR0","dXR1","dO0","dO1","tT","tV","tS","tG","CC"]])
    return _finish(nc, ctx, p)


def _finish(nc, ctx, p):
    sems = {}
    for s in ["dC", "dX0", "dX1", "dTP", "dW0", "dW1", "dWD0", "dWD1", "dKD", "dZN", "dZI",
              "dZB", "dWO0", "dWO1", "dXR0", "dXR1", "dO0", "dO1",
              "tT", "tV", "tS", "tG", "CC"]:
        sems[s] = ctx.enter_context(nc.semaphore(s))
    blk = ctx.enter_context(nc.Block())
    state = {}

    @blk.sync
    def _(e):
        p.run("sync", e, sems, state)

    @blk.tensor
    def _(e):
        p.run("tensor", e, sems, state)

    @blk.vector
    def _(e):
        p.run("vector", e, sems, state)

    @blk.scalar
    def _(e):
        p.run("scalar", e, sems, state)

    @blk.gpsimd
    def _(e):
        p.run("gpsimd", e, sems, state)

    ctx.close()
    return nc



def kernel(**inputs):
    from concourse.bass_utils import run_bass_kernel_spmd
    maps = stage_inputs(inputs, n_cores=8)
    nc = build_nc(n_cores=8, n_heads=NB, stop_after="full", taps=())
    res = run_bass_kernel_spmd(nc, maps, core_ids=list(range(8)))
    return gather_out(res.results)
